# version 09
# baseline (speedup 1.0000x reference)
import sys
sys.path.insert(0, '/opt/trn_rl_repo')
import contextlib
import numpy as np
import concourse.bass as bass
import concourse.mybir as mybir

F32 = mybir.dt.float32
F32R = mybir.dt.float32r
BF16 = mybir.dt.bfloat16
FP8 = mybir.dt.float8e4
AF = mybir.ActivationFunctionType
ALU = mybir.AluOpType
AX = mybir.AxisListType

ENGINES = ("sync", "vector", "scalar", "tensor", "gpsimd")

H = 256
HC = H + 4
NPOS = 512
LH = NPOS + HC
Q = 64
EPS = 1e-5


class Prog:
    def __init__(self, name="mk"):
        self.nc = bass.Bass("TRN2", target_bir_lowering=False)
        self.ops = {e: [] for e in ENGINES}
        self.cnt = {e: 0 for e in ENGINES}
        self.sems = {}
        self.last_writer = {}
        self.readers = {}
        self.waited = {e: {e2: 0 for e2 in ENGINES} for e in ENGINES}
        self.stack = contextlib.ExitStack()
        self.tiles = {}
        self._tile_id = 0

    def sbuf(self, name, shape, dtype):
        t = self.stack.enter_context(self.nc.sbuf_tensor(name, list(shape), dtype))
        self.tiles[name] = t
        return t

    def psum(self, name, shape, dtype=F32):
        t = self.stack.enter_context(self.nc.psum_tensor(name, list(shape), dtype))
        self.tiles[name] = t
        return t

    def _wait(self, eng, dep_eng, dep_cnt):
        if self.waited[eng][dep_eng] >= dep_cnt:
            return
        self.waited[eng][dep_eng] = dep_cnt
        sems = self.sems

        def w(e, dep_eng=dep_eng, dep_cnt=dep_cnt):
            e.wait_ge(sems[dep_eng], dep_cnt)
        self.ops[eng].append(w)

    def emit(self, eng, fn, reads=(), writes=(), dma=False, inc=None):
        for r in reads:
            lw = self.last_writer.get(r)
            if lw is not None:
                dep_eng, dep_cnt = lw[0], lw[1]
                if dep_eng == eng and eng == "tensor":
                    pass
                elif dep_eng == eng and not dma and self.cnt[eng] == dep_cnt and False:
                    pass
                else:
                    if not (dep_eng == eng and self.waited[eng][eng] >= dep_cnt):
                        self._wait(eng, dep_eng, dep_cnt)
        for w_ in writes:
            rd = self.readers.get(w_)
            if rd:
                for dep_eng, dep_cnt in rd.items():
                    if dep_eng == eng and eng == "tensor":
                        continue
                    self._wait(eng, dep_eng, dep_cnt)
            lw = self.last_writer.get(w_)
            if lw is not None and not (lw[0] == eng == "tensor"):
                if dma and len(lw) > 2 and lw[2]:
                    pass
                else:
                    self._wait(eng, lw[0], lw[1])
        n = inc if inc is not None else (16 if dma else 1)
        sems = self.sems
        cnt_after = self.cnt[eng] + n
        my = eng

        def run(e, fn=fn, n=n, my=my):
            ins = fn(e)
            ins.then_inc(sems[my], n)
        self.ops[eng].append(run)
        self.cnt[eng] = cnt_after
        if dma:
            cnt_after = self.cnt[eng]
        for w_ in writes:
            self.last_writer[w_] = (eng, cnt_after, dma)
            self.readers.pop(w_, None)
        for r in reads:
            self.readers.setdefault(r, {})
            cur = self.readers[r].get(eng, 0)
            self.readers[r][eng] = max(cur, cnt_after)
        return cnt_after

    def raw(self, eng, fn):
        self.ops[eng].append(fn)

    def barrier(self):
        snap = dict(self.cnt)
        for e in ENGINES:
            for e2 in ENGINES:
                if e2 != e and snap[e2] > 0:
                    self._wait(e, e2, snap[e2])

    @contextlib.contextmanager
    def scope(self):
        outer = self.stack
        inner = contextlib.ExitStack()
        self.stack = inner
        try:
            yield
        finally:
            self.barrier()
            self.stack = outer
            inner.close()

    def build(self):
        nc = self.nc
        for e in ENGINES:
            self.sems[e] = self.stack.enter_context(nc.semaphore("sem_" + e))
        blk = self.stack.enter_context(nc.Block())
        for e in ENGINES:
            ops = self.ops[e]

            def body(engobj, ops=ops):
                for f in ops:
                    f(engobj)
            getattr(blk, e)(body)
        self.stack.close()
        return nc

    def finish(self):
        for e in ENGINES:
            if e == "gpsimd":
                continue
            c = self.cnt[e]
            if c:
                self._wait("gpsimd", e, c)
        c = self.cnt["gpsimd"]
        if c:
            sems = self.sems

            def w(eng, c=c):
                eng.wait_ge(sems["gpsimd"], c)
            self.ops["gpsimd"].append(w)



def ceil_div(a, b):
    return (a + b - 1) // b


def emit_const_setup(K):
    nc = K.nc
    ones_row = K.sbuf("ones_row", [1, 512], BF16)
    ones_col = K.sbuf("ones_col", [128, 1], F32R)
    ones_col32 = K.sbuf("ones_col32", [128, 1], F32)
    onesb_col = K.sbuf("onesb_col", [128, 1], BF16)
    ones_row128 = K.sbuf("ones_row128", [1, 128], F32)
    K.emit("gpsimd", lambda e: e.memset(ones_row[:, :], 1.0), writes=("ones_row",))
    K.emit("gpsimd", lambda e: e.memset(ones_col32[:, :], 1.0), writes=("ones_col32",))
    K.emit("vector", lambda e: e.tensor_copy(ones_col[:, :], ones_col32[:, :]), reads=("ones_col32",), writes=("ones_col",))
    K.emit("gpsimd", lambda e: e.memset(onesb_col[:, :], 1.0), writes=("onesb_col",))
    K.emit("gpsimd", lambda e: e.memset(ones_row128[:, :], 1.0), writes=("ones_row128",))
    cpool = K.sbuf("cpool", [128, 8], F32)
    for i, v in enumerate((1e-5, -0.5, 1e-6, 0.5)):
        K.emit("gpsimd", lambda e, i=i, v=v: e.memset(cpool[:, i:i + 1], v), writes=("cpool",))
        K.nc.const_aps.register(F32, v, cpool[:, i:i + 1])
    return dict(ones_row=ones_row, ones_col=ones_col, onesb_col=onesb_col, ones_row128=ones_row128,
                cpool=cpool)


def emit_ln(K, C, xt, xsq, stat_ps, rep_ps, mvec, T, extra_scale_ap=None,
            tag=""):
    nc = K.nc
    TM = xt.shape[1] // 8
    kx, ksq, kst = tag + "xt", tag + "xsq", tag + "stat"
    kmv = tag + "mvec"
    krepA, krepB = tag + "repA", tag + "repB"
    mA, mB = mvec[0:1, 0:512], mvec[0:1, 512:1024]
    K.emit("scalar", lambda e: e.activation(xsq[:, 0:8 * TM], xt[:, 0:8 * TM], AF.Square),
           reads=(kx,), writes=(ksq,))
    oc = C["ones_col"]
    for c in range(8):
        K.emit("tensor", lambda e, c=c: e.matmul(
            stat_ps[0:1, 0:T], oc[:, :], xt[:, c * TM:c * TM + T],
            start=(c == 0), stop=(c == 7)), reads=(kx, "ones_col"), writes=(kst,))
    for c in range(8):
        K.emit("tensor", lambda e, c=c: e.matmul(
            stat_ps[0:1, 512:512 + T], oc[:, :], xsq[:, c * TM:c * TM + T],
            start=(c == 0), stop=(c == 7)), reads=(ksq, "ones_col"), writes=(kst,))
    K.emit("vector", lambda e: e.tensor_scalar_mul(mA[:, 0:T], stat_ps[0:1, 0:T], 1.0 / 1024.0),
           reads=(kst,), writes=(kmv + "A",))
    K.emit("vector", lambda e: e.tensor_scalar_mul(mB[:, 0:T], stat_ps[0:1, 512:512 + T], 1.0 / 1024.0),
           reads=(kst,), writes=(kmv + "B",))
    orow = C["ones_row128"]
    K.emit("tensor", lambda e: e.matmul(rep_ps[:, 0:T], orow[0:1, :], mA[:, 0:T], start=True, stop=True),
           reads=(kmv + "A", "ones_row128"), writes=(krepA,))
    K.emit("vector", lambda e: e.tensor_tensor(stat_ps[0:1, 0:T], mA[:, 0:T], mA[:, 0:T], ALU.mult),
           reads=(kmv + "A",), writes=(kst,))
    K.emit("vector", lambda e: e.tensor_tensor(mB[:, 0:T], mB[:, 0:T], stat_ps[0:1, 0:T], ALU.subtract),
           reads=(kmv + "B", kst), writes=(kmv + "B",))
    K.emit("scalar", lambda e: e.activation(mA[:, 0:T], mB[:, 0:T], AF.Ln, bias=EPS),
           reads=(kmv + "B",), writes=(kmv + "A",))
    K.emit("scalar", lambda e: e.activation(mA[:, 0:T], mA[:, 0:T], AF.Exp, scale=-0.5),
           reads=(kmv + "A",), writes=(kmv + "A",))
    if extra_scale_ap is not None:
        K.emit("vector", lambda e: e.tensor_tensor(
            mA[:, 0:T], mA[:, 0:T], extra_scale_ap.broadcast_to([1, T]), ALU.mult),
            reads=(kmv + "A", "recip_s"), writes=(kmv + "A",))
    K.emit("tensor", lambda e: e.matmul(rep_ps[:, TM:TM + T], orow[0:1, :], mA[:, 0:T], start=True, stop=True),
           reads=(kmv + "A", "ones_row128"), writes=(krepB,))
    return krepA


def alloc_p0(K, pfx="p0_"):
    b = {}
    b['xt'] = K.sbuf(pfx + "xt", [128, 4096], F32R)
    b['xw'] = K.sbuf(pfx + "xw", [128, 4096], F32)
    b['xsq'] = K.sbuf(pfx + "xsq", [128, 4096], F32R)
    b['carve32'] = K.sbuf(pfx + "carve", [128, 1024], F32)
    b['xb'] = K.sbuf(pfx + "xb", [128, 4096], BF16)
    b['xnew'] = K.sbuf(pfx + "xnew", [128, 4096], F32)
    b['w0b'] = K.sbuf(pfx + "w0b", [128, 8192], BF16)
    b['biasb'] = K.sbuf(pfx + "biasb", [1, 1024], BF16)
    b['bias32'] = K.sbuf(pfx + "bias32", [1, 1024], F32)
    b['sc'] = K.sbuf(pfx + "sc", [1, 8], F32)
    b['c_ap'] = K.sbuf(pfx + "cap", [128, 1], F32)
    b['mvec'] = K.sbuf(pfx + "mvec", [1, 1024], F32)
    b['stat_ps'] = K.psum(pfx + "stat", [1, 1024], F32)
    b['rep_ps'] = K.psum(pfx + "rep", [128, 1024], F32)
    b['out_ps'] = K.psum(pfx + "out", [128, 1024], F32)
    return b


def emit_p0_setup(K, C, b, wT_h, bias_h, sin_h, sw_h):
    nc = K.nc
    w0b, biasb, bias32, sc, c_ap = b['w0b'], b['biasb'], b['bias32'], b['sc'], b['c_ap']
    for ic in range(8):
        K.emit("gpsimd", lambda e, ic=ic: e.dma_start(
            out=w0b[:, ic * 1024:(ic + 1) * 1024], in_=wT_h[ic * 128:(ic + 1) * 128, :]),
            writes=("w0b",), dma=True)
    K.emit("gpsimd", lambda e: e.dma_start(out=bias32[0:1, :], in_=bias_h[None, :]),
           writes=("bias32",), dma=True)
    K.emit("gpsimd", lambda e: e.dma_start(out=sc[0:1, 0:1], in_=sin_h[None, :]),
           writes=("sc",), dma=True)
    K.emit("gpsimd", lambda e: e.dma_start(out=sc[0:1, 1:2], in_=sw_h[None, :]),
           writes=("sc",), dma=True)
    K.emit("vector", lambda e: e.tensor_tensor(sc[0:1, 2:3], sc[0:1, 0:1], sc[0:1, 1:2], ALU.mult),
           reads=("sc",), writes=("sc2",))
    K.emit("vector", lambda e: e.reciprocal(sc[0:1, 3:4], sc[0:1, 2:3]),
           reads=("sc2",), writes=("sc3",))
    K.emit("vector", lambda e: e.tensor_scalar_mul(sc[0:1, 4:5], sc[0:1, 2:3], 0.5),
           reads=("sc2",), writes=("sc4",))
    K.emit("vector", lambda e: e.reciprocal(sc[0:1, 5:6], sc[0:1, 0:1]),
           reads=("sc",), writes=("recip_s",))
    K.emit("vector", lambda e: e.tensor_tensor(biasb[0:1, :], bias32[0:1, :],
                                               sc[0:1, 3:4].broadcast_to([1, 1024]), ALU.mult),
           reads=("bias32", "sc3"), writes=("biasb",))
    K.emit("tensor", lambda e: e.matmul(b['rep_ps'][:, 0:1], C['ones_row128'][0:1, :], sc[0:1, 4:5],
                                        start=True, stop=True),
           reads=("sc4", "ones_row128"), writes=("p0_repA",))
    K.emit("vector", lambda e: e.tensor_copy(c_ap[:, :], b['rep_ps'][:, 0:1]),
           reads=("p0_repA",), writes=("c_ap",))


def emit_p0_tile(K, C, b, xres, bidx, t0, T, tag):
    nc = K.nc
    xsrc, xdst = xres if isinstance(xres, tuple) else (xres, xres)
    TM = 512
    xt, xw, xsq, xb, xnew = b['xt'], b['xw'], b['xsq'], b['xb'], b['xnew']
    carve = b['carve32'].ap().bitcast(FP8)
    kx = "p0_xt"
    if T == TM:
        K.emit("gpsimd", lambda e, t0=t0: e.dma_start(
            out=xt[:, :].rearrange("p (c t) -> p c t", t=TM),
            in_=xsrc[bidx, :, t0:t0 + TM].rearrange("(c p) l -> p c l", p=128)),
            writes=(kx,), dma=True)
    else:
        for c in range(8):
            K.emit("gpsimd", lambda e, c=c: e.dma_start(
                out=xt[:, c * TM:c * TM + T], in_=xsrc[bidx, c * 128:(c + 1) * 128, t0:t0 + T]),
                writes=(kx,), dma=True)
    krep = emit_ln(K, C, xt, xsq, b['stat_ps'], b['rep_ps'], b['mvec'], T,
                   extra_scale_ap=b['sc'][0:1, 5:6], tag="p0_")
    rep = b['rep_ps']
    for c in range(8):
        K.emit("vector", lambda e, c=c: e.tensor_tensor(xw[:, c * TM:c * TM + T], xt[:, c * TM:c * TM + T],
                                                        rep[:, 0:T], ALU.subtract),
               reads=(kx, "p0_repA"), writes=("p0_xw",))
        K.emit("vector", lambda e, c=c: e.tensor_tensor(xw[:, c * TM:c * TM + T], xw[:, c * TM:c * TM + T],
                                                        rep[:, TM:TM + T], ALU.mult),
               reads=("p0_xw", "p0_repB"), writes=("p0_xw",))
        K.emit("vector", lambda e, c=c: e.tensor_scalar(carve[:, c * TM:c * TM + T], xw[:, c * TM:c * TM + T],
                                                        448.0, -448.0, ALU.min, ALU.max),
               reads=("p0_xw",), writes=("p0_carve",))
        K.emit("scalar", lambda e, c=c: e.activation(xb[:, c * TM:c * TM + T], carve[:, c * TM:c * TM + T], AF.Copy),
               reads=("p0_carve",), writes=("p0_xb",))
    out_ps = b['out_ps']
    for oc in range(8):
        pk = "p0_out%d" % (oc % 2)
        K.emit("tensor", lambda e, oc=oc: e.matmul(
            out_ps[:, (oc % 2) * 512:(oc % 2) * 512 + T],
            b['biasb'][0:1, oc * 128:(oc + 1) * 128], C['ones_row'][0:1, 0:T],
            start=True, stop=False), reads=("biasb", "ones_row"), writes=(pk,))
        for ic in range(8):
            K.emit("tensor", lambda e, oc=oc, ic=ic: e.matmul(
                out_ps[:, (oc % 2) * 512:(oc % 2) * 512 + T],
                b['w0b'][:, ic * 1024 + oc * 128:ic * 1024 + (oc + 1) * 128],
                xb[:, ic * TM:ic * TM + T],
                start=False, stop=(ic == 7)), reads=("w0b", "p0_xb"), writes=(pk,))
        K.emit("vector", lambda e, oc=oc: e.scalar_tensor_tensor(
            xnew[:, oc * TM:oc * TM + T], out_ps[:, (oc % 2) * 512:(oc % 2) * 512 + T],
            b['c_ap'][:, :], xt[:, oc * TM:oc * TM + T], ALU.mult, ALU.add),
            reads=(pk, "c_ap", kx), writes=("p0_xnew%d" % oc,))
        K.emit("sync", lambda e, oc=oc: e.dma_start(
            out=xdst[bidx, oc * 128:(oc + 1) * 128, t0:t0 + T], in_=xnew[:, oc * TM:oc * TM + T]),
            reads=("p0_xnew%d" % oc,), dma=True, writes=())


def emit_p0(K, C, b, xres, L):
    tiles = [(0, 512), (512, L - 512)] if L > 512 else [(0, L)]
    assert all(T % 2 == 0 for _, T in tiles)
    for bidx in range(16):
        for (t0, T) in tiles:
            emit_p0_tile(K, C, b, xres, bidx, t0, T, tag="p0_%d_%d_" % (bidx, t0))



def alloc_p1(K):
    b = {}
    b['xt'] = K.sbuf("q_xt", [128, 4096], F32R)
    b['xsq'] = K.sbuf("q_xsq", [128, 4096], F32R)
    b['dts'] = K.sbuf("q_dts2", [32, 772], F32)

    b['mvec'] = K.sbuf("q_mvec", [1, 1024], F32)
    b['w_in0'] = K.sbuf("q_win0", [128, 2048], BF16)
    b['w_in1'] = b['w_in0']
    b['w_op'] = K.sbuf("q_wop", [128, 2048], BF16)
    b['xbc_pre'] = K.sbuf("q_xbcp", [128, 18 * 772], BF16)
    b['xbc2'] = K.sbuf("q_xbc2", [128, 18 * 768], BF16)
    b['dtraw'] = K.sbuf("q_dtraw", [32, 772], F32)

    b['sdt'] = K.sbuf("q_sdt", [64, 772], F32)
    b['sposatmp'] = K.sbuf("q_sposatmp", [64, 768], F32)
    b['spos'] = b['sdt'][:, 0:384]
    b['atmp'] = b['sdt'][:, 384:768]

    b['s_all'] = K.sbuf("q_sall", [64, 12 * 32], F32)
    b['bmid'] = K.sbuf("q_bmid", [1, 384], F32)
    b['slast'] = K.sbuf("q_slast", [1, 384], F32)
    b['wbase'] = K.sbuf("q_wbase", [1, 384], F32)
    b['g1row'] = K.sbuf("q_g1row", [1, 384], F32)

    b['ubf0'] = K.sbuf("q_ubf0", [64, 96], BF16)
    b['ubf1'] = K.sbuf("q_ubf1", [64, 96], BF16)
    b['uvw0'] = K.sbuf("q_uvw0", [64, 96], F32)
    b['uvw1'] = K.sbuf("q_uvw1", [64, 96], F32)
    b['XT0'] = K.sbuf("q_XT0", [128, 2048], BF16)
    b['BT'] = K.sbuf("q_BT", [128, 768], BF16)
    b['Xu0'] = K.sbuf("q_Xu0", [64, 2048], BF16)
    b['Xu1'] = K.sbuf("q_Xu1", [64, 2048], BF16)
    b['Xw0'] = K.sbuf("q_Xw0", [64, 2048], BF16)
    b['Xw1'] = K.sbuf("q_Xw1", [64, 2048], BF16)
    b['gm0'] = K.sbuf("q_gm0", [64, 64], BF16)
    b['gm1'] = K.sbuf("q_gm1", [64, 64], BF16)
    b['btc0'] = K.sbuf("q_btc0", [64, 128], BF16)
    b['btc1'] = K.sbuf("q_btc1", [64, 128], BF16)
    b['xtc0'] = K.sbuf("q_xtc0", [64, 2048], BF16)
    b['xtc1'] = b['xtc0']
    b['T'] = K.sbuf("q_T", [128, 2048], BF16)
    b['Yf'] = K.sbuf("q_Yf", [128, 8192], BF16)
    b['xn'] = b['Yf']
    b['zg'] = K.sbuf("q_zg", [128, 8192], BF16)
    b['scr1'] = K.sbuf("q_scr1", [128, 512], BF16)
    b['convd'] = K.sbuf("q_convd", [128, 72 * 128], BF16)
    b['convb'] = K.sbuf("q_convb", [128, 18], F32)
    b['cwt'] = K.sbuf("q_cwt", [128, 72], F32)
    b['dtb'] = K.sbuf("q_dtb", [32, 1], F32)
    b['negA'] = K.sbuf("q_negA", [32, 2], F32)
    b['Dcol'] = K.sbuf("q_Dcol", [128, 16], F32)
    b['nwcol'] = K.sbuf("q_nw", [128, 16], F32)
    b['flagc'] = K.sbuf("q_flagc", [128, 1], F32)
    b['tril'] = K.sbuf("q_tril", [64, 64], F32)
    b['triu_m'] = K.sbuf("q_trium", [64, 64], F32)
    b['esel'] = K.sbuf("q_esel", [64, 2], F32)
    b['e31'] = b['esel'][:, 0:1]
    b['e63'] = b['esel'][:, 1:2]
    b['ident'] = K.sbuf("q_ident", [64, 64], F32)
    b['ps_stat'] = K.psum("q_psstat", [1, 1024], F32)
    b['ps_mr'] = K.psum("q_psmr", [128, 1024], F32)
    b['ps_big'] = K.psum("q_psbig", [128, 2048], F32)
    return b


def emit_p1_setup(K, C, b, hb):
    nc = K.nc
    for nm, dst in (("cwt", b['cwt']), ("dtb", b['dtb']), ("Dcol", b['Dcol']),
                    ("nwcol", b['nwcol']), ("tril", b['tril']),
                    ("triu", b['triu_m']), ("esel", b['esel']),
                    ("ident", b['ident'])):
        K.emit("sync", lambda e, nm=nm, dst=dst: e.dma_start(out=dst[:, :], in_=hb[nm]),
               writes=("q_" + nm,), dma=True)
    K.emit("sync", lambda e: e.dma_start(out=b['convb'][:, :], in_=hb["convb"]),
           writes=("q_convb",), dma=True)
    K.emit("sync", lambda e: e.dma_start(out=b['flagc'][0:1, 0:1], in_=hb["hflag"]),
           writes=("q_flagc",), dma=True)
    K.emit("tensor", lambda e: e.matmul(b['ps_mr'][:, 0:1], C['ones_row128'][0:1, :], b['flagc'][0:1, 0:1],
                                        start=True, stop=True), reads=("q_flagc", "ones_row128"), writes=("q_repA",))
    K.emit("vector", lambda e: e.tensor_copy(b['flagc'][:, :], b['ps_mr'][:, 0:1]),
           reads=("q_repA",), writes=("q_flagc",))
    K.emit("gpsimd", lambda e: e.dma_start(out=b['negA'][:, 0:1], in_=hb["A_log"]),
           writes=("q_negA",), dma=True)
    K.emit("scalar", lambda e: e.activation(b['negA'][:, 1:2], b['negA'][:, 0:1], AF.Exp),
           reads=("q_negA",), writes=("q_negA1",))
    K.emit("vector", lambda e: e.tensor_scalar_mul(b['negA'][:, 0:1], b['negA'][:, 1:2], -1.0),
           reads=("q_negA1",), writes=("q_negA0",))
    b['hb_wop'] = hb["w_op"]
    identb = b['convd'][:, 0:128]
    K.emit("gpsimd", lambda e: e.dma_start(out=identb, in_=hb["ident128"]),
           writes=("q_convd",), dma=True)
    pairs = [(k, ch) for k in range(4) for ch in range(18)]
    pairs.remove((0, 0))
    pairs.append((0, 0))
    for k, ch in pairs:
        K.emit("vector", lambda e, k=k, ch=ch: e.tensor_scalar(
            b['convd'][:, (k * 18 + ch) * 128:(k * 18 + ch + 1) * 128],
            b['convd'][:, 0:128], b['cwt'][:, ch * 4 + k:ch * 4 + k + 1], None, ALU.mult),
            reads=("q_convd", "q_cwt"), writes=("q_convd",))


def emit_p1_batch(K, C, b, xA, xB, bidx):
    nc = K.nc
    TM = 512
    xt, xsq, xn = b['xt'], b['xsq'], b['xn']
    ps_mr, ps_stat, ps_big = b['ps_mr'], b['ps_stat'], b['ps_big']
    hbw = b['hbw']
    kbig = "q_big"
    K.emit("gpsimd", lambda e: e.memset(b['T'][:, :], 0.0), writes=("q_T",))

    tiles = [(0, 512), (512, 260)]
    for ti, (t0, T) in enumerate(tiles):
        if T == TM:
            K.emit("gpsimd", lambda e, t0=t0: e.dma_start(
                out=xt[:, :].rearrange("p (c t) -> p c t", t=TM),
                in_=xA[bidx, :, t0:t0 + TM].rearrange("(c p) l -> p c l", p=128)),
                writes=("q_xt",), dma=True)
        else:
            for c in range(8):
                K.emit("gpsimd", lambda e, c=c, t0=t0, T=T: e.dma_start(
                    out=xt[:, c * TM:c * TM + T], in_=xA[bidx, c * 128:(c + 1) * 128, t0:t0 + T]),
                    writes=("q_xt",), dma=True)
        emit_ln(K, C, xt, xsq, ps_stat, ps_mr, b['mvec'], T, tag="q_")
        for c in range(8):
            K.emit("vector", lambda e, c=c, T=T: e.tensor_tensor(
                xsq[:, c * TM:c * TM + T], xt[:, c * TM:c * TM + T], ps_mr[:, 0:T], ALU.subtract),
                reads=("q_xt", "q_repA"), writes=("q_xsq",))
            K.emit("vector", lambda e, c=c, T=T, t0=t0: e.tensor_tensor(
                xn[:, c * 772 + t0: c * 772 + t0 + T], xsq[:, c * TM:c * TM + T], ps_mr[:, TM:TM + T], ALU.mult),
                reads=("q_xsq", "q_repB"), writes=("q_Yf",))
    groups = [(i * 256, 256) for i in range(17)] + [(4352, 32)]
    for gi, (g0, gw) in enumerate(groups):
        wbuf = b['w_in0']
        wkey = "q_win0"
        K.emit("gpsimd", lambda e, g0=g0, gw=gw, wbuf=wbuf: e.dma_start(
            out=wbuf[:, 0:8 * gw].rearrange("p (ic c) -> p ic c", c=gw),
            in_=hbw[:, g0:g0 + gw].rearrange("(ic p) c -> p ic c", p=128)),
            writes=(wkey,), dma=True)
        nj = (gw + 127) // 128
        for j in range(nj):
            fc_abs = (g0 + j * 128) // 128
            M = min(128, gw - j * 128)
            for ti, (t0, T) in enumerate(tiles):
                half = (2 * fc_abs + ti) % 2
                pk = "q_rep%s" % ("A" if half == 0 else "B")
                for ic in range(8):
                    K.emit("tensor", lambda e, ic=ic, j=j, M=M, half=half, gw=gw, T=T, t0=t0, wbuf=wbuf: e.matmul(
                        ps_mr[0:M, half * 512:half * 512 + T],
                        wbuf[:, ic * gw + j * 128: ic * gw + j * 128 + M],
                        xn[:, ic * 772 + t0: ic * 772 + t0 + T],
                        start=(ic == 0), stop=(ic == 7)),
                        reads=(wkey, "q_Yf"), writes=(pk,))
                if fc_abs < 16:
                    cc = fc_abs
                    if ti == 0:
                        K.emit("scalar", lambda e, cc=cc, half=half: e.activation(
                            b['zg'][:, cc * 512: cc * 512 + 252],
                            ps_mr[:, half * 512 + 260: half * 512 + 512], AF.Copy),
                            reads=(pk,), writes=("q_zg",))
                    else:
                        K.emit("scalar", lambda e, cc=cc, half=half: e.activation(
                            b['zg'][:, cc * 512 + 252: cc * 512 + 512],
                            ps_mr[:, half * 512: half * 512 + 260], AF.Copy),
                            reads=(pk,), writes=("q_zg",))
                elif fc_abs < 34:
                    ch = fc_abs - 16
                    if ti == 0:
                        K.emit("scalar", lambda e, ch=ch, half=half: e.activation(
                            b['xbc_pre'][:, ch * 772: ch * 772 + 512],
                            ps_mr[:, half * 512: half * 512 + 512], AF.Copy),
                            reads=(pk,), writes=("q_xbcp",))
                    else:
                        K.emit("scalar", lambda e, ch=ch, half=half: e.activation(
                            b['xbc_pre'][:, ch * 772 + 512: ch * 772 + 772],
                            ps_mr[:, half * 512: half * 512 + 260], AF.Copy),
                            reads=(pk,), writes=("q_xbcp",))
                else:
                    K.emit("vector", lambda e, half=half, t0=t0, T=T: e.tensor_copy(
                        b['dtraw'][0:32, t0:t0 + T], ps_mr[0:32, half * 512: half * 512 + T]),
                        reads=(pk,), writes=("q_dtraw",))
    for ti in range(2):
        if ti == 0:
            o0, OT, rs = 0, 508, 1
        else:
            o0, OT, rs = 508, 260, 509
        for ch in range(18):
            for k in range(4):
                K.emit("tensor", lambda e, ch=ch, k=k, OT=OT, rs=rs: e.matmul(
                    ps_mr[:, 0:OT], b['convd'][:, ((3 - k) * 18 + ch) * 128:((3 - k) * 18 + ch + 1) * 128],
                    b['xbc_pre'][:, ch * 772 + rs + k: ch * 772 + rs + k + OT],
                    start=(k == 0), stop=(k == 3)), reads=("q_convd", "q_xbcp"), writes=("q_repA",))
            K.emit("scalar", lambda e, ch=ch, o0=o0, OT=OT: e.activation(
                b['xbc2'][:, ch * 768 + o0: ch * 768 + o0 + OT], ps_mr[:, 0:OT], AF.Silu,
                bias=b['convb'][:, ch:ch + 1]),
                reads=("q_repA", "q_convb"), writes=("q_xbc2",))
    dts = b['dts']
    dA = dts[0:32, :]
    dR = b['dtraw'][0:32, :]
    K.emit("vector", lambda e: e.tensor_scalar(dA, dR, b['dtb'][:, :], None, ALU.add),
           reads=("q_dtraw", "q_dtb"), writes=("q_xsq",))
    K.emit("scalar", lambda e: e.activation(dR, dA, AF.Abs),
           reads=("q_xsq",), writes=("q_dtraw",))
    K.emit("scalar", lambda e: e.activation(dR, dR, AF.Exp, scale=-1.0),
           reads=("q_dtraw",), writes=("q_dtraw",))
    K.emit("scalar", lambda e: e.activation(dR, dR, AF.Ln, bias=1.0),
           reads=("q_dtraw",), writes=("q_dtraw",))
    K.emit("vector", lambda e: e.scalar_tensor_tensor(dA, dA, 0.0, dR, ALU.max, ALU.add),
           reads=("q_xsq", "q_dtraw"), writes=("q_xsq",))
    K.emit("vector", lambda e: e.tensor_copy(b['sdt'][32:64, 0:772], dts[0:32, :]),
           reads=("q_xsq",), writes=("q_sdt",))
    K.emit("vector", lambda e: e.tensor_scalar(b['sdt'][32:64, 0:260], b['sdt'][32:64, 0:260],
                                               b['flagc'][32:64, :], None, ALU.mult),
           reads=("q_sdt", "q_flagc"), writes=("q_sdt",))
    K.emit("vector", lambda e: e.tensor_scalar(b['sdt'][0:32, 0:772], dts[0:32, :], b['negA'][:, 0:1], None, ALU.mult),
           reads=("q_xsq", "q_negA0"), writes=("q_sdt",))
    for w in range(3):
        for i in range(4):
            c = w * 4 + i
            K.emit("tensor", lambda e, c=c, i=i: e.transpose(
                ps_big[0:64, i * 512: i * 512 + 64],
                b['sdt'][:, 4 + 64 * c: 68 + 64 * c], b['ident'][:, :]),
                reads=("q_sdt", "q_ident"), writes=(kbig,))
        for i in range(4):
            c = w * 4 + i
            K.emit("vector", lambda e, c=c, i=i: e.tensor_copy(
                b['sposatmp'][:, 384 + c * 32: 384 + (c + 1) * 32], ps_big[0:64, i * 512: i * 512 + 32]),
                reads=(kbig,), writes=("q_sdt",))
            K.emit("vector", lambda e, c=c, i=i: e.tensor_copy(
                b['sposatmp'][:, c * 32:(c + 1) * 32], ps_big[0:64, i * 512 + 32: i * 512 + 64]),
                reads=(kbig,), writes=("q_sdt",))
    K.emit("tensor", lambda e: e.matmul(ps_big[0:64, 0:384], b['tril'][:, :],
                                        b['sposatmp'][:, 384:768], start=True, stop=True),
           reads=("q_tril", "q_sdt"), writes=(kbig,))
    K.emit("vector", lambda e: e.tensor_copy(b['s_all'][:, 0:384], ps_big[0:64, 0:384]),
           reads=(kbig,), writes=("q_sall",))
    K.emit("tensor", lambda e: e.matmul(ps_big[0:1, 0:384], b['esel'][:, 0:1],
                                        b['s_all'][:, 0:384], start=True, stop=True),
           reads=("q_esel", "q_sall"), writes=(kbig,))
    K.emit("vector", lambda e: e.tensor_copy(b['bmid'][0:1, 0:384], ps_big[0:1, 0:384]),
           reads=(kbig,), writes=("q_bmid",))
    K.emit("tensor", lambda e: e.matmul(ps_big[0:1, 0:384], b['esel'][:, 1:2],
                                        b['s_all'][:, 0:384], start=True, stop=True),
           reads=("q_esel", "q_sall"), writes=(kbig,))
    K.emit("vector", lambda e: e.tensor_copy(b['slast'][0:1, 0:384], ps_big[0:1, 0:384]),
           reads=(kbig,), writes=("q_slast",))
    K.emit("vector", lambda e: e.tensor_tensor(b['wbase'][0:1, 0:352], b['slast'][0:1, 0:352],
                                               b['bmid'][0:1, 32:384], ALU.add),
           reads=("q_slast", "q_bmid"), writes=("q_wbase",))
    K.emit("vector", lambda e: e.tensor_copy(b['wbase'][0:1, 352:384], b['slast'][0:1, 352:384]),
           reads=("q_slast",), writes=("q_wbase",))
    K.emit("vector", lambda e: e.tensor_tensor(b['g1row'][0:1, 0:384], b['wbase'][0:1, 0:384],
                                               b['bmid'][0:1, 0:384], ALU.subtract),
           reads=("q_wbase", "q_bmid"), writes=("q_g1row",))
    K.emit("scalar", lambda e: e.activation(b['g1row'][0:1, 0:384], b['g1row'][0:1, 0:384], AF.Exp),
           reads=("q_g1row",), writes=("q_g1row",))
    for cp in range(6):
        K.emit("sync", lambda e, cp=cp: e.dma_start_transpose(
            b['BT'][:, cp * 128:(cp + 1) * 128], b['xbc2'][:, 16 * 768 + cp * 128: 16 * 768 + (cp + 1) * 128]),
            reads=("q_xbc2",), writes=("q_BT",), dma=True)
    def prep(c):
        q = c % 2
        cp, off = c // 2, (c % 2) * 64
        uvw, ubf = b['uvw%d' % q], b['ubf%d' % q]
        kuvw, kubf = "q_uvw%d" % q, "q_ubf%d" % q
        kXT, kxtc = "q_XT0", "q_xtc0"
        kXu, kXw = "q_Xu%d" % q, "q_Xw%d" % q
        if c % 2 == 0:
            for cc in range(16):
                K.emit("sync", lambda e, cc=cc, cp=cp: e.dma_start_transpose(
                    b['XT0'][:, cc * 128:(cc + 1) * 128],
                    b['xbc2'][:, cc * 768 + cp * 128: cc * 768 + (cp + 1) * 128]),
                    reads=("q_xbc2",), writes=("q_XT0",), dma=True)
        K.emit("tensor", lambda e, c=c: e.matmul(ps_mr[0:64, 64:96], C['ones_row128'][0:1, 0:64],
                                                 b['bmid'][0:1, c * 32:(c + 1) * 32], start=True, stop=True),
               reads=("ones_row128", "q_bmid"), writes=("q_repA",))
        K.emit("vector", lambda e, c=c, uvw=uvw: e.tensor_tensor(uvw[:, 0:32], ps_mr[0:64, 64:96],
                                                        b['s_all'][:, c * 32:(c + 1) * 32], ALU.subtract),
               reads=("q_repA", "q_sall"), writes=(kuvw,))
        K.emit("vector", lambda e, c=c, uvw=uvw: e.tensor_tensor(uvw[:, 32:64], b['s_all'][:, c * 32:(c + 1) * 32],
                                                        ps_mr[0:64, 64:96], ALU.subtract),
               reads=("q_repA", "q_sall"), writes=(kuvw,))
        K.emit("tensor", lambda e, c=c: e.matmul(ps_mr[0:64, 96:128], C['ones_row128'][0:1, 0:64],
                                                 b['wbase'][0:1, c * 32:(c + 1) * 32], start=True, stop=True),
               reads=("ones_row128", "q_wbase"), writes=("q_repA",))
        K.emit("vector", lambda e, c=c, uvw=uvw: e.tensor_tensor(uvw[:, 64:96], ps_mr[0:64, 96:128],
                                                        b['s_all'][:, c * 32:(c + 1) * 32], ALU.subtract),
               reads=("q_repA", "q_sall"), writes=(kuvw,))
        K.emit("scalar", lambda e, uvw=uvw: e.activation(uvw[:, 0:96], uvw[:, 0:96], AF.Exp),
               reads=(kuvw,), writes=(kuvw,))
        K.emit("vector", lambda e, c=c, uvw=uvw, ubf=ubf: e.tensor_tensor(ubf[:, 0:32], uvw[:, 0:32],
                                                        b['sposatmp'][:, c * 32:(c + 1) * 32], ALU.mult),
               reads=(kuvw, "q_sdt"), writes=(kubf,))
        K.emit("vector", lambda e, uvw=uvw, ubf=ubf: e.tensor_copy(ubf[:, 32:64], uvw[:, 32:64]),
               reads=(kuvw,), writes=(kubf,))
        K.emit("vector", lambda e, c=c, uvw=uvw, ubf=ubf: e.tensor_tensor(ubf[:, 64:96], uvw[:, 64:96],
                                                        b['sposatmp'][:, c * 32:(c + 1) * 32], ALU.mult),
               reads=(kuvw, "q_sdt"), writes=(kubf,))
        ub = ubf[:, 0:32].rearrange("p (h q) -> p h q", q=1).broadcast_to([64, 32, 64])
        wb = ubf[:, 64:96].rearrange("p (h q) -> p h q", q=1).broadcast_to([64, 32, 64])
        xtc = b['xtc0']
        K.emit("vector", lambda e, off=off, xtc=xtc: e.tensor_copy(xtc[:, :], b['XT0'][off:off + 64, :]),
               reads=("q_XT0",), writes=(kxtc,))
        xtv = xtc[:, :].rearrange("p (h q) -> p h q", q=64)
        K.emit("vector", lambda e, q2=q: e.tensor_tensor(b['Xu%d' % q2][:, :].rearrange("p (h q) -> p h q", q=64), xtv, ub, ALU.mult),
               reads=(kxtc, kubf), writes=(kXu,))
        K.emit("vector", lambda e, q2=q: e.tensor_tensor(b['Xw%d' % q2][:, :].rearrange("p (h q) -> p h q", q=64), xtv, wb, ALU.mult),
               reads=(kxtc, kubf), writes=(kXw,))
        K.emit("vector", lambda e, cp=cp, off=off, q2=q: e.tensor_copy(
            b['btc%d' % q2][:, :], b['BT'][off:off + 64, cp * 128:(cp + 1) * 128]),
            reads=("q_BT",), writes=("q_btc%d" % q,))
        if c >= 4:
            K.emit("tensor", lambda e, c=c: e.matmul(ps_mr[0:64, 448:512],
                                                     b['xbc2'][:, 16 * 768 + c * 64: 16 * 768 + (c + 1) * 64],
                                                     b['xbc2'][:, 17 * 768 + c * 64: 17 * 768 + (c + 1) * 64],
                                                     start=True, stop=True),
                   reads=("q_xbc2",), writes=("q_repA",))
            K.emit("vector", lambda e, q2=q: e.tensor_tensor(b['gm%d' % q2][:, :], ps_mr[0:64, 448:512], b['triu_m'][:, :], ALU.mult),
                   reads=("q_repA", "q_trium"), writes=("q_gm%d" % q,))

    def main(c):
        q = c % 2
        is_blk = c >= 4
        kXu, kXw = "q_Xu%d" % q, "q_Xw%d" % q
        if is_blk:
            for nb in range(4):
                K.emit("tensor", lambda e, nb=nb, q2=q: e.matmul(
                    ps_big[0:64, nb * 512:(nb + 1) * 512], b['gm%d' % q2][:, :], b['Xu%d' % q2][:, nb * 512:(nb + 1) * 512],
                    start=True, stop=False), reads=("q_gm%d" % q, kXu), writes=(kbig,))
                K.emit("tensor", lambda e, nb=nb, c=c: e.matmul(
                    ps_big[0:64, nb * 512:(nb + 1) * 512],
                    b['xbc2'][:, 17 * 768 + c * 64: 17 * 768 + (c + 1) * 64],
                    b['T'][:, nb * 512:(nb + 1) * 512],
                    start=False, stop=True), reads=("q_xbc2", "q_T"), writes=(kbig,))
            vb = b['ubf%d' % q][:, 32:64].rearrange("p (h q) -> p h q", q=1).broadcast_to([64, 32, 64])
            K.emit("vector", lambda e, q2=q: e.tensor_tensor(b['Xu%d' % q2][:, :].rearrange("p (h q) -> p h q", q=64),
                                                       ps_big[0:64, 0:2048].rearrange("p (h q) -> p h q", q=64),
                                                       vb, ALU.mult),
                   reads=(kbig, "q_ubf%d" % q, kXu), writes=(kXu,))
            for cc in range(16):
                K.emit("sync", lambda e, cc=cc, c=c, q2=q: e.dma_start_transpose(
                    b['Yf'][:, cc * 512 + (c - 4) * 64: cc * 512 + (c - 4) * 64 + 64],
                    b['Xu%d' % q2][:, cc * 128:(cc + 1) * 128]),
                    reads=(kXu,), writes=("q_Yf",), dma=True)
        K.emit("vector", lambda e, q2=q: e.tensor_copy(b['btc%d' % q2][:, :], b['btc%d' % q2][:, :]),
               reads=("q_btc%d" % q,), writes=("q_btc%d" % q,)) if False else None
        for nb in range(4):
            K.emit("tensor", lambda e, nb=nb, q2=q: e.matmul(
                ps_big[0:128, nb * 512:(nb + 1) * 512], b['btc%d' % q2][:, :],
                b['Xw%d' % q2][:, nb * 512:(nb + 1) * 512],
                start=True, stop=True), reads=("q_btc%d" % q, kXw), writes=(kbig,))
        K.emit("tensor", lambda e, c=c: e.matmul(ps_mr[0:128, 0:32], C['ones_row128'][0:1, :],
                                                 b['g1row'][0:1, c * 32:(c + 1) * 32], start=True, stop=True),
               reads=("ones_row128", "q_g1row"), writes=("q_repA",))
        g1v = ps_mr[0:128, 0:32].rearrange("p (h q) -> p h q", q=1).broadcast_to([128, 32, 64])
        K.emit("vector", lambda e: e.tensor_tensor(b['T'][:, :].rearrange("p (h q) -> p h q", q=64),
                                                   b['T'][:, :].rearrange("p (h q) -> p h q", q=64), g1v, ALU.mult),
               reads=("q_T", "q_repA"), writes=("q_T",))
        K.emit("vector", lambda e: e.tensor_tensor(b['T'][:, :], b['T'][:, :], ps_big[0:128, 0:2048], ALU.add),
               reads=("q_T", kbig), writes=("q_T",))

    prep(0)
    for c in range(12):
        if c + 1 < 12:
            prep(c + 1)
        main(c)
    for cc in range(16):
        K.emit("scalar", lambda e, cc=cc: e.activation(b['scr1'][:, :], b['zg'][:, cc * 512:(cc + 1) * 512], AF.Silu),
               reads=("q_zg",), writes=("q_scr1",))
        K.emit("vector", lambda e, cc=cc: e.tensor_tensor(
            b['Yf'][:, cc * 512:(cc + 1) * 512], b['Yf'][:, cc * 512:(cc + 1) * 512], b['scr1'][:, :], ALU.mult),
            reads=("q_Yf", "q_scr1"), writes=("q_Yf",))
        K.emit("vector", lambda e, cc=cc: e.scalar_tensor_tensor(
            b['scr1'][:, :], b['xbc2'][:, cc * 768 + 256: cc * 768 + 768], b['Dcol'][:, cc:cc + 1],
            b['scr1'][:, :], ALU.mult, ALU.mult),
            reads=("q_xbc2", "q_Dcol", "q_scr1"), writes=("q_scr1",))
        K.emit("vector", lambda e, cc=cc: e.tensor_tensor(
            b['Yf'][:, cc * 512:(cc + 1) * 512], b['Yf'][:, cc * 512:(cc + 1) * 512], b['scr1'][:, :], ALU.add),
            reads=("q_Yf", "q_scr1"), writes=("q_Yf",))
        K.emit("scalar", lambda e, cc=cc: e.activation(b['scr1'][:, :], b['Yf'][:, cc * 512:(cc + 1) * 512], AF.Square),
               reads=("q_Yf",), writes=("q_scr1",))
        K.emit("tensor", lambda e, cc=cc: e.matmul(ps_stat[0:1, 0:512], C['onesb_col'][:, :], b['scr1'][:, :],
                                                   start=(cc == 0), stop=(cc == 15)),
               reads=("q_scr1", "onesb_col"), writes=("q_stat",))
    K.emit("vector", lambda e: e.tensor_scalar_mul(b['mvec'][0:1, 0:512], ps_stat[0:1, 0:512], 1.0 / 2048.0),
           reads=("q_stat",), writes=("q_mvecA",))
    K.emit("scalar", lambda e: e.activation(b['mvec'][0:1, 0:512], b['mvec'][0:1, 0:512], AF.Ln, bias=EPS),
           reads=("q_mvecA",), writes=("q_mvecA",))
    K.emit("scalar", lambda e: e.activation(b['mvec'][0:1, 0:512], b['mvec'][0:1, 0:512], AF.Exp, scale=-0.5),
           reads=("q_mvecA",), writes=("q_mvecA",))
    K.emit("tensor", lambda e: e.matmul(ps_mr[:, 0:512], C['ones_row128'][0:1, :], b['mvec'][0:1, 0:512],
                                        start=True, stop=True), reads=("ones_row128", "q_mvecA"), writes=("q_repA",))
    for cc in range(16):
        K.emit("vector", lambda e, cc=cc: e.scalar_tensor_tensor(
            b['zg'][:, cc * 512:(cc + 1) * 512], b['Yf'][:, cc * 512:(cc + 1) * 512],
            b['nwcol'][:, cc:cc + 1], ps_mr[:, 0:512], ALU.mult, ALU.mult),
            reads=("q_Yf", "q_nw", "q_repA"), writes=("q_zg",))
    K.emit("gpsimd", lambda e: e.dma_start(
        out=xt[:, :].rearrange("p (c t) -> p c t", t=TM),
        in_=xA[bidx, :, 260:772].rearrange("(c p) l -> p c l", p=128)),
        writes=("q_xt",), dma=True)
    for oc in range(8):
        K.emit("gpsimd", lambda e, oc=oc: e.dma_start(
            out=b['w_op'][:, :].rearrange("p (kc c) -> p kc c", c=128),
            in_=b['hb_wop'][:, oc * 128:(oc + 1) * 128].rearrange("(kc p) c -> p kc c", p=128)),
            writes=("q_wop",), dma=True)
        for kc in range(16):
            K.emit("tensor", lambda e, oc=oc, kc=kc: e.matmul(
                ps_mr[:, (oc % 2) * 512:(oc % 2) * 512 + 512],
                b['w_op'][:, kc * 128:(kc + 1) * 128],
                b['zg'][:, kc * 512:(kc + 1) * 512],
                start=(kc == 0), stop=(kc == 15)), reads=("q_wop", "q_zg"),
                writes=("q_rep%s" % ("A" if oc % 2 == 0 else "B"),))
        K.emit("vector", lambda e, oc=oc: e.tensor_tensor(
            xsq[:, oc * TM:(oc + 1) * TM], ps_mr[:, (oc % 2) * 512:(oc % 2) * 512 + 512],
            xt[:, oc * TM:(oc + 1) * TM], ALU.add),
            reads=("q_rep%s" % ("A" if oc % 2 == 0 else "B"), "q_xt"), writes=("q_xsq",))
        K.emit("gpsimd", lambda e, oc=oc: e.dma_start(
            out=xB[bidx, oc * 128:(oc + 1) * 128, 0:512], in_=xsq[:, oc * TM:(oc + 1) * TM]),
            reads=("q_xsq",), dma=True)


def emit_p1(K, C, b, xA, xB):
    for bidx in range(16):
        emit_p1_batch(K, C, b, xA, xB, bidx)



def alloc_p3(K, pfx="m_"):
    b = {'pfx': pfx}
    S = lambda n, sh, dt: K.sbuf(pfx + n, sh, dt)
    b['xt'] = S("xt", [128, 4096], F32R)
    b['xsq'] = S("xsq", [128, 4096], F32R)
    b['xn'] = S("xn", [128, 4096], BF16)
    b['xnr'] = S("xnr", [128, 4096], F32R)
    b['mvec'] = S("mvec", [1, 1024], F32)
    b['wc'] = S("wc", [128, 1024], F32R)
    b['mx'] = S("mx", [128, 16], F32)
    b['maskb'] = S("maskb", [128, 8192], BF16)
    b['maskT'] = S("maskT", [64, 2048], BF16)
    b['xnT'] = S("xnT", [64, 8192], BF16)
    b['xselsb'] = S("xselsb", [128, 1024], BF16)
    for p_ in (0, 1):
        b['w1sb%d' % p_] = S("w1sb%d" % p_, [128, 256], BF16)
        b['w2sb%d' % p_] = S("w2sb%d" % p_, [32, 1024], BF16)
        b['b2sb%d' % p_] = S("b2sb%d" % p_, [1, 1024], BF16)
        b['innersb%d' % p_] = S("innersb%d" % p_, [32, 256], BF16)
        b['xsel_es%d' % p_] = S("xsel_es%d" % p_, [128, 2048], BF16)
        b['rsb%d' % p_] = S("rsb%d" % p_, [128, 4096], BF16)
    b['b1sb'] = S("b1sb", [32, 128], F32)
    b['scat64'] = S("scat64", [64, 8192], BF16)
    b['scatT'] = S("scatT", [128, 512], BF16)
    b['ps_stat'] = K.psum(pfx + "psstat", [1, 1024], F32)
    b['ps_mr'] = K.psum(pfx + "psmr", [128, 1024], F32)
    b['ps_a'] = K.psum(pfx + "psa", [128, 1024], F32)
    b['ps_t'] = K.psum(pfx + "pst", [128, 128], BF16)
    b['ps_i'] = K.psum(pfx + "psi", [32, 256], F32)
    b['identb'] = S("identb", [128, 128], BF16)
    b['bcc'] = S("bcc", [128, 1], F32)
    b['ties'] = S("ties", [128, 512], F32)
    b['lgsb'] = S("lgsb", [128, 512], F32)
    return b


def emit_p3_setup(K, C, b, hb):
    pfx = b['pfx']
    for ic in range(8):
        K.emit("gpsimd", lambda e, ic=ic: e.dma_start(
            out=b['wc'][:, ic * 128:(ic + 1) * 128], in_=hb["wc"][ic * 128:(ic + 1) * 128, :]),
            writes=(pfx + "wc",), dma=True)
    K.emit("gpsimd", lambda e: e.dma_start(out=b['identb'][:, :], in_=hb["ident128"]),
           writes=(pfx + "identb",), dma=True)
    K.emit("sync", lambda e: e.dma_start(out=b['bcc'][:, :], in_=hb["bc"]),
           writes=(pfx + "bcc",), dma=True)
    K.emit("sync", lambda e: e.dma_start(out=b['ties'][:, :], in_=hb["tie"]),
           writes=(pfx + "ties",), dma=True)


def emit_p3(K, C, b, xsrc, xdst, hb, scratch):
    pfx = b['pfx']
    TM = 512
    xt, xsq, xn, xnr = b['xt'], b['xsq'], b['xn'], b['xnr']
    ps_mr, ps_stat, ps_a, ps_t, ps_i = b['ps_mr'], b['ps_stat'], b['ps_a'], b['ps_t'], b['ps_i']
    kmrA, kmrB = pfx + "repA", pfx + "repB"
    ka, kt, ki = pfx + "psa", pfx + "pst", pfx + "psi"
    xsel_h, r_h, scat_h = scratch['xsel_h'], scratch['r_h'], scratch['scat_h']

    for bidx in range(16):
        K.emit("gpsimd", lambda e, bidx=bidx: e.dma_start(
            out=xt[:, :].rearrange("p (c t) -> p c t", t=TM),
            in_=xsrc[bidx].rearrange("(c p) l -> p c l", p=128)),
            writes=(pfx + "xt",), dma=True)
        emit_ln(K, C, xt, xsq, ps_stat, ps_mr, b['mvec'], TM, tag=pfx)
        for c in range(8):
            K.emit("vector", lambda e, c=c: e.tensor_tensor(
                xnr[:, c * TM:(c + 1) * TM], xt[:, c * TM:(c + 1) * TM], ps_mr[:, 0:TM], ALU.subtract),
                reads=(pfx + "xt", kmrA), writes=(pfx + "xnr",))
            K.emit("vector", lambda e, c=c: e.tensor_tensor(
                xnr[:, c * TM:(c + 1) * TM], xnr[:, c * TM:(c + 1) * TM], ps_mr[:, TM:2 * TM], ALU.mult),
                reads=(pfx + "xnr", kmrB), writes=(pfx + "xnr",))
            K.emit("scalar", lambda e, c=c: e.activation(
                xn[:, c * TM:(c + 1) * TM], xnr[:, c * TM:(c + 1) * TM], AF.Copy),
                reads=(pfx + "xnr",), writes=(pfx + "xn",))
        for ic in range(8):
            K.emit("tensor", lambda e, ic=ic: e.matmul(
                ps_a[:, 0:512], b['wc'][:, ic * 128:(ic + 1) * 128], xnr[:, ic * TM:(ic + 1) * TM],
                start=(ic == 0), stop=(ic == 7)), reads=(pfx + "wc", pfx + "xnr"), writes=(ka,))
        K.emit("vector", lambda e: e.tensor_scalar(
            b['lgsb'][:, :], ps_a[:, 0:512], b['bcc'][:, :], None, ALU.add),
            reads=(ka, pfx + "bcc"), writes=(pfx + "lgsb",))
        K.emit("vector", lambda e: e.tensor_tensor(
            b['lgsb'][:, :], b['lgsb'][:, :], b['ties'][:, :], ALU.add),
            reads=(pfx + "lgsb", pfx + "ties"), writes=(pfx + "lgsb",))
        K.emit("vector", lambda e: e.tensor_reduce(
            b['mx'][:, :].rearrange("p (g o) -> p g o", o=1),
            b['lgsb'][:, :].rearrange("p (g t) -> p g t", t=32), AX.X, ALU.max),
            reads=(pfx + "lgsb",), writes=(pfx + "mx",))
        mxb = b['mx'][:, :].rearrange("p (g o) -> p g o", o=1).broadcast_to([128, 16, 32])
        K.emit("vector", lambda e, bidx=bidx: e.tensor_tensor(
            b['maskb'][:, bidx * 512:(bidx + 1) * 512].rearrange("p (g t) -> p g t", t=32),
            b['lgsb'][:, :].rearrange("p (g t) -> p g t", t=32), mxb, ALU.is_equal),
            reads=(pfx + "lgsb", pfx + "mx"), writes=(pfx + "maskb",))
        for g in range(16):
            K.emit("tensor", lambda e, g=g, bidx=bidx: e.transpose(
                ps_t[0:32, 0:128], b['maskb'][:, bidx * 512 + g * 32: bidx * 512 + (g + 1) * 32],
                b['identb'][:, :]), reads=(pfx + "maskb", pfx + "identb"), writes=(kt,))
            K.emit("vector", lambda e, g=g: e.tensor_copy(
                b['maskT'][(g % 2) * 32:(g % 2) * 32 + 32, (g // 2) * 128:(g // 2 + 1) * 128],
                ps_t[0:32, 0:128]), reads=(kt,), writes=(pfx + "maskT",))
        for pc in range(4):
            for ic in range(8):
                K.emit("tensor", lambda e, pc=pc, ic=ic: e.transpose(
                    ps_t[:, :], xn[:, ic * TM + pc * 128: ic * TM + (pc + 1) * 128], b['identb'][:, :]),
                    reads=(pfx + "xn", pfx + "identb"), writes=(kt,))
                K.emit("vector", lambda e, pc=pc, ic=ic: e.tensor_copy(
                    b['xnT'][0:64, (2 * pc) * 1024 + ic * 128:(2 * pc) * 1024 + (ic + 1) * 128],
                    ps_t[0:64, :]), reads=(kt,), writes=(pfx + "xnT",))
                K.emit("vector", lambda e, pc=pc, ic=ic: e.tensor_copy(
                    b['xnT'][0:64, (2 * pc + 1) * 1024 + ic * 128:(2 * pc + 1) * 1024 + (ic + 1) * 128],
                    ps_t[64:128, :]), reads=(kt,), writes=(pfx + "xnT",))
        for g in range(16):
            base = (g % 2) * 32
            for nh in range(2):
                K.emit("tensor", lambda e, g=g, nh=nh, base=base: e.matmul(
                    ps_a[:, nh * 512:(nh + 1) * 512],
                    b['maskT'][base:base + 32, (g // 2) * 128:(g // 2 + 1) * 128],
                    b['xnT'][base:base + 32, (g // 2) * 1024 + nh * 512:(g // 2) * 1024 + (nh + 1) * 512],
                    start=True, stop=True), reads=(pfx + "maskT", pfx + "xnT"), writes=(ka,))
            K.emit("vector", lambda e: e.tensor_copy(b['xselsb'][:, :], ps_a[:, 0:1024]),
                   reads=(ka,), writes=(pfx + "xselsb",))
            K.emit("sync", lambda e, g=g, bidx=bidx: e.dma_start(
                out=xsel_h[:, bidx * 16 + g, :], in_=b['xselsb'][:, :]),
                reads=(pfx + "xselsb",), writes=(pfx + "h_xsel",), dma=True)

    for es in range(128):
        p_ = es % 2
        w1sb, w2sb, b2sb = b['w1sb%d' % p_], b['w2sb%d' % p_], b['b2sb%d' % p_]
        innersb, xsel_es = b['innersb%d' % p_], b['xsel_es%d' % p_]
        kw1, kw2, kb2 = pfx + "w1sb%d" % p_, pfx + "w2sb%d" % p_, pfx + "b2sb%d" % p_
        kin, kxe = pfx + "innersb%d" % p_, pfx + "xsel_es%d" % p_
        kap = ka + str(p_)
        psa = ps_a[:, (p_ % 2) * 0:1024] if False else ps_a
        K.emit("gpsimd", lambda e, es=es, w1sb=w1sb: e.dma_start(
            out=w1sb[:, :].rearrange("p (dc f) -> p dc f", f=32),
            in_=hb["w1"][es].rearrange("dc p f -> p dc f")),
            writes=(kw1,), dma=True)
        K.emit("gpsimd", lambda e, es=es, w2sb=w2sb: e.dma_start(out=w2sb[:, :], in_=hb["w2"][es]),
               writes=(kw2,), dma=True)
        K.emit("gpsimd", lambda e, es=es, b2sb=b2sb: e.dma_start(out=b2sb[0:1, :], in_=hb["b2"][es:es + 1, :]),
               writes=(kb2,), dma=True)
        if es == 0:
            K.emit("sync", lambda e: e.dma_start(out=b['b1sb'][:, :], in_=hb["b1"]),
                   writes=(pfx + "b1sb",), dma=True)
        for dc in range(8):
            K.emit("sync", lambda e, es=es, dc=dc, xsel_es=xsel_es: e.dma_start_transpose(
                xsel_es[:, dc * 256:(dc + 1) * 256], xsel_h[es, :, dc * 128:(dc + 1) * 128]),
                reads=(pfx + "h_xsel",), writes=(kxe,), dma=True)
        for dc in range(8):
            K.emit("tensor", lambda e, dc=dc, w1sb=w1sb, xsel_es=xsel_es: e.matmul(
                ps_i[0:32, (p_ % 2) * 0:256] if False else ps_i[:, :], w1sb[:, dc * 32:(dc + 1) * 32], xsel_es[:, dc * 256:(dc + 1) * 256],
                start=(dc == 0), stop=(dc == 7)), reads=(kw1, kxe), writes=(ki,))
        K.emit("scalar", lambda e, es=es, innersb=innersb: e.activation(
            innersb[:, :], ps_i[:, :], AF.Relu, bias=b['b1sb'][:, es:es + 1]),
            reads=(ki, pfx + "b1sb"), writes=(kin,))
        for gc in range(2):
            for nh in range(2):
                K.emit("tensor", lambda e, nh=nh, b2sb=b2sb: e.matmul(
                    ps_a[:, nh * 512:(nh + 1) * 512], C['ones_row'][0:1, 0:128],
                    b2sb[0:1, nh * 512:(nh + 1) * 512],
                    start=True, stop=False), reads=(kb2, "ones_row"), writes=(ka,))
                K.emit("tensor", lambda e, gc=gc, nh=nh, innersb=innersb, w2sb=w2sb: e.matmul(
                    ps_a[:, nh * 512:(nh + 1) * 512],
                    innersb[:, gc * 128:(gc + 1) * 128],
                    w2sb[:, nh * 512:(nh + 1) * 512],
                    start=False, stop=True), reads=(kin, kw2), writes=(ka,))
            K.emit("vector", lambda e: e.tensor_copy(b['xselsb'][:, :], ps_a[:, 0:1024]),
                   reads=(ka,), writes=(pfx + "xselsb",))
            K.emit("sync", lambda e, es=es, gc=gc: e.dma_start(
                out=r_h[gc * 128:(gc + 1) * 128, es, :], in_=b['xselsb'][:, :]),
                reads=(pfx + "xselsb",), writes=(pfx + "h_r",), dma=True)

    for bidx in range(16):
        for gq in range(4):
            rsb = b['rsb%d' % (gq % 2)]
            krsb = pfx + "rsb%d" % (gq % 2)
            K.emit("gpsimd", lambda e, bidx=bidx, gq=gq, rsb=rsb: e.dma_start(
                out=rsb[:, :].rearrange("p (g d) -> p g d", d=1024),
                in_=r_h[bidx * 16 + gq * 4: bidx * 16 + gq * 4 + 4, :, :].rearrange("g p d -> p g d")),
                reads=(pfx + "h_r",), writes=(krsb,), dma=True)
            for gi in range(4):
                g = gq * 4 + gi
                for nh in range(2):
                    K.emit("tensor", lambda e, g=g, gi=gi, nh=nh, bidx=bidx, rsb=rsb: e.matmul(
                        ps_a[0:32, nh * 512:(nh + 1) * 512],
                        b['maskb'][:, bidx * 512 + g * 32: bidx * 512 + (g + 1) * 32],
                        rsb[:, gi * 1024 + nh * 512: gi * 1024 + (nh + 1) * 512],
                        start=True, stop=True), reads=(pfx + "maskb", krsb), writes=(ka,))
                K.emit("vector", lambda e, g=g: e.tensor_copy(
                    b['scat64'][(g % 2) * 32:(g % 2) * 32 + 32, (g // 2) * 1024:(g // 2 + 1) * 1024],
                    ps_a[0:32, 0:1024]), reads=(ka,), writes=(pfx + "scat64",))
        K.emit("sync", lambda e, bidx=bidx: e.dma_start(
            out=scat_h[bidx].rearrange("(gh gm t) d -> (gm t) gh d", gm=2, t=32),
            in_=b['scat64'][:, :].rearrange("p (gh d) -> p gh d", d=1024)),
            reads=(pfx + "scat64",), writes=(pfx + "h_scat",), dma=True)
        K.emit("gpsimd", lambda e, bidx=bidx: e.dma_start(
            out=xt[:, :].rearrange("p (c t) -> p c t", t=TM),
            in_=xsrc[bidx].rearrange("(c p) l -> p c l", p=128)),
            writes=(pfx + "xt",), dma=True)
        for dc in range(8):
            K.emit("sync", lambda e, dc=dc, bidx=bidx: e.dma_start_transpose(
                b['scatT'][:, :], scat_h[bidx, :, dc * 128:(dc + 1) * 128]),
                reads=(pfx + "h_scat",), writes=(pfx + "scatT",), dma=True)
            K.emit("vector", lambda e, dc=dc: e.scalar_tensor_tensor(
                xsq[:, dc * TM:(dc + 1) * TM], b['scatT'][:, :], 0.5, xt[:, dc * TM:(dc + 1) * TM],
                ALU.mult, ALU.add), reads=(pfx + "scatT", pfx + "xt"), writes=(pfx + "xsq",))
            K.emit("gpsimd", lambda e, dc=dc, bidx=bidx: e.dma_start(
                out=xdst[bidx, dc * 128:(dc + 1) * 128, :], in_=xsq[:, dc * TM:(dc + 1) * TM]),
                reads=(pfx + "xsq",), dma=True)



def alloc_p4(K):
    b = {}
    S = lambda n, sh, dt: K.sbuf("a_" + n, sh, dt)
    b['xt0'] = S("xt0", [128, 2048], F32R)
    b['xt1'] = b['xt0']
    b['xsq'] = S("xsq", [128, 2048], F32R)
    b['xn0'] = S("xn0", [128, 2048], BF16)
    b['xn1'] = S("xn1", [128, 2048], BF16)
    b['xnew'] = S("xnew", [128, 2048], F32)
    b['mvec'] = S("mvec", [1, 1024], F32)
    b['wk'] = S("wk", [128, 8192], BF16)
    b['wv'] = S("wv", [128, 8192], BF16)
    b['wq'] = S("wq", [128, 8192], BF16)
    b['S'] = S("S", [128, 16384], F32)
    b['scr'] = S("scr", [128, 4096], F32)
    b['kT0'] = S("kT0", [128, 2048], BF16)
    b['vT0'] = S("vT0", [128, 2048], BF16)
    b['qT0'] = S("qT0", [128, 2048], BF16)
    b['kT1'] = S("kT1", [128, 2048], BF16)
    b['vT1'] = S("vT1", [128, 2048], BF16)
    b['qT1'] = S("qT1", [128, 2048], BF16)
    b['ret'] = S("ret", [128, 1024], F32)
    b['sil'] = S("sil", [128, 1024], F32)
    b['attn'] = S("attn", [128, 1024], BF16)
    b['ssum'] = S("ssum", [128, 128], F32)
    b['rr'] = S("rr", [128, 128], F32)
    b['ps_stat'] = K.psum("a_psstat", [1, 1024], F32)
    b['ps_mr'] = K.psum("a_psmr", [128, 1024], F32)
    b['ps_t'] = K.psum("a_pst", [128, 128], BF16)
    b['identb'] = S("identb", [128, 128], BF16)
    return b


def emit_p4_setup(K, C, b, hb):
    for nm, dst in (("wkT", b['wk']), ("wvT", b['wv']), ("wqT", b['wq'])):
        for ic in range(8):
            K.emit("gpsimd", lambda e, nm=nm, dst=dst, ic=ic: e.dma_start(
                out=dst[:, ic * 1024:(ic + 1) * 1024], in_=hb[nm][ic * 128:(ic + 1) * 128, :]),
                writes=("a_" + nm,), dma=True)
    K.emit("gpsimd", lambda e: e.dma_start(out=b['identb'][:, :], in_=hb["ident128"]),
           writes=("a_identb",), dma=True)


def emit_p4(K, C, b, xsrc, xdst):
    TM = 256
    xsq = b['xsq']
    ps_mr, ps_stat, ps_t = b['ps_mr'], b['ps_stat'], b['ps_t']
    kmrA, kmrB, kt_ = "a_repA", "a_repB", "a_pst"
    for half in range(2):
        h0 = half * 256
        K.emit("gpsimd", lambda e: e.memset(b['S'][:, :], 0.0), writes=("a_S",))
        for t in range(16):
            par = t % 2
            xt, xn = b['xt0'], b['xn%d' % par]
            K.emit("gpsimd", lambda e, t=t, h0=h0: e.dma_start(
                out=xt[:, :].rearrange("p (c t2) -> p c t2", t2=TM),
                in_=xsrc[t, :, h0:h0 + 256].rearrange("(c p) l -> p c l", p=128)),
                writes=("a_xt",), dma=True)
            emit_ln(K, C, xt, xsq, ps_stat, ps_mr, b['mvec'], TM, tag="a_")
            for c in range(8):
                K.emit("vector", lambda e, c=c: e.tensor_tensor(
                    xsq[:, c * TM:(c + 1) * TM], xt[:, c * TM:(c + 1) * TM], ps_mr[:, 0:TM], ALU.subtract),
                    reads=("a_xt", kmrA), writes=("a_xsq",))
                K.emit("vector", lambda e, c=c: e.tensor_tensor(
                    xn[:, c * TM:(c + 1) * TM], xsq[:, c * TM:(c + 1) * TM], ps_mr[:, TM:2 * TM], ALU.mult),
                    reads=("a_xsq", kmrB), writes=("a_xn%d" % par,))
            for wnm, wt, dst, dk in (("wk", b['wk'], b['kT%d' % par], "a_kT%d" % par),
                                     ("wv", b['wv'], b['vT%d' % par], "a_vT%d" % par),
                                     ("wq", b['wq'], b['qT%d' % par], "a_qT%d" % par)):
                for pc in range(2):
                    for ic in range(8):
                        for nh in range(2):
                            K.emit("tensor", lambda e, pc=pc, ic=ic, nh=nh, wt=wt: e.matmul(
                                ps_mr[:, nh * 512:(nh + 1) * 512],
                                xn[:, ic * TM + pc * 128: ic * TM + (pc + 1) * 128],
                                wt[:, ic * 1024 + nh * 512: ic * 1024 + (nh + 1) * 512],
                                start=(ic == 0), stop=(ic == 7)),
                                reads=("a_xn%d" % par, "a_" + wnm + "T"), writes=(kmrA, kmrB))
                    K.emit("vector", lambda e, pc=pc, dst=dst: e.tensor_copy(
                        dst[:, pc * 1024:(pc + 1) * 1024], ps_mr[:, 0:1024]),
                        reads=(kmrA, kmrB), writes=(dk,))
            for pc in range(2):
                K.emit("scalar", lambda e, pc=pc, par=par: e.activation(
                    b['scr'][:, 0:1024], b['kT%d' % par][:, pc * 1024:(pc + 1) * 1024], AF.Square),
                    reads=("a_kT%d" % par,), writes=("a_scr",))
                K.emit("vector", lambda e: e.tensor_reduce(
                    b['ssum'][:, :].rearrange("p (n o) -> p n o", o=1),
                    b['scr'][:, 0:1024].rearrange("p (n j) -> p n j", j=8), AX.X, ALU.add),
                    reads=("a_scr",), writes=("a_ssum",))
                K.emit("scalar", lambda e: e.activation(b['rr'][:, :], b['ssum'][:, :], AF.Ln),
                       reads=("a_ssum",), writes=("a_rr",))
                K.emit("scalar", lambda e: e.activation(b['rr'][:, :], b['rr'][:, :], AF.Exp, scale=0.5),
                       reads=("a_rr",), writes=("a_rr",))
                K.emit("vector", lambda e: e.tensor_scalar_add(b['rr'][:, :], b['rr'][:, :], 1e-6),
                       reads=("a_rr",), writes=("a_rr",))
                K.emit("vector", lambda e: e.reciprocal(b['rr'][:, :], b['rr'][:, :]),
                       reads=("a_rr",), writes=("a_rr",))
                rrb = b['rr'][:, :].rearrange("p (n o) -> p n o", o=1).broadcast_to([128, 128, 8])
                K.emit("vector", lambda e, pc=pc, par=par: e.tensor_tensor(
                    b['kT%d' % par][:, pc * 1024:(pc + 1) * 1024].rearrange("p (n j) -> p n j", j=8),
                    b['kT%d' % par][:, pc * 1024:(pc + 1) * 1024].rearrange("p (n j) -> p n j", j=8),
                    rrb, ALU.mult), reads=("a_kT%d" % par, "a_rr"), writes=("a_kT%d" % par,))
            for pc in range(2):
                Spc = b['S'][:, pc * 8192:(pc + 1) * 8192]
                knb = b['kT%d' % par][:, pc * 1024:(pc + 1) * 1024]
                for sub in range(2):
                    Ssub = Spc[:, sub * 4096:(sub + 1) * 4096].rearrange("p (n i j) -> p n i j", i=8, j=8)
                    knsub = knb[:, sub * 512:(sub + 1) * 512].rearrange("p (n j) -> p n j", j=8)
                    knsub_bi = knb[:, sub * 512:(sub + 1) * 512].rearrange("p (n o j) -> p n o j", o=1, j=8).broadcast_to([128, 64, 8, 8])
                    K.emit("vector", lambda e, Ssub=Ssub, knsub_bi=knsub_bi: e.tensor_tensor(
                        b['scr'][:, :].rearrange("p (n i j) -> p n i j", i=8, j=8), Ssub, knsub_bi, ALU.mult),
                        reads=("a_S", "a_kT%d" % par), writes=("a_scr",))
                    K.emit("vector", lambda e, sub=sub: e.tensor_reduce(
                        b['ret'][:, sub * 512:(sub + 1) * 512].rearrange("p (f o) -> p f o", o=1),
                        b['scr'][:, :].rearrange("p (f j) -> p f j", j=8), AX.X, ALU.add),
                        reads=("a_scr",), writes=("a_ret",))
                K.emit("vector", lambda e, pc=pc, par=par: e.tensor_tensor(
                    b['ret'][:, :], b['vT%d' % par][:, pc * 1024:(pc + 1) * 1024], b['ret'][:, :], ALU.subtract),
                    reads=("a_vT%d" % par, "a_ret"), writes=("a_ret",))
                for sub in range(2):
                    Ssub = Spc[:, sub * 4096:(sub + 1) * 4096].rearrange("p (n i j) -> p n i j", i=8, j=8)
                    knsub_bi = knb[:, sub * 512:(sub + 1) * 512].rearrange("p (n o j) -> p n o j", o=1, j=8).broadcast_to([128, 64, 8, 8])
                    dsub_bj = b['ret'][:, sub * 512:(sub + 1) * 512].rearrange("p (n i o) -> p n i o", o=1, i=8).broadcast_to([128, 64, 8, 8])
                    K.emit("vector", lambda e, dsub_bj=dsub_bj, knsub_bi=knsub_bi: e.tensor_tensor(
                        b['scr'][:, :].rearrange("p (n i j) -> p n i j", i=8, j=8), dsub_bj, knsub_bi, ALU.mult),
                        reads=("a_ret", "a_kT%d" % par), writes=("a_scr",))
                    K.emit("vector", lambda e, Ssub=Ssub: e.tensor_tensor(
                        Ssub, Ssub, b['scr'][:, :].rearrange("p (n i j) -> p n i j", i=8, j=8), ALU.add),
                        reads=("a_S", "a_scr"), writes=("a_S",))
                    K.emit("scalar", lambda e, Spc=Spc, sub=sub: e.activation(
                        Spc[:, sub * 4096:(sub + 1) * 4096], Spc[:, sub * 4096:(sub + 1) * 4096], AF.Tanh),
                        reads=("a_S",), writes=("a_S",))
                    qsub_bi = b['qT%d' % par][:, pc * 1024 + sub * 512: pc * 1024 + (sub + 1) * 512].rearrange(
                        "p (n o j) -> p n o j", o=1, j=8).broadcast_to([128, 64, 8, 8])
                    K.emit("vector", lambda e, Ssub=Ssub, qsub_bi=qsub_bi: e.tensor_tensor(
                        b['scr'][:, :].rearrange("p (n i j) -> p n i j", i=8, j=8), Ssub, qsub_bi, ALU.mult),
                        reads=("a_S", "a_qT%d" % par), writes=("a_scr",))
                    K.emit("vector", lambda e, sub=sub: e.tensor_reduce(
                        b['ret'][:, sub * 512:(sub + 1) * 512].rearrange("p (f o) -> p f o", o=1),
                        b['scr'][:, :].rearrange("p (f j) -> p f j", j=8), AX.X, ALU.add),
                        reads=("a_scr",), writes=("a_ret",))
                K.emit("scalar", lambda e: e.activation(b['sil'][:, :], b['ret'][:, :], AF.Silu),
                       reads=("a_ret",), writes=("a_sil",))
                K.emit("vector", lambda e: e.tensor_tensor(b['attn'][:, :], b['ret'][:, :], b['sil'][:, :], ALU.mult),
                       reads=("a_ret", "a_sil"), writes=("a_attn",))
                for oc in range(8):
                    K.emit("tensor", lambda e, oc=oc: e.transpose(
                        ps_t[:, :], b['attn'][:, oc * 128:(oc + 1) * 128], b['identb'][:, :]),
                        reads=("a_attn", "a_identb"), writes=(kt_,))
                    K.emit("vector", lambda e, oc=oc, pc=pc: e.tensor_tensor(
                        b['xnew'][:, oc * TM + pc * 128: oc * TM + (pc + 1) * 128],
                        xt[:, oc * TM + pc * 128: oc * TM + (pc + 1) * 128], ps_t[:, :], ALU.add),
                        reads=("a_xt", kt_), writes=("a_xnew",))
            K.emit("gpsimd", lambda e, t=t, h0=h0: e.dma_start(
                out=xdst[t, :, h0:h0 + 256].rearrange("(c p) l -> p c l", p=128),
                in_=b['xnew'][:, :].rearrange("p (c t2) -> p c t2", t2=TM)),
                reads=("a_xnew",), dma=True)


from concourse.bass_utils import run_bass_kernel_spmd


def build_megakernel():
    K = Prog()
    nc = K.nc
    F = F32
    xin = nc.dram_tensor("xin", [16, 1024, LH], F, kind="ExternalInput").ap()
    xout = nc.dram_tensor("xout", [16, 1024, 512], F, kind="ExternalOutput").ap()
    xA = nc.dram_tensor("xA", [16, 1024, LH], F, kind="Internal").ap()
    xB = nc.dram_tensor("xB", [16, 1024, 512], F, kind="Internal").ap()
    xC = nc.dram_tensor("xC", [16, 1024, 512], F, kind="Internal").ap()
    xD = nc.dram_tensor("xD", [16, 1024, 512], F, kind="Internal").ap()
    xE = nc.dram_tensor("xE", [16, 1024, 512], F, kind="Internal").ap()
    hb = {}
    for nm, shape in [("wT", (1024, 1024)), ("bias", (1024,)), ("sin", (1,)), ("sw", (1,)),
                      ("w_in", (1024, 4384)), ("w_op", (2048, 1024)), ("cwt", (128, 72)),
                      ("convb", (128, 18)), ("dtb", (32, 1)), ("A_log", (32, 1)),
                      ("Dcol", (128, 16)), ("nwcol", (128, 16)), ("hflag", (1, 1)),
                      ("tril", (64, 64)), ("triu", (64, 64)), ("esel", (64, 2)),
                      ("ident", (64, 64)), ("ident128", (128, 128)),
                      ("wc", (1024, 128)), ("bc", (128, 1)), ("tie", (128, 512)), ("w1", (128, 8, 128, 32)), ("w2", (128, 32, 1024)),
                      ("b1", (32, 128)), ("b2", (128, 1024)),
                      ("wkT", (1024, 1024)), ("wvT", (1024, 1024)), ("wqT", (1024, 1024))]:
        hb[nm] = nc.dram_tensor(nm, list(shape), F, kind="ExternalInput").ap()
    scratch = {
        'xsel_h': nc.dram_tensor("xsel_h", [128, 256, 1024], BF16, kind="Internal").ap(),
        'r_h': nc.dram_tensor("r_h", [256, 128, 1024], BF16, kind="Internal").ap(),
        'scat_h': nc.dram_tensor("scat_h", [16, 512, 1024], BF16, kind="Internal").ap(),
    }
    C = emit_const_setup(K)
    with K.scope():
        b0 = alloc_p0(K, "p0_")
        emit_p0_setup(K, C, b0, hb["wT"], hb["bias"], hb["sin"], hb["sw"])
        emit_p0(K, C, b0, (xin, xA), LH)
    with K.scope():
        b1 = alloc_p1(K)
        b1['hbw'] = hb["w_in"]
        emit_p1_setup(K, C, b1, hb)
        emit_p1(K, C, b1, xA, xB)
    with K.scope():
        b2 = alloc_p0(K, "p2_")
        emit_p0_setup(K, C, b2, hb["wT"], hb["bias"], hb["sin"], hb["sw"])
        emit_p0(K, C, b2, (xB, xC), 512)
    with K.scope():
        b3 = alloc_p3(K, "m_")
        emit_p3_setup(K, C, b3, hb)
        emit_p3(K, C, b3, xC, xD, hb, scratch)
    with K.scope():
        b4 = alloc_p4(K)
        emit_p4_setup(K, C, b4, hb)
        emit_p4(K, C, b4, xD, xE)
    with K.scope():
        b5 = alloc_p3(K, "n_")
        emit_p3_setup(K, C, b5, hb)
        emit_p3(K, C, b5, xE, xout, hb, scratch)
    K.finish()
    return K.build()


def kernel(**ins):
    x = np.asarray(ins['L0_attn_x'], np.float32)
    w = {
        "wT": np.ascontiguousarray(np.asarray(ins['L0_ffn_weight_fp8'], np.float32).T),
        "bias": np.asarray(ins['L0_ffn_bias'], np.float32),
        "sin": np.asarray(ins['L0_ffn_input_scale'], np.float32),
        "sw": np.asarray(ins['L0_ffn_weight_scale'], np.float32),
        "w_in": np.asarray(ins['L0_attn_in_proj_w'], np.float32),
        "w_op": np.asarray(ins['L0_attn_out_proj_w'], np.float32),
        "cwt": np.ascontiguousarray(np.asarray(ins['L0_attn_conv_w'], np.float32).reshape(4, 18, 128).transpose(2, 1, 0).reshape(128, 72)),
        "convb": np.ascontiguousarray(np.asarray(ins['L0_attn_conv_b'], np.float32).reshape(18, 128).T),
        "dtb": np.asarray(ins['L0_attn_dt_bias'], np.float32).reshape(32, 1),
        "A_log": np.asarray(ins['L0_attn_A_log'], np.float32).reshape(32, 1),
        "Dcol": np.ascontiguousarray(np.repeat(np.asarray(ins['L0_attn_D'], np.float32), 64).reshape(16, 128).T),
        "nwcol": np.ascontiguousarray(np.asarray(ins['L0_attn_norm_w'], np.float32).reshape(16, 128).T),
        "tril": np.triu(np.ones((64, 64), np.float32)),
        "triu": np.triu(np.ones((64, 64), np.float32)),
        "esel": np.ascontiguousarray(np.stack([np.eye(64, 1, k=-31, dtype=np.float32)[:, 0],
                                               np.eye(64, 1, k=-63, dtype=np.float32)[:, 0]], axis=1)),
        "ident": np.eye(64, dtype=np.float32),
        "ident128": np.eye(128, dtype=np.float32),
        "wc": np.ascontiguousarray(np.asarray(ins['L1_ffn_Wc'], np.float32).reshape(1024, 128)),
        "bc": np.ascontiguousarray(np.asarray(ins['L1_ffn_bc'], np.float32).reshape(128, 1)),
        "tie": np.ascontiguousarray(np.broadcast_to(np.tile(np.linspace(0, 1e-6, 32, dtype=np.float32), 16), (128, 512)).copy()),
        "w1": np.ascontiguousarray(np.asarray(ins['L1_ffn_W1'], np.float32).reshape(128, 8, 128, 32)),
        "w2": np.ascontiguousarray(np.asarray(ins['L1_ffn_W2'], np.float32).reshape(128, 1024, 32).transpose(0, 2, 1)),
        "b1": np.ascontiguousarray(np.asarray(ins['L1_ffn_b1'], np.float32).reshape(128, 32).T),
        "b2": np.ascontiguousarray(np.asarray(ins['L1_ffn_b2'], np.float32).reshape(128, 1024)),
        "wkT": np.ascontiguousarray(np.asarray(ins['L1_attn_W_k'], np.float32).T),
        "wvT": np.ascontiguousarray(np.asarray(ins['L1_attn_W_v'], np.float32).T),
        "wqT": np.ascontiguousarray(np.asarray(ins['L1_attn_W_q'], np.float32).T),
    }
    in_maps = []
    for c in range(8):
        lo = 512 * c - HC
        pad = max(0, -lo)
        sl = x[:, max(lo, 0):512 * (c + 1), :]
        xc = np.ascontiguousarray(np.pad(sl, ((0, 0), (pad, 0), (0, 0))).transpose(0, 2, 1))
        im = dict(w)
        im["xin"] = xc
        im["hflag"] = np.array([[0.0 if c == 0 else 1.0]], np.float32)
        in_maps.append(im)
    nc = build_megakernel()
    res = run_bass_kernel_spmd(nc, in_maps, core_ids=list(range(8)))
    out = np.concatenate([np.asarray(r["xout"]).transpose(0, 2, 1) for r in res.results], axis=1)
    return np.ascontiguousarray(out.astype(np.float32))


# version 10
# speedup vs baseline: 1.0040x; 1.0040x over previous
import sys
sys.path.insert(0, '/opt/trn_rl_repo')
import contextlib
import numpy as np
import concourse.bass as bass
import concourse.mybir as mybir

F32 = mybir.dt.float32
F32R = mybir.dt.float32r
BF16 = mybir.dt.bfloat16
FP8 = mybir.dt.float8e4
AF = mybir.ActivationFunctionType
ALU = mybir.AluOpType
AX = mybir.AxisListType

ENGINES = ("sync", "vector", "scalar", "tensor", "gpsimd")

H = 256
HC = H + 4
NPOS = 512
LH = NPOS + HC
Q = 64
EPS = 1e-5


class Prog:
    def __init__(self, name="mk"):
        self.nc = bass.Bass("TRN2", target_bir_lowering=False)
        self.ops = {e: [] for e in ENGINES}
        self.cnt = {e: 0 for e in ENGINES}
        self.sems = {}
        self.last_writer = {}
        self.readers = {}
        self.waited = {e: {e2: 0 for e2 in ENGINES} for e in ENGINES}
        self.stack = contextlib.ExitStack()
        self.tiles = {}
        self._tile_id = 0

    def sbuf(self, name, shape, dtype):
        t = self.stack.enter_context(self.nc.sbuf_tensor(name, list(shape), dtype))
        self.tiles[name] = t
        return t

    def psum(self, name, shape, dtype=F32):
        t = self.stack.enter_context(self.nc.psum_tensor(name, list(shape), dtype))
        self.tiles[name] = t
        return t

    def _wait(self, eng, dep_eng, dep_cnt):
        if self.waited[eng][dep_eng] >= dep_cnt:
            return
        self.waited[eng][dep_eng] = dep_cnt
        sems = self.sems

        def w(e, dep_eng=dep_eng, dep_cnt=dep_cnt):
            e.wait_ge(sems[dep_eng], dep_cnt)
        self.ops[eng].append(w)

    def emit(self, eng, fn, reads=(), writes=(), dma=False, inc=None):
        for r in reads:
            lw = self.last_writer.get(r)
            if lw is not None:
                dep_eng, dep_cnt = lw[0], lw[1]
                if dep_eng == eng and eng == "tensor":
                    pass
                elif dep_eng == eng and not dma and self.cnt[eng] == dep_cnt and False:
                    pass
                else:
                    if not (dep_eng == eng and self.waited[eng][eng] >= dep_cnt):
                        self._wait(eng, dep_eng, dep_cnt)
        for w_ in writes:
            rd = self.readers.get(w_)
            if rd:
                for dep_eng, dep_cnt in rd.items():
                    if dep_eng == eng and eng == "tensor":
                        continue
                    self._wait(eng, dep_eng, dep_cnt)
            lw = self.last_writer.get(w_)
            if lw is not None and not (lw[0] == eng == "tensor"):
                if dma and len(lw) > 2 and lw[2]:
                    pass
                else:
                    self._wait(eng, lw[0], lw[1])
        n = inc if inc is not None else (16 if dma else 1)
        sems = self.sems
        cnt_after = self.cnt[eng] + n
        my = eng

        def run(e, fn=fn, n=n, my=my):
            ins = fn(e)
            ins.then_inc(sems[my], n)
        self.ops[eng].append(run)
        self.cnt[eng] = cnt_after
        if dma:
            cnt_after = self.cnt[eng]
        for w_ in writes:
            self.last_writer[w_] = (eng, cnt_after, dma)
            self.readers.pop(w_, None)
        for r in reads:
            self.readers.setdefault(r, {})
            cur = self.readers[r].get(eng, 0)
            self.readers[r][eng] = max(cur, cnt_after)
        return cnt_after

    def raw(self, eng, fn):
        self.ops[eng].append(fn)

    def barrier(self):
        snap = dict(self.cnt)
        for e in ENGINES:
            for e2 in ENGINES:
                if e2 != e and snap[e2] > 0:
                    self._wait(e, e2, snap[e2])

    @contextlib.contextmanager
    def scope(self):
        outer = self.stack
        inner = contextlib.ExitStack()
        self.stack = inner
        try:
            yield
        finally:
            self.barrier()
            self.stack = outer
            inner.close()

    def build(self):
        nc = self.nc
        for e in ENGINES:
            self.sems[e] = self.stack.enter_context(nc.semaphore("sem_" + e))
        blk = self.stack.enter_context(nc.Block())
        for e in ENGINES:
            ops = self.ops[e]

            def body(engobj, ops=ops):
                for f in ops:
                    f(engobj)
            getattr(blk, e)(body)
        self.stack.close()
        return nc

    def finish(self):
        for e in ENGINES:
            if e == "gpsimd":
                continue
            c = self.cnt[e]
            if c:
                self._wait("gpsimd", e, c)
        c = self.cnt["gpsimd"]
        if c:
            sems = self.sems

            def w(eng, c=c):
                eng.wait_ge(sems["gpsimd"], c)
            self.ops["gpsimd"].append(w)



def ceil_div(a, b):
    return (a + b - 1) // b


def emit_const_setup(K):
    nc = K.nc
    ones_row = K.sbuf("ones_row", [1, 512], BF16)
    ones_col = K.sbuf("ones_col", [128, 1], F32R)
    ones_col32 = K.sbuf("ones_col32", [128, 1], F32)
    onesb_col = K.sbuf("onesb_col", [128, 1], BF16)
    ones_row128 = K.sbuf("ones_row128", [1, 128], F32)
    K.emit("gpsimd", lambda e: e.memset(ones_row[:, :], 1.0), writes=("ones_row",))
    K.emit("gpsimd", lambda e: e.memset(ones_col32[:, :], 1.0), writes=("ones_col32",))
    K.emit("vector", lambda e: e.tensor_copy(ones_col[:, :], ones_col32[:, :]), reads=("ones_col32",), writes=("ones_col",))
    K.emit("gpsimd", lambda e: e.memset(onesb_col[:, :], 1.0), writes=("onesb_col",))
    K.emit("gpsimd", lambda e: e.memset(ones_row128[:, :], 1.0), writes=("ones_row128",))
    cpool = K.sbuf("cpool", [128, 8], F32)
    for i, v in enumerate((1e-5, -0.5, 1e-6, 0.5)):
        K.emit("gpsimd", lambda e, i=i, v=v: e.memset(cpool[:, i:i + 1], v), writes=("cpool",))
        K.nc.const_aps.register(F32, v, cpool[:, i:i + 1])
    return dict(ones_row=ones_row, ones_col=ones_col, onesb_col=onesb_col, ones_row128=ones_row128,
                cpool=cpool)


def emit_ln(K, C, xt, xsq, stat_ps, rep_ps, mvec, T, extra_scale_ap=None,
            tag=""):
    nc = K.nc
    TM = xt.shape[1] // 8
    kx, ksq, kst = tag + "xt", tag + "xsq", tag + "stat"
    kmv = tag + "mvec"
    krepA, krepB = tag + "repA", tag + "repB"
    mA, mB = mvec[0:1, 0:512], mvec[0:1, 512:1024]
    K.emit("scalar", lambda e: e.activation(xsq[:, 0:8 * TM], xt[:, 0:8 * TM], AF.Square),
           reads=(kx,), writes=(ksq,))
    oc = C["ones_col"]
    for c in range(8):
        K.emit("tensor", lambda e, c=c: e.matmul(
            stat_ps[0:1, 0:T], oc[:, :], xt[:, c * TM:c * TM + T],
            start=(c == 0), stop=(c == 7)), reads=(kx, "ones_col"), writes=(kst,))
    for c in range(8):
        K.emit("tensor", lambda e, c=c: e.matmul(
            stat_ps[0:1, 512:512 + T], oc[:, :], xsq[:, c * TM:c * TM + T],
            start=(c == 0), stop=(c == 7)), reads=(ksq, "ones_col"), writes=(kst,))
    K.emit("vector", lambda e: e.tensor_scalar_mul(mA[:, 0:T], stat_ps[0:1, 0:T], 1.0 / 1024.0),
           reads=(kst,), writes=(kmv + "A",))
    K.emit("vector", lambda e: e.tensor_scalar_mul(mB[:, 0:T], stat_ps[0:1, 512:512 + T], 1.0 / 1024.0),
           reads=(kst,), writes=(kmv + "B",))
    orow = C["ones_row128"]
    K.emit("tensor", lambda e: e.matmul(rep_ps[:, 0:T], orow[0:1, :], mA[:, 0:T], start=True, stop=True),
           reads=(kmv + "A", "ones_row128"), writes=(krepA,))
    K.emit("vector", lambda e: e.tensor_tensor(stat_ps[0:1, 0:T], mA[:, 0:T], mA[:, 0:T], ALU.mult),
           reads=(kmv + "A",), writes=(kst,))
    K.emit("vector", lambda e: e.tensor_tensor(mB[:, 0:T], mB[:, 0:T], stat_ps[0:1, 0:T], ALU.subtract),
           reads=(kmv + "B", kst), writes=(kmv + "B",))
    K.emit("scalar", lambda e: e.activation(mA[:, 0:T], mB[:, 0:T], AF.Ln, bias=EPS),
           reads=(kmv + "B",), writes=(kmv + "A",))
    K.emit("scalar", lambda e: e.activation(mA[:, 0:T], mA[:, 0:T], AF.Exp, scale=-0.5),
           reads=(kmv + "A",), writes=(kmv + "A",))
    if extra_scale_ap is not None:
        K.emit("vector", lambda e: e.tensor_tensor(
            mA[:, 0:T], mA[:, 0:T], extra_scale_ap.broadcast_to([1, T]), ALU.mult),
            reads=(kmv + "A", "recip_s"), writes=(kmv + "A",))
    K.emit("tensor", lambda e: e.matmul(rep_ps[:, TM:TM + T], orow[0:1, :], mA[:, 0:T], start=True, stop=True),
           reads=(kmv + "A", "ones_row128"), writes=(krepB,))
    return krepA


def alloc_p0(K, pfx="p0_"):
    b = {}
    b['xt'] = K.sbuf(pfx + "xt", [128, 4096], F32R)
    b['xw'] = K.sbuf(pfx + "xw", [128, 4096], F32)
    b['xsq'] = K.sbuf(pfx + "xsq", [128, 4096], F32R)
    b['carve32'] = K.sbuf(pfx + "carve", [128, 1024], F32)
    b['xb'] = K.sbuf(pfx + "xb", [128, 4096], BF16)
    b['xnew'] = K.sbuf(pfx + "xnew", [128, 4096], F32)
    b['w0b'] = K.sbuf(pfx + "w0b", [128, 8192], BF16)
    b['biasb'] = K.sbuf(pfx + "biasb", [1, 1024], BF16)
    b['bias32'] = K.sbuf(pfx + "bias32", [1, 1024], F32)
    b['sc'] = K.sbuf(pfx + "sc", [1, 8], F32)
    b['c_ap'] = K.sbuf(pfx + "cap", [128, 1], F32)
    b['mvec'] = K.sbuf(pfx + "mvec", [1, 1024], F32)
    b['stat_ps'] = K.psum(pfx + "stat", [1, 1024], F32)
    b['rep_ps'] = K.psum(pfx + "rep", [128, 1024], F32)
    b['out_ps'] = K.psum(pfx + "out", [128, 1024], F32)
    return b


def emit_p0_setup(K, C, b, wT_h, bias_h, sin_h, sw_h):
    nc = K.nc
    w0b, biasb, bias32, sc, c_ap = b['w0b'], b['biasb'], b['bias32'], b['sc'], b['c_ap']
    for ic in range(8):
        K.emit("gpsimd", lambda e, ic=ic: e.dma_start(
            out=w0b[:, ic * 1024:(ic + 1) * 1024], in_=wT_h[ic * 128:(ic + 1) * 128, :]),
            writes=("w0b",), dma=True)
    K.emit("gpsimd", lambda e: e.dma_start(out=bias32[0:1, :], in_=bias_h[None, :]),
           writes=("bias32",), dma=True)
    K.emit("gpsimd", lambda e: e.dma_start(out=sc[0:1, 0:1], in_=sin_h[None, :]),
           writes=("sc",), dma=True)
    K.emit("gpsimd", lambda e: e.dma_start(out=sc[0:1, 1:2], in_=sw_h[None, :]),
           writes=("sc",), dma=True)
    K.emit("vector", lambda e: e.tensor_tensor(sc[0:1, 2:3], sc[0:1, 0:1], sc[0:1, 1:2], ALU.mult),
           reads=("sc",), writes=("sc2",))
    K.emit("vector", lambda e: e.reciprocal(sc[0:1, 3:4], sc[0:1, 2:3]),
           reads=("sc2",), writes=("sc3",))
    K.emit("vector", lambda e: e.tensor_scalar_mul(sc[0:1, 4:5], sc[0:1, 2:3], 0.5),
           reads=("sc2",), writes=("sc4",))
    K.emit("vector", lambda e: e.reciprocal(sc[0:1, 5:6], sc[0:1, 0:1]),
           reads=("sc",), writes=("recip_s",))
    K.emit("vector", lambda e: e.tensor_tensor(biasb[0:1, :], bias32[0:1, :],
                                               sc[0:1, 3:4].broadcast_to([1, 1024]), ALU.mult),
           reads=("bias32", "sc3"), writes=("biasb",))
    K.emit("tensor", lambda e: e.matmul(b['rep_ps'][:, 0:1], C['ones_row128'][0:1, :], sc[0:1, 4:5],
                                        start=True, stop=True),
           reads=("sc4", "ones_row128"), writes=("p0_repA",))
    K.emit("vector", lambda e: e.tensor_copy(c_ap[:, :], b['rep_ps'][:, 0:1]),
           reads=("p0_repA",), writes=("c_ap",))


def emit_p0_tile(K, C, b, xres, bidx, t0, T, tag):
    nc = K.nc
    xsrc, xdst = xres if isinstance(xres, tuple) else (xres, xres)
    TM = 512
    xt, xw, xsq, xb, xnew = b['xt'], b['xw'], b['xsq'], b['xb'], b['xnew']
    carve = b['carve32'].ap().bitcast(FP8)
    kx = "p0_xt"
    if T == TM:
        K.emit("gpsimd", lambda e, t0=t0: e.dma_start(
            out=xt[:, :].rearrange("p (c t) -> p c t", t=TM),
            in_=xsrc[bidx, :, t0:t0 + TM].rearrange("(c p) l -> p c l", p=128)),
            writes=(kx,), dma=True)
    else:
        for c in range(8):
            K.emit("gpsimd", lambda e, c=c: e.dma_start(
                out=xt[:, c * TM:c * TM + T], in_=xsrc[bidx, c * 128:(c + 1) * 128, t0:t0 + T]),
                writes=(kx,), dma=True)
    krep = emit_ln(K, C, xt, xsq, b['stat_ps'], b['rep_ps'], b['mvec'], T,
                   extra_scale_ap=b['sc'][0:1, 5:6], tag="p0_")
    rep = b['rep_ps']
    for c in range(8):
        K.emit("vector", lambda e, c=c: e.tensor_tensor(xw[:, c * TM:c * TM + T], xt[:, c * TM:c * TM + T],
                                                        rep[:, 0:T], ALU.subtract),
               reads=(kx, "p0_repA"), writes=("p0_xw",))
        K.emit("vector", lambda e, c=c: e.tensor_tensor(xw[:, c * TM:c * TM + T], xw[:, c * TM:c * TM + T],
                                                        rep[:, TM:TM + T], ALU.mult),
               reads=("p0_xw", "p0_repB"), writes=("p0_xw",))
        K.emit("vector", lambda e, c=c: e.tensor_scalar(carve[:, c * TM:c * TM + T], xw[:, c * TM:c * TM + T],
                                                        448.0, -448.0, ALU.min, ALU.max),
               reads=("p0_xw",), writes=("p0_carve",))
        K.emit("scalar", lambda e, c=c: e.activation(xb[:, c * TM:c * TM + T], carve[:, c * TM:c * TM + T], AF.Copy),
               reads=("p0_carve",), writes=("p0_xb",))
    out_ps = b['out_ps']
    for oc in range(8):
        pk = "p0_out%d" % (oc % 2)
        K.emit("tensor", lambda e, oc=oc: e.matmul(
            out_ps[:, (oc % 2) * 512:(oc % 2) * 512 + T],
            b['biasb'][0:1, oc * 128:(oc + 1) * 128], C['ones_row'][0:1, 0:T],
            start=True, stop=False), reads=("biasb", "ones_row"), writes=(pk,))
        for ic in range(8):
            K.emit("tensor", lambda e, oc=oc, ic=ic: e.matmul(
                out_ps[:, (oc % 2) * 512:(oc % 2) * 512 + T],
                b['w0b'][:, ic * 1024 + oc * 128:ic * 1024 + (oc + 1) * 128],
                xb[:, ic * TM:ic * TM + T],
                start=False, stop=(ic == 7)), reads=("w0b", "p0_xb"), writes=(pk,))
        K.emit("vector", lambda e, oc=oc: e.scalar_tensor_tensor(
            xnew[:, oc * TM:oc * TM + T], out_ps[:, (oc % 2) * 512:(oc % 2) * 512 + T],
            b['c_ap'][:, :], xt[:, oc * TM:oc * TM + T], ALU.mult, ALU.add),
            reads=(pk, "c_ap", kx), writes=("p0_xnew%d" % oc,))
        K.emit("sync", lambda e, oc=oc: e.dma_start(
            out=xdst[bidx, oc * 128:(oc + 1) * 128, t0:t0 + T], in_=xnew[:, oc * TM:oc * TM + T]),
            reads=("p0_xnew%d" % oc,), dma=True, writes=())


def emit_p0(K, C, b, xres, L):
    tiles = [(0, 512), (512, L - 512)] if L > 512 else [(0, L)]
    assert all(T % 2 == 0 for _, T in tiles)
    for bidx in range(16):
        for (t0, T) in tiles:
            emit_p0_tile(K, C, b, xres, bidx, t0, T, tag="p0_%d_%d_" % (bidx, t0))



def alloc_p1(K):
    b = {}
    b['xt'] = K.sbuf("q_xt", [128, 4096], F32R)
    b['xsq'] = K.sbuf("q_xsq", [128, 4096], F32R)
    b['dts'] = K.sbuf("q_dts2", [32, 772], F32)

    b['mvec'] = K.sbuf("q_mvec", [1, 1024], F32)
    b['w_in0'] = K.sbuf("q_win0", [128, 2048], BF16)
    b['w_in1'] = b['w_in0']
    b['w_op'] = K.sbuf("q_wop", [128, 2048], BF16)
    b['xbc_pre'] = K.sbuf("q_xbcp", [128, 18 * 772], BF16)
    b['xbc2'] = K.sbuf("q_xbc2", [128, 18 * 768], BF16)
    b['dtraw'] = K.sbuf("q_dtraw", [32, 772], F32)

    b['sdt'] = K.sbuf("q_sdt", [64, 772], F32)
    b['sposatmp'] = K.sbuf("q_sposatmp", [64, 768], F32)
    b['spos'] = b['sdt'][:, 0:384]
    b['atmp'] = b['sdt'][:, 384:768]

    b['s_all'] = K.sbuf("q_sall", [64, 12 * 32], F32)
    b['bmid'] = K.sbuf("q_bmid", [1, 384], F32)
    b['slast'] = K.sbuf("q_slast", [1, 384], F32)
    b['wbase'] = K.sbuf("q_wbase", [1, 384], F32)
    b['g1row'] = K.sbuf("q_g1row", [1, 384], F32)

    b['ubf0'] = K.sbuf("q_ubf0", [64, 96], BF16)
    b['ubf1'] = K.sbuf("q_ubf1", [64, 96], BF16)
    b['uvw0'] = K.sbuf("q_uvw0", [64, 96], F32)
    b['uvw1'] = K.sbuf("q_uvw1", [64, 96], F32)
    b['XT0'] = K.sbuf("q_XT0", [128, 2048], BF16)
    b['BT'] = K.sbuf("q_BT", [128, 768], BF16)
    b['Xu0'] = K.sbuf("q_Xu0", [64, 2048], BF16)
    b['Xu1'] = K.sbuf("q_Xu1", [64, 2048], BF16)
    b['Xw0'] = K.sbuf("q_Xw0", [64, 2048], BF16)
    b['Xw1'] = K.sbuf("q_Xw1", [64, 2048], BF16)
    b['gm0'] = K.sbuf("q_gm0", [64, 64], BF16)
    b['gm1'] = K.sbuf("q_gm1", [64, 64], BF16)
    b['btc0'] = K.sbuf("q_btc0", [64, 128], BF16)
    b['btc1'] = K.sbuf("q_btc1", [64, 128], BF16)
    b['xtc0'] = K.sbuf("q_xtc0", [64, 2048], BF16)
    b['xtc1'] = b['xtc0']
    b['T'] = K.sbuf("q_T", [128, 2048], BF16)
    b['Yf'] = K.sbuf("q_Yf", [128, 8192], BF16)
    b['xn'] = b['Yf']
    b['zg'] = K.sbuf("q_zg", [128, 8192], BF16)
    b['scr1'] = K.sbuf("q_scr1", [128, 512], BF16)
    b['convd'] = K.sbuf("q_convd", [128, 72 * 128], BF16)
    b['convb'] = K.sbuf("q_convb", [128, 18], F32)
    b['cwt'] = K.sbuf("q_cwt", [128, 72], F32)
    b['dtb'] = K.sbuf("q_dtb", [32, 1], F32)
    b['negA'] = K.sbuf("q_negA", [32, 2], F32)
    b['Dcol'] = K.sbuf("q_Dcol", [128, 16], F32)
    b['nwcol'] = K.sbuf("q_nw", [128, 16], F32)
    b['flagc'] = K.sbuf("q_flagc", [128, 1], F32)
    b['tril'] = K.sbuf("q_tril", [64, 64], F32)
    b['triu_m'] = K.sbuf("q_trium", [64, 64], F32)
    b['esel'] = K.sbuf("q_esel", [64, 2], F32)
    b['e31'] = b['esel'][:, 0:1]
    b['e63'] = b['esel'][:, 1:2]
    b['ident'] = K.sbuf("q_ident", [64, 64], F32)
    b['ps_stat'] = K.psum("q_psstat", [1, 1024], F32)
    b['ps_mr'] = K.psum("q_psmr", [128, 1024], F32)
    b['ps_big'] = K.psum("q_psbig", [128, 2048], F32)
    return b


def emit_p1_setup(K, C, b, hb):
    nc = K.nc
    for nm, dst in (("cwt", b['cwt']), ("dtb", b['dtb']), ("Dcol", b['Dcol']),
                    ("nwcol", b['nwcol']), ("tril", b['tril']),
                    ("triu", b['triu_m']), ("esel", b['esel']),
                    ("ident", b['ident'])):
        K.emit("sync", lambda e, nm=nm, dst=dst: e.dma_start(out=dst[:, :], in_=hb[nm]),
               writes=("q_" + nm,), dma=True)
    K.emit("sync", lambda e: e.dma_start(out=b['convb'][:, :], in_=hb["convb"]),
           writes=("q_convb",), dma=True)
    K.emit("sync", lambda e: e.dma_start(out=b['flagc'][0:1, 0:1], in_=hb["hflag"]),
           writes=("q_flagc",), dma=True)
    K.emit("tensor", lambda e: e.matmul(b['ps_mr'][:, 0:1], C['ones_row128'][0:1, :], b['flagc'][0:1, 0:1],
                                        start=True, stop=True), reads=("q_flagc", "ones_row128"), writes=("q_repA",))
    K.emit("vector", lambda e: e.tensor_copy(b['flagc'][:, :], b['ps_mr'][:, 0:1]),
           reads=("q_repA",), writes=("q_flagc",))
    K.emit("gpsimd", lambda e: e.dma_start(out=b['negA'][:, 0:1], in_=hb["A_log"]),
           writes=("q_negA",), dma=True)
    K.emit("scalar", lambda e: e.activation(b['negA'][:, 1:2], b['negA'][:, 0:1], AF.Exp),
           reads=("q_negA",), writes=("q_negA1",))
    K.emit("vector", lambda e: e.tensor_scalar_mul(b['negA'][:, 0:1], b['negA'][:, 1:2], -1.0),
           reads=("q_negA1",), writes=("q_negA0",))
    b['hb_wop'] = hb["w_op"]
    identb = b['convd'][:, 0:128]
    K.emit("gpsimd", lambda e: e.dma_start(out=identb, in_=hb["ident128"]),
           writes=("q_convd",), dma=True)
    pairs = [(k, ch) for k in range(4) for ch in range(18)]
    pairs.remove((0, 0))
    pairs.append((0, 0))
    for k, ch in pairs:
        K.emit("vector", lambda e, k=k, ch=ch: e.tensor_scalar(
            b['convd'][:, (k * 18 + ch) * 128:(k * 18 + ch + 1) * 128],
            b['convd'][:, 0:128], b['cwt'][:, ch * 4 + k:ch * 4 + k + 1], None, ALU.mult),
            reads=("q_convd", "q_cwt"), writes=("q_convd",))


def emit_p1_batch(K, C, b, xA, xB, bidx):
    nc = K.nc
    TM = 512
    xt, xsq, xn = b['xt'], b['xsq'], b['xn']
    ps_mr, ps_stat, ps_big = b['ps_mr'], b['ps_stat'], b['ps_big']
    hbw = b['hbw']
    kbig = "q_big"
    K.emit("gpsimd", lambda e: e.memset(b['T'][:, :], 0.0), writes=("q_T",))

    tiles = [(0, 512), (512, 260)]
    for ti, (t0, T) in enumerate(tiles):
        if T == TM:
            K.emit("gpsimd", lambda e, t0=t0: e.dma_start(
                out=xt[:, :].rearrange("p (c t) -> p c t", t=TM),
                in_=xA[bidx, :, t0:t0 + TM].rearrange("(c p) l -> p c l", p=128)),
                writes=("q_xt",), dma=True)
        else:
            for c in range(8):
                K.emit("gpsimd", lambda e, c=c, t0=t0, T=T: e.dma_start(
                    out=xt[:, c * TM:c * TM + T], in_=xA[bidx, c * 128:(c + 1) * 128, t0:t0 + T]),
                    writes=("q_xt",), dma=True)
        emit_ln(K, C, xt, xsq, ps_stat, ps_mr, b['mvec'], T, tag="q_")
        for c in range(8):
            K.emit("vector", lambda e, c=c, T=T: e.tensor_tensor(
                xsq[:, c * TM:c * TM + T], xt[:, c * TM:c * TM + T], ps_mr[:, 0:T], ALU.subtract),
                reads=("q_xt", "q_repA"), writes=("q_xsq",))
            K.emit("vector", lambda e, c=c, T=T, t0=t0: e.tensor_tensor(
                xn[:, c * 772 + t0: c * 772 + t0 + T], xsq[:, c * TM:c * TM + T], ps_mr[:, TM:TM + T], ALU.mult),
                reads=("q_xsq", "q_repB"), writes=("q_Yf",))
    groups = [(i * 256, 256) for i in range(17)] + [(4352, 32)]
    for gi, (g0, gw) in enumerate(groups):
        wbuf = b['w_in0']
        wkey = "q_win0"
        K.emit("gpsimd", lambda e, g0=g0, gw=gw, wbuf=wbuf: e.dma_start(
            out=wbuf[:, 0:8 * gw].rearrange("p (ic c) -> p ic c", c=gw),
            in_=hbw[:, g0:g0 + gw].rearrange("(ic p) c -> p ic c", p=128)),
            writes=(wkey,), dma=True)
        nj = (gw + 127) // 128
        for j in range(nj):
            fc_abs = (g0 + j * 128) // 128
            M = min(128, gw - j * 128)
            for ti, (t0, T) in enumerate(tiles):
                half = (2 * fc_abs + ti) % 2
                pk = "q_rep%s" % ("A" if half == 0 else "B")
                for ic in range(8):
                    K.emit("tensor", lambda e, ic=ic, j=j, M=M, half=half, gw=gw, T=T, t0=t0, wbuf=wbuf: e.matmul(
                        ps_mr[0:M, half * 512:half * 512 + T],
                        wbuf[:, ic * gw + j * 128: ic * gw + j * 128 + M],
                        xn[:, ic * 772 + t0: ic * 772 + t0 + T],
                        start=(ic == 0), stop=(ic == 7)),
                        reads=(wkey, "q_Yf"), writes=(pk,))
                if fc_abs < 16:
                    cc = fc_abs
                    if ti == 0:
                        K.emit("scalar", lambda e, cc=cc, half=half: e.activation(
                            b['zg'][:, cc * 512: cc * 512 + 252],
                            ps_mr[:, half * 512 + 260: half * 512 + 512], AF.Copy),
                            reads=(pk,), writes=("q_zg",))
                    else:
                        K.emit("scalar", lambda e, cc=cc, half=half: e.activation(
                            b['zg'][:, cc * 512 + 252: cc * 512 + 512],
                            ps_mr[:, half * 512: half * 512 + 260], AF.Copy),
                            reads=(pk,), writes=("q_zg",))
                elif fc_abs < 34:
                    ch = fc_abs - 16
                    if ti == 0:
                        K.emit("scalar", lambda e, ch=ch, half=half: e.activation(
                            b['xbc_pre'][:, ch * 772: ch * 772 + 512],
                            ps_mr[:, half * 512: half * 512 + 512], AF.Copy),
                            reads=(pk,), writes=("q_xbcp",))
                    else:
                        K.emit("scalar", lambda e, ch=ch, half=half: e.activation(
                            b['xbc_pre'][:, ch * 772 + 512: ch * 772 + 772],
                            ps_mr[:, half * 512: half * 512 + 260], AF.Copy),
                            reads=(pk,), writes=("q_xbcp",))
                else:
                    K.emit("vector", lambda e, half=half, t0=t0, T=T: e.tensor_copy(
                        b['dtraw'][0:32, t0:t0 + T], ps_mr[0:32, half * 512: half * 512 + T]),
                        reads=(pk,), writes=("q_dtraw",))
    for ti in range(2):
        if ti == 0:
            o0, OT, rs = 0, 508, 1
        else:
            o0, OT, rs = 508, 260, 509
        for ch in range(18):
            for k in range(4):
                K.emit("tensor", lambda e, ch=ch, k=k, OT=OT, rs=rs: e.matmul(
                    ps_mr[:, 0:OT], b['convd'][:, ((3 - k) * 18 + ch) * 128:((3 - k) * 18 + ch + 1) * 128],
                    b['xbc_pre'][:, ch * 772 + rs + k: ch * 772 + rs + k + OT],
                    start=(k == 0), stop=(k == 3)), reads=("q_convd", "q_xbcp"), writes=("q_repA",))
            K.emit("scalar", lambda e, ch=ch, o0=o0, OT=OT: e.activation(
                b['xbc2'][:, ch * 768 + o0: ch * 768 + o0 + OT], ps_mr[:, 0:OT], AF.Silu,
                bias=b['convb'][:, ch:ch + 1]),
                reads=("q_repA", "q_convb"), writes=("q_xbc2",))
    dts = b['dts']
    dA = dts[0:32, :]
    dR = b['dtraw'][0:32, :]
    K.emit("vector", lambda e: e.tensor_scalar(dA, dR, b['dtb'][:, :], None, ALU.add),
           reads=("q_dtraw", "q_dtb"), writes=("q_xsq",))
    K.emit("scalar", lambda e: e.activation(dR, dA, AF.Abs),
           reads=("q_xsq",), writes=("q_dtraw",))
    K.emit("scalar", lambda e: e.activation(dR, dR, AF.Exp, scale=-1.0),
           reads=("q_dtraw",), writes=("q_dtraw",))
    K.emit("scalar", lambda e: e.activation(dR, dR, AF.Ln, bias=1.0),
           reads=("q_dtraw",), writes=("q_dtraw",))
    K.emit("vector", lambda e: e.scalar_tensor_tensor(dA, dA, 0.0, dR, ALU.max, ALU.add),
           reads=("q_xsq", "q_dtraw"), writes=("q_xsq",))
    K.emit("vector", lambda e: e.tensor_copy(b['sdt'][32:64, 0:772], dts[0:32, :]),
           reads=("q_xsq",), writes=("q_sdt",))
    K.emit("vector", lambda e: e.tensor_scalar(b['sdt'][32:64, 0:260], b['sdt'][32:64, 0:260],
                                               b['flagc'][32:64, :], None, ALU.mult),
           reads=("q_sdt", "q_flagc"), writes=("q_sdt",))
    K.emit("vector", lambda e: e.tensor_scalar(b['sdt'][0:32, 0:772], dts[0:32, :], b['negA'][:, 0:1], None, ALU.mult),
           reads=("q_xsq", "q_negA0"), writes=("q_sdt",))
    for w in range(3):
        for i in range(4):
            c = w * 4 + i
            K.emit("tensor", lambda e, c=c, i=i: e.transpose(
                ps_big[0:64, i * 512: i * 512 + 64],
                b['sdt'][:, 4 + 64 * c: 68 + 64 * c], b['ident'][:, :]),
                reads=("q_sdt", "q_ident"), writes=(kbig,))
        for i in range(4):
            c = w * 4 + i
            K.emit("vector", lambda e, c=c, i=i: e.tensor_copy(
                b['sposatmp'][:, 384 + c * 32: 384 + (c + 1) * 32], ps_big[0:64, i * 512: i * 512 + 32]),
                reads=(kbig,), writes=("q_sdt",))
            K.emit("vector", lambda e, c=c, i=i: e.tensor_copy(
                b['sposatmp'][:, c * 32:(c + 1) * 32], ps_big[0:64, i * 512 + 32: i * 512 + 64]),
                reads=(kbig,), writes=("q_sdt",))
    K.emit("tensor", lambda e: e.matmul(ps_big[0:64, 0:384], b['tril'][:, :],
                                        b['sposatmp'][:, 384:768], start=True, stop=True),
           reads=("q_tril", "q_sdt"), writes=(kbig,))
    K.emit("vector", lambda e: e.tensor_copy(b['s_all'][:, 0:384], ps_big[0:64, 0:384]),
           reads=(kbig,), writes=("q_sall",))
    K.emit("tensor", lambda e: e.matmul(ps_big[0:1, 0:384], b['esel'][:, 0:1],
                                        b['s_all'][:, 0:384], start=True, stop=True),
           reads=("q_esel", "q_sall"), writes=(kbig,))
    K.emit("vector", lambda e: e.tensor_copy(b['bmid'][0:1, 0:384], ps_big[0:1, 0:384]),
           reads=(kbig,), writes=("q_bmid",))
    K.emit("tensor", lambda e: e.matmul(ps_big[0:1, 0:384], b['esel'][:, 1:2],
                                        b['s_all'][:, 0:384], start=True, stop=True),
           reads=("q_esel", "q_sall"), writes=(kbig,))
    K.emit("vector", lambda e: e.tensor_copy(b['slast'][0:1, 0:384], ps_big[0:1, 0:384]),
           reads=(kbig,), writes=("q_slast",))
    K.emit("vector", lambda e: e.tensor_tensor(b['wbase'][0:1, 0:352], b['slast'][0:1, 0:352],
                                               b['bmid'][0:1, 32:384], ALU.add),
           reads=("q_slast", "q_bmid"), writes=("q_wbase",))
    K.emit("vector", lambda e: e.tensor_copy(b['wbase'][0:1, 352:384], b['slast'][0:1, 352:384]),
           reads=("q_slast",), writes=("q_wbase",))
    K.emit("vector", lambda e: e.tensor_tensor(b['g1row'][0:1, 0:384], b['wbase'][0:1, 0:384],
                                               b['bmid'][0:1, 0:384], ALU.subtract),
           reads=("q_wbase", "q_bmid"), writes=("q_g1row",))
    K.emit("scalar", lambda e: e.activation(b['g1row'][0:1, 0:384], b['g1row'][0:1, 0:384], AF.Exp),
           reads=("q_g1row",), writes=("q_g1row",))
    for cp in range(6):
        K.emit("sync", lambda e, cp=cp: e.dma_start_transpose(
            b['BT'][:, cp * 128:(cp + 1) * 128], b['xbc2'][:, 16 * 768 + cp * 128: 16 * 768 + (cp + 1) * 128]),
            reads=("q_xbc2",), writes=("q_BT",), dma=True)
    def prep(c):
        q = c % 2
        cp, off = c // 2, (c % 2) * 64
        uvw, ubf = b['uvw%d' % q], b['ubf%d' % q]
        kuvw, kubf = "q_uvw%d" % q, "q_ubf%d" % q
        kXT, kxtc = "q_XT0", "q_xtc0"
        kXu, kXw = "q_Xu%d" % q, "q_Xw%d" % q
        if c % 2 == 0:
            for cc in range(16):
                K.emit("sync", lambda e, cc=cc, cp=cp: e.dma_start_transpose(
                    b['XT0'][:, cc * 128:(cc + 1) * 128],
                    b['xbc2'][:, cc * 768 + cp * 128: cc * 768 + (cp + 1) * 128]),
                    reads=("q_xbc2",), writes=("q_XT0",), dma=True)
        K.emit("tensor", lambda e, c=c: e.matmul(ps_mr[0:64, 576:608], C['ones_row128'][0:1, 0:64],
                                                 b['bmid'][0:1, c * 32:(c + 1) * 32], start=True, stop=True),
               reads=("ones_row128", "q_bmid"), writes=("q_prep",))
        K.emit("vector", lambda e, c=c, uvw=uvw: e.tensor_tensor(uvw[:, 0:32], ps_mr[0:64, 576:608],
                                                        b['s_all'][:, c * 32:(c + 1) * 32], ALU.subtract),
               reads=("q_prep", "q_sall"), writes=(kuvw,))
        K.emit("vector", lambda e, c=c, uvw=uvw: e.tensor_tensor(uvw[:, 32:64], b['s_all'][:, c * 32:(c + 1) * 32],
                                                        ps_mr[0:64, 576:608], ALU.subtract),
               reads=("q_prep", "q_sall"), writes=(kuvw,))
        K.emit("tensor", lambda e, c=c: e.matmul(ps_mr[0:64, 608:640], C['ones_row128'][0:1, 0:64],
                                                 b['wbase'][0:1, c * 32:(c + 1) * 32], start=True, stop=True),
               reads=("ones_row128", "q_wbase"), writes=("q_prep",))
        K.emit("vector", lambda e, c=c, uvw=uvw: e.tensor_tensor(uvw[:, 64:96], ps_mr[0:64, 608:640],
                                                        b['s_all'][:, c * 32:(c + 1) * 32], ALU.subtract),
               reads=("q_prep", "q_sall"), writes=(kuvw,))
        K.emit("scalar", lambda e, uvw=uvw: e.activation(uvw[:, 0:96], uvw[:, 0:96], AF.Exp),
               reads=(kuvw,), writes=(kuvw,))
        K.emit("vector", lambda e, c=c, uvw=uvw, ubf=ubf: e.tensor_tensor(ubf[:, 0:32], uvw[:, 0:32],
                                                        b['sposatmp'][:, c * 32:(c + 1) * 32], ALU.mult),
               reads=(kuvw, "q_sdt"), writes=(kubf,))
        K.emit("vector", lambda e, uvw=uvw, ubf=ubf: e.tensor_copy(ubf[:, 32:64], uvw[:, 32:64]),
               reads=(kuvw,), writes=(kubf,))
        K.emit("vector", lambda e, c=c, uvw=uvw, ubf=ubf: e.tensor_tensor(ubf[:, 64:96], uvw[:, 64:96],
                                                        b['sposatmp'][:, c * 32:(c + 1) * 32], ALU.mult),
               reads=(kuvw, "q_sdt"), writes=(kubf,))
        ub = ubf[:, 0:32].rearrange("p (h q) -> p h q", q=1).broadcast_to([64, 32, 64])
        wb = ubf[:, 64:96].rearrange("p (h q) -> p h q", q=1).broadcast_to([64, 32, 64])
        xtc = b['xtc0']
        K.emit("vector", lambda e, off=off, xtc=xtc: e.tensor_copy(xtc[:, :], b['XT0'][off:off + 64, :]),
               reads=("q_XT0",), writes=(kxtc,))
        xtv = xtc[:, :].rearrange("p (h q) -> p h q", q=64)
        K.emit("vector", lambda e, q2=q: e.tensor_tensor(b['Xu%d' % q2][:, :].rearrange("p (h q) -> p h q", q=64), xtv, ub, ALU.mult),
               reads=(kxtc, kubf), writes=(kXu,))
        K.emit("vector", lambda e, q2=q: e.tensor_tensor(b['Xw%d' % q2][:, :].rearrange("p (h q) -> p h q", q=64), xtv, wb, ALU.mult),
               reads=(kxtc, kubf), writes=(kXw,))
        K.emit("vector", lambda e, cp=cp, off=off, q2=q: e.tensor_copy(
            b['btc%d' % q2][:, :], b['BT'][off:off + 64, cp * 128:(cp + 1) * 128]),
            reads=("q_BT",), writes=("q_btc%d" % q,))
        if c >= 4:
            K.emit("tensor", lambda e, c=c: e.matmul(ps_mr[0:64, 960:1024],
                                                     b['xbc2'][:, 16 * 768 + c * 64: 16 * 768 + (c + 1) * 64],
                                                     b['xbc2'][:, 17 * 768 + c * 64: 17 * 768 + (c + 1) * 64],
                                                     start=True, stop=True),
                   reads=("q_xbc2",), writes=("q_prep",))
            K.emit("vector", lambda e, q2=q: e.tensor_tensor(b['gm%d' % q2][:, :], ps_mr[0:64, 960:1024], b['triu_m'][:, :], ALU.mult),
                   reads=("q_prep", "q_trium"), writes=("q_gm%d" % q,))

    def main(c):
        q = c % 2
        is_blk = c >= 4
        kXu, kXw = "q_Xu%d" % q, "q_Xw%d" % q
        if is_blk:
            for nb in range(4):
                K.emit("tensor", lambda e, nb=nb, q2=q: e.matmul(
                    ps_big[0:64, nb * 512:(nb + 1) * 512], b['gm%d' % q2][:, :], b['Xu%d' % q2][:, nb * 512:(nb + 1) * 512],
                    start=True, stop=False), reads=("q_gm%d" % q, kXu), writes=(kbig,))
                K.emit("tensor", lambda e, nb=nb, c=c: e.matmul(
                    ps_big[0:64, nb * 512:(nb + 1) * 512],
                    b['xbc2'][:, 17 * 768 + c * 64: 17 * 768 + (c + 1) * 64],
                    b['T'][:, nb * 512:(nb + 1) * 512],
                    start=False, stop=True), reads=("q_xbc2", "q_T"), writes=(kbig,))
            vb = b['ubf%d' % q][:, 32:64].rearrange("p (h q) -> p h q", q=1).broadcast_to([64, 32, 64])
            K.emit("vector", lambda e, q2=q: e.tensor_tensor(b['Xu%d' % q2][:, :].rearrange("p (h q) -> p h q", q=64),
                                                       ps_big[0:64, 0:2048].rearrange("p (h q) -> p h q", q=64),
                                                       vb, ALU.mult),
                   reads=(kbig, "q_ubf%d" % q, kXu), writes=(kXu,))
            for cc in range(16):
                K.emit("sync", lambda e, cc=cc, c=c, q2=q: e.dma_start_transpose(
                    b['Yf'][:, cc * 512 + (c - 4) * 64: cc * 512 + (c - 4) * 64 + 64],
                    b['Xu%d' % q2][:, cc * 128:(cc + 1) * 128]),
                    reads=(kXu,), writes=("q_Yf",), dma=True)
        K.emit("vector", lambda e, q2=q: e.tensor_copy(b['btc%d' % q2][:, :], b['btc%d' % q2][:, :]),
               reads=("q_btc%d" % q,), writes=("q_btc%d" % q,)) if False else None
        for nb in range(4):
            K.emit("tensor", lambda e, nb=nb, q2=q: e.matmul(
                ps_big[0:128, nb * 512:(nb + 1) * 512], b['btc%d' % q2][:, :],
                b['Xw%d' % q2][:, nb * 512:(nb + 1) * 512],
                start=True, stop=True), reads=("q_btc%d" % q, kXw), writes=(kbig,))
        K.emit("tensor", lambda e, c=c: e.matmul(ps_mr[0:128, 0:32], C['ones_row128'][0:1, :],
                                                 b['g1row'][0:1, c * 32:(c + 1) * 32], start=True, stop=True),
               reads=("ones_row128", "q_g1row"), writes=("q_g1",))
        g1v = ps_mr[0:128, 0:32].rearrange("p (h q) -> p h q", q=1).broadcast_to([128, 32, 64])
        K.emit("vector", lambda e: e.tensor_tensor(b['T'][:, :].rearrange("p (h q) -> p h q", q=64),
                                                   b['T'][:, :].rearrange("p (h q) -> p h q", q=64), g1v, ALU.mult),
               reads=("q_T", "q_g1"), writes=("q_T",))
        K.emit("vector", lambda e: e.tensor_tensor(b['T'][:, :], b['T'][:, :], ps_big[0:128, 0:2048], ALU.add),
               reads=("q_T", kbig), writes=("q_T",))

    prep(0)
    for c in range(12):
        if c + 1 < 12:
            prep(c + 1)
        main(c)
    for cc in range(16):
        K.emit("scalar", lambda e, cc=cc: e.activation(b['scr1'][:, :], b['zg'][:, cc * 512:(cc + 1) * 512], AF.Silu),
               reads=("q_zg",), writes=("q_scr1",))
        K.emit("vector", lambda e, cc=cc: e.tensor_tensor(
            b['Yf'][:, cc * 512:(cc + 1) * 512], b['Yf'][:, cc * 512:(cc + 1) * 512], b['scr1'][:, :], ALU.mult),
            reads=("q_Yf", "q_scr1"), writes=("q_Yf",))
        K.emit("vector", lambda e, cc=cc: e.scalar_tensor_tensor(
            b['scr1'][:, :], b['xbc2'][:, cc * 768 + 256: cc * 768 + 768], b['Dcol'][:, cc:cc + 1],
            b['scr1'][:, :], ALU.mult, ALU.mult),
            reads=("q_xbc2", "q_Dcol", "q_scr1"), writes=("q_scr1",))
        K.emit("vector", lambda e, cc=cc: e.tensor_tensor(
            b['Yf'][:, cc * 512:(cc + 1) * 512], b['Yf'][:, cc * 512:(cc + 1) * 512], b['scr1'][:, :], ALU.add),
            reads=("q_Yf", "q_scr1"), writes=("q_Yf",))
        K.emit("scalar", lambda e, cc=cc: e.activation(b['scr1'][:, :], b['Yf'][:, cc * 512:(cc + 1) * 512], AF.Square),
               reads=("q_Yf",), writes=("q_scr1",))
        K.emit("tensor", lambda e, cc=cc: e.matmul(ps_stat[0:1, 0:512], C['onesb_col'][:, :], b['scr1'][:, :],
                                                   start=(cc == 0), stop=(cc == 15)),
               reads=("q_scr1", "onesb_col"), writes=("q_stat",))
    K.emit("vector", lambda e: e.tensor_scalar_mul(b['mvec'][0:1, 0:512], ps_stat[0:1, 0:512], 1.0 / 2048.0),
           reads=("q_stat",), writes=("q_mvecA",))
    K.emit("scalar", lambda e: e.activation(b['mvec'][0:1, 0:512], b['mvec'][0:1, 0:512], AF.Ln, bias=EPS),
           reads=("q_mvecA",), writes=("q_mvecA",))
    K.emit("scalar", lambda e: e.activation(b['mvec'][0:1, 0:512], b['mvec'][0:1, 0:512], AF.Exp, scale=-0.5),
           reads=("q_mvecA",), writes=("q_mvecA",))
    K.emit("tensor", lambda e: e.matmul(ps_mr[:, 0:512], C['ones_row128'][0:1, :], b['mvec'][0:1, 0:512],
                                        start=True, stop=True), reads=("ones_row128", "q_mvecA"), writes=("q_repA",))
    for cc in range(16):
        K.emit("vector", lambda e, cc=cc: e.scalar_tensor_tensor(
            b['zg'][:, cc * 512:(cc + 1) * 512], b['Yf'][:, cc * 512:(cc + 1) * 512],
            b['nwcol'][:, cc:cc + 1], ps_mr[:, 0:512], ALU.mult, ALU.mult),
            reads=("q_Yf", "q_nw", "q_repA"), writes=("q_zg",))
    K.emit("gpsimd", lambda e: e.dma_start(
        out=xt[:, :].rearrange("p (c t) -> p c t", t=TM),
        in_=xA[bidx, :, 260:772].rearrange("(c p) l -> p c l", p=128)),
        writes=("q_xt",), dma=True)
    for oc in range(8):
        K.emit("gpsimd", lambda e, oc=oc: e.dma_start(
            out=b['w_op'][:, :].rearrange("p (kc c) -> p kc c", c=128),
            in_=b['hb_wop'][:, oc * 128:(oc + 1) * 128].rearrange("(kc p) c -> p kc c", p=128)),
            writes=("q_wop",), dma=True)
        for kc in range(16):
            K.emit("tensor", lambda e, oc=oc, kc=kc: e.matmul(
                ps_mr[:, (oc % 2) * 512:(oc % 2) * 512 + 512],
                b['w_op'][:, kc * 128:(kc + 1) * 128],
                b['zg'][:, kc * 512:(kc + 1) * 512],
                start=(kc == 0), stop=(kc == 15)), reads=("q_wop", "q_zg"),
                writes=("q_rep%s" % ("A" if oc % 2 == 0 else "B"),))
        K.emit("vector", lambda e, oc=oc: e.tensor_tensor(
            xsq[:, oc * TM:(oc + 1) * TM], ps_mr[:, (oc % 2) * 512:(oc % 2) * 512 + 512],
            xt[:, oc * TM:(oc + 1) * TM], ALU.add),
            reads=("q_rep%s" % ("A" if oc % 2 == 0 else "B"), "q_xt"), writes=("q_xsq",))
        K.emit("gpsimd", lambda e, oc=oc: e.dma_start(
            out=xB[bidx, oc * 128:(oc + 1) * 128, 0:512], in_=xsq[:, oc * TM:(oc + 1) * TM]),
            reads=("q_xsq",), dma=True)


def emit_p1(K, C, b, xA, xB):
    for bidx in range(16):
        emit_p1_batch(K, C, b, xA, xB, bidx)



def alloc_p3(K, pfx="m_"):
    b = {'pfx': pfx}
    S = lambda n, sh, dt: K.sbuf(pfx + n, sh, dt)
    b['xt'] = S("xt", [128, 4096], F32R)
    b['xsq'] = S("xsq", [128, 4096], F32R)
    b['xn'] = S("xn", [128, 4096], BF16)
    b['xnr'] = S("xnr", [128, 4096], F32R)
    b['mvec'] = S("mvec", [1, 1024], F32)
    b['wc'] = S("wc", [128, 1024], F32R)
    b['mx'] = S("mx", [128, 16], F32)
    b['maskb'] = S("maskb", [128, 8192], BF16)
    b['maskT'] = S("maskT", [64, 2048], BF16)
    b['xnT'] = S("xnT", [64, 8192], BF16)
    b['xselsb'] = S("xselsb", [128, 1024], BF16)
    for p_ in (0, 1):
        b['w1sb%d' % p_] = S("w1sb%d" % p_, [128, 256], BF16)
        b['w2sb%d' % p_] = S("w2sb%d" % p_, [32, 1024], BF16)
        b['b2sb%d' % p_] = S("b2sb%d" % p_, [1, 1024], BF16)
        b['innersb%d' % p_] = S("innersb%d" % p_, [32, 256], BF16)
        b['xsel_es%d' % p_] = S("xsel_es%d" % p_, [128, 2048], BF16)
        b['rsb%d' % p_] = S("rsb%d" % p_, [128, 4096], BF16)
    b['b1sb'] = S("b1sb", [32, 128], F32)
    b['scat64'] = S("scat64", [64, 8192], BF16)
    b['scatT'] = S("scatT", [128, 512], BF16)
    b['ps_stat'] = K.psum(pfx + "psstat", [1, 1024], F32)
    b['ps_mr'] = K.psum(pfx + "psmr", [128, 1024], F32)
    b['ps_a'] = K.psum(pfx + "psa", [128, 1024], F32)
    b['ps_t'] = K.psum(pfx + "pst", [128, 128], BF16)
    b['ps_i'] = K.psum(pfx + "psi", [32, 256], F32)
    b['identb'] = S("identb", [128, 128], BF16)
    b['bcc'] = S("bcc", [128, 1], F32)
    b['ties'] = S("ties", [128, 512], F32)
    b['lgsb'] = S("lgsb", [128, 512], F32)
    return b


def emit_p3_setup(K, C, b, hb):
    pfx = b['pfx']
    for ic in range(8):
        K.emit("gpsimd", lambda e, ic=ic: e.dma_start(
            out=b['wc'][:, ic * 128:(ic + 1) * 128], in_=hb["wc"][ic * 128:(ic + 1) * 128, :]),
            writes=(pfx + "wc",), dma=True)
    K.emit("gpsimd", lambda e: e.dma_start(out=b['identb'][:, :], in_=hb["ident128"]),
           writes=(pfx + "identb",), dma=True)
    K.emit("sync", lambda e: e.dma_start(out=b['bcc'][:, :], in_=hb["bc"]),
           writes=(pfx + "bcc",), dma=True)
    K.emit("sync", lambda e: e.dma_start(out=b['ties'][:, :], in_=hb["tie"]),
           writes=(pfx + "ties",), dma=True)


def emit_p3(K, C, b, xsrc, xdst, hb, scratch):
    pfx = b['pfx']
    TM = 512
    xt, xsq, xn, xnr = b['xt'], b['xsq'], b['xn'], b['xnr']
    ps_mr, ps_stat, ps_a, ps_t, ps_i = b['ps_mr'], b['ps_stat'], b['ps_a'], b['ps_t'], b['ps_i']
    kmrA, kmrB = pfx + "repA", pfx + "repB"
    ka, kt, ki = pfx + "psa", pfx + "pst", pfx + "psi"
    xsel_h, r_h, scat_h = scratch['xsel_h'], scratch['r_h'], scratch['scat_h']

    for bidx in range(16):
        K.emit("gpsimd", lambda e, bidx=bidx: e.dma_start(
            out=xt[:, :].rearrange("p (c t) -> p c t", t=TM),
            in_=xsrc[bidx].rearrange("(c p) l -> p c l", p=128)),
            writes=(pfx + "xt",), dma=True)
        emit_ln(K, C, xt, xsq, ps_stat, ps_mr, b['mvec'], TM, tag=pfx)
        for c in range(8):
            K.emit("vector", lambda e, c=c: e.tensor_tensor(
                xnr[:, c * TM:(c + 1) * TM], xt[:, c * TM:(c + 1) * TM], ps_mr[:, 0:TM], ALU.subtract),
                reads=(pfx + "xt", kmrA), writes=(pfx + "xnr",))
            K.emit("vector", lambda e, c=c: e.tensor_tensor(
                xnr[:, c * TM:(c + 1) * TM], xnr[:, c * TM:(c + 1) * TM], ps_mr[:, TM:2 * TM], ALU.mult),
                reads=(pfx + "xnr", kmrB), writes=(pfx + "xnr",))
            K.emit("scalar", lambda e, c=c: e.activation(
                xn[:, c * TM:(c + 1) * TM], xnr[:, c * TM:(c + 1) * TM], AF.Copy),
                reads=(pfx + "xnr",), writes=(pfx + "xn",))
        for ic in range(8):
            K.emit("tensor", lambda e, ic=ic: e.matmul(
                ps_a[:, 0:512], b['wc'][:, ic * 128:(ic + 1) * 128], xnr[:, ic * TM:(ic + 1) * TM],
                start=(ic == 0), stop=(ic == 7)), reads=(pfx + "wc", pfx + "xnr"), writes=(ka,))
        K.emit("vector", lambda e: e.tensor_scalar(
            b['lgsb'][:, :], ps_a[:, 0:512], b['bcc'][:, :], None, ALU.add),
            reads=(ka, pfx + "bcc"), writes=(pfx + "lgsb",))
        K.emit("vector", lambda e: e.tensor_tensor(
            b['lgsb'][:, :], b['lgsb'][:, :], b['ties'][:, :], ALU.add),
            reads=(pfx + "lgsb", pfx + "ties"), writes=(pfx + "lgsb",))
        K.emit("vector", lambda e: e.tensor_reduce(
            b['mx'][:, :].rearrange("p (g o) -> p g o", o=1),
            b['lgsb'][:, :].rearrange("p (g t) -> p g t", t=32), AX.X, ALU.max),
            reads=(pfx + "lgsb",), writes=(pfx + "mx",))
        mxb = b['mx'][:, :].rearrange("p (g o) -> p g o", o=1).broadcast_to([128, 16, 32])
        K.emit("vector", lambda e, bidx=bidx: e.tensor_tensor(
            b['maskb'][:, bidx * 512:(bidx + 1) * 512].rearrange("p (g t) -> p g t", t=32),
            b['lgsb'][:, :].rearrange("p (g t) -> p g t", t=32), mxb, ALU.is_equal),
            reads=(pfx + "lgsb", pfx + "mx"), writes=(pfx + "maskb",))
        for g in range(16):
            K.emit("tensor", lambda e, g=g, bidx=bidx: e.transpose(
                ps_t[0:32, 0:128], b['maskb'][:, bidx * 512 + g * 32: bidx * 512 + (g + 1) * 32],
                b['identb'][:, :]), reads=(pfx + "maskb", pfx + "identb"), writes=(kt,))
            K.emit("vector", lambda e, g=g: e.tensor_copy(
                b['maskT'][(g % 2) * 32:(g % 2) * 32 + 32, (g // 2) * 128:(g // 2 + 1) * 128],
                ps_t[0:32, 0:128]), reads=(kt,), writes=(pfx + "maskT",))
        for pc in range(4):
            for ic in range(8):
                K.emit("tensor", lambda e, pc=pc, ic=ic: e.transpose(
                    ps_t[:, :], xn[:, ic * TM + pc * 128: ic * TM + (pc + 1) * 128], b['identb'][:, :]),
                    reads=(pfx + "xn", pfx + "identb"), writes=(kt,))
                K.emit("vector", lambda e, pc=pc, ic=ic: e.tensor_copy(
                    b['xnT'][0:64, (2 * pc) * 1024 + ic * 128:(2 * pc) * 1024 + (ic + 1) * 128],
                    ps_t[0:64, :]), reads=(kt,), writes=(pfx + "xnT",))
                K.emit("vector", lambda e, pc=pc, ic=ic: e.tensor_copy(
                    b['xnT'][0:64, (2 * pc + 1) * 1024 + ic * 128:(2 * pc + 1) * 1024 + (ic + 1) * 128],
                    ps_t[64:128, :]), reads=(kt,), writes=(pfx + "xnT",))
        for g in range(16):
            base = (g % 2) * 32
            for nh in range(2):
                K.emit("tensor", lambda e, g=g, nh=nh, base=base: e.matmul(
                    ps_a[:, nh * 512:(nh + 1) * 512],
                    b['maskT'][base:base + 32, (g // 2) * 128:(g // 2 + 1) * 128],
                    b['xnT'][base:base + 32, (g // 2) * 1024 + nh * 512:(g // 2) * 1024 + (nh + 1) * 512],
                    start=True, stop=True), reads=(pfx + "maskT", pfx + "xnT"), writes=(ka,))
            K.emit("vector", lambda e: e.tensor_copy(b['xselsb'][:, :], ps_a[:, 0:1024]),
                   reads=(ka,), writes=(pfx + "xselsb",))
            K.emit("sync", lambda e, g=g, bidx=bidx: e.dma_start(
                out=xsel_h[:, bidx * 16 + g, :], in_=b['xselsb'][:, :]),
                reads=(pfx + "xselsb",), writes=(pfx + "h_xsel",), dma=True)

    for es in range(128):
        p_ = es % 2
        w1sb, w2sb, b2sb = b['w1sb%d' % p_], b['w2sb%d' % p_], b['b2sb%d' % p_]
        innersb, xsel_es = b['innersb%d' % p_], b['xsel_es%d' % p_]
        kw1, kw2, kb2 = pfx + "w1sb%d" % p_, pfx + "w2sb%d" % p_, pfx + "b2sb%d" % p_
        kin, kxe = pfx + "innersb%d" % p_, pfx + "xsel_es%d" % p_
        kap = ka + str(p_)
        psa = ps_a[:, (p_ % 2) * 0:1024] if False else ps_a
        K.emit("gpsimd", lambda e, es=es, w1sb=w1sb: e.dma_start(
            out=w1sb[:, :].rearrange("p (dc f) -> p dc f", f=32),
            in_=hb["w1"][es].rearrange("dc p f -> p dc f")),
            writes=(kw1,), dma=True)
        K.emit("gpsimd", lambda e, es=es, w2sb=w2sb: e.dma_start(out=w2sb[:, :], in_=hb["w2"][es]),
               writes=(kw2,), dma=True)
        K.emit("gpsimd", lambda e, es=es, b2sb=b2sb: e.dma_start(out=b2sb[0:1, :], in_=hb["b2"][es:es + 1, :]),
               writes=(kb2,), dma=True)
        if es == 0:
            K.emit("sync", lambda e: e.dma_start(out=b['b1sb'][:, :], in_=hb["b1"]),
                   writes=(pfx + "b1sb",), dma=True)
        for dc in range(8):
            K.emit("sync", lambda e, es=es, dc=dc, xsel_es=xsel_es: e.dma_start_transpose(
                xsel_es[:, dc * 256:(dc + 1) * 256], xsel_h[es, :, dc * 128:(dc + 1) * 128]),
                reads=(pfx + "h_xsel",), writes=(kxe,), dma=True)
        for dc in range(8):
            K.emit("tensor", lambda e, dc=dc, w1sb=w1sb, xsel_es=xsel_es: e.matmul(
                ps_i[0:32, (p_ % 2) * 0:256] if False else ps_i[:, :], w1sb[:, dc * 32:(dc + 1) * 32], xsel_es[:, dc * 256:(dc + 1) * 256],
                start=(dc == 0), stop=(dc == 7)), reads=(kw1, kxe), writes=(ki,))
        K.emit("scalar", lambda e, es=es, innersb=innersb: e.activation(
            innersb[:, :], ps_i[:, :], AF.Relu, bias=b['b1sb'][:, es:es + 1]),
            reads=(ki, pfx + "b1sb"), writes=(kin,))
        for gc in range(2):
            for nh in range(2):
                K.emit("tensor", lambda e, nh=nh, b2sb=b2sb: e.matmul(
                    ps_a[:, nh * 512:(nh + 1) * 512], C['ones_row'][0:1, 0:128],
                    b2sb[0:1, nh * 512:(nh + 1) * 512],
                    start=True, stop=False), reads=(kb2, "ones_row"), writes=(ka,))
                K.emit("tensor", lambda e, gc=gc, nh=nh, innersb=innersb, w2sb=w2sb: e.matmul(
                    ps_a[:, nh * 512:(nh + 1) * 512],
                    innersb[:, gc * 128:(gc + 1) * 128],
                    w2sb[:, nh * 512:(nh + 1) * 512],
                    start=False, stop=True), reads=(kin, kw2), writes=(ka,))
            K.emit("vector", lambda e: e.tensor_copy(b['xselsb'][:, :], ps_a[:, 0:1024]),
                   reads=(ka,), writes=(pfx + "xselsb",))
            K.emit("sync", lambda e, es=es, gc=gc: e.dma_start(
                out=r_h[gc * 128:(gc + 1) * 128, es, :], in_=b['xselsb'][:, :]),
                reads=(pfx + "xselsb",), writes=(pfx + "h_r",), dma=True)

    for bidx in range(16):
        for gq in range(4):
            rsb = b['rsb%d' % (gq % 2)]
            krsb = pfx + "rsb%d" % (gq % 2)
            K.emit("gpsimd", lambda e, bidx=bidx, gq=gq, rsb=rsb: e.dma_start(
                out=rsb[:, :].rearrange("p (g d) -> p g d", d=1024),
                in_=r_h[bidx * 16 + gq * 4: bidx * 16 + gq * 4 + 4, :, :].rearrange("g p d -> p g d")),
                reads=(pfx + "h_r",), writes=(krsb,), dma=True)
            for gi in range(4):
                g = gq * 4 + gi
                for nh in range(2):
                    K.emit("tensor", lambda e, g=g, gi=gi, nh=nh, bidx=bidx, rsb=rsb: e.matmul(
                        ps_a[0:32, nh * 512:(nh + 1) * 512],
                        b['maskb'][:, bidx * 512 + g * 32: bidx * 512 + (g + 1) * 32],
                        rsb[:, gi * 1024 + nh * 512: gi * 1024 + (nh + 1) * 512],
                        start=True, stop=True), reads=(pfx + "maskb", krsb), writes=(ka,))
                K.emit("vector", lambda e, g=g: e.tensor_copy(
                    b['scat64'][(g % 2) * 32:(g % 2) * 32 + 32, (g // 2) * 1024:(g // 2 + 1) * 1024],
                    ps_a[0:32, 0:1024]), reads=(ka,), writes=(pfx + "scat64",))
        K.emit("sync", lambda e, bidx=bidx: e.dma_start(
            out=scat_h[bidx].rearrange("(gh gm t) d -> (gm t) gh d", gm=2, t=32),
            in_=b['scat64'][:, :].rearrange("p (gh d) -> p gh d", d=1024)),
            reads=(pfx + "scat64",), writes=(pfx + "h_scat",), dma=True)
        K.emit("gpsimd", lambda e, bidx=bidx: e.dma_start(
            out=xt[:, :].rearrange("p (c t) -> p c t", t=TM),
            in_=xsrc[bidx].rearrange("(c p) l -> p c l", p=128)),
            writes=(pfx + "xt",), dma=True)
        for dc in range(8):
            K.emit("sync", lambda e, dc=dc, bidx=bidx: e.dma_start_transpose(
                b['scatT'][:, :], scat_h[bidx, :, dc * 128:(dc + 1) * 128]),
                reads=(pfx + "h_scat",), writes=(pfx + "scatT",), dma=True)
            K.emit("vector", lambda e, dc=dc: e.scalar_tensor_tensor(
                xsq[:, dc * TM:(dc + 1) * TM], b['scatT'][:, :], 0.5, xt[:, dc * TM:(dc + 1) * TM],
                ALU.mult, ALU.add), reads=(pfx + "scatT", pfx + "xt"), writes=(pfx + "xsq",))
            K.emit("gpsimd", lambda e, dc=dc, bidx=bidx: e.dma_start(
                out=xdst[bidx, dc * 128:(dc + 1) * 128, :], in_=xsq[:, dc * TM:(dc + 1) * TM]),
                reads=(pfx + "xsq",), dma=True)



def alloc_p4(K):
    b = {}
    S = lambda n, sh, dt: K.sbuf("a_" + n, sh, dt)
    b['xt0'] = S("xt0", [128, 2048], F32R)
    b['xt1'] = b['xt0']
    b['xsq'] = S("xsq", [128, 2048], F32R)
    b['xn0'] = S("xn0", [128, 2048], BF16)
    b['xn1'] = S("xn1", [128, 2048], BF16)
    b['xnew'] = S("xnew", [128, 2048], F32)
    b['mvec'] = S("mvec", [1, 1024], F32)
    b['wk'] = S("wk", [128, 8192], BF16)
    b['wv'] = S("wv", [128, 8192], BF16)
    b['wq'] = S("wq", [128, 8192], BF16)
    b['S'] = S("S", [128, 16384], F32)
    b['scr'] = S("scr", [128, 4096], F32)
    b['kT0'] = S("kT0", [128, 2048], BF16)
    b['vT0'] = S("vT0", [128, 2048], BF16)
    b['qT0'] = S("qT0", [128, 2048], BF16)
    b['kT1'] = S("kT1", [128, 2048], BF16)
    b['vT1'] = S("vT1", [128, 2048], BF16)
    b['qT1'] = S("qT1", [128, 2048], BF16)
    b['ret'] = S("ret", [128, 1024], F32)
    b['sil'] = S("sil", [128, 1024], F32)
    b['attn'] = S("attn", [128, 1024], BF16)
    b['ssum'] = S("ssum", [128, 128], F32)
    b['rr'] = S("rr", [128, 128], F32)
    b['ps_stat'] = K.psum("a_psstat", [1, 1024], F32)
    b['ps_mr'] = K.psum("a_psmr", [128, 1024], F32)
    b['ps_t'] = K.psum("a_pst", [128, 128], BF16)
    b['identb'] = S("identb", [128, 128], BF16)
    return b


def emit_p4_setup(K, C, b, hb):
    for nm, dst in (("wkT", b['wk']), ("wvT", b['wv']), ("wqT", b['wq'])):
        for ic in range(8):
            K.emit("gpsimd", lambda e, nm=nm, dst=dst, ic=ic: e.dma_start(
                out=dst[:, ic * 1024:(ic + 1) * 1024], in_=hb[nm][ic * 128:(ic + 1) * 128, :]),
                writes=("a_" + nm,), dma=True)
    K.emit("gpsimd", lambda e: e.dma_start(out=b['identb'][:, :], in_=hb["ident128"]),
           writes=("a_identb",), dma=True)


def emit_p4(K, C, b, xsrc, xdst):
    TM = 256
    xsq = b['xsq']
    ps_mr, ps_stat, ps_t = b['ps_mr'], b['ps_stat'], b['ps_t']
    kmrA, kmrB, kt_ = "a_repA", "a_repB", "a_pst"
    for half in range(2):
        h0 = half * 256
        K.emit("gpsimd", lambda e: e.memset(b['S'][:, :], 0.0), writes=("a_S",))
        for t in range(16):
            par = t % 2
            xt, xn = b['xt0'], b['xn%d' % par]
            K.emit("gpsimd", lambda e, t=t, h0=h0: e.dma_start(
                out=xt[:, :].rearrange("p (c t2) -> p c t2", t2=TM),
                in_=xsrc[t, :, h0:h0 + 256].rearrange("(c p) l -> p c l", p=128)),
                writes=("a_xt",), dma=True)
            emit_ln(K, C, xt, xsq, ps_stat, ps_mr, b['mvec'], TM, tag="a_")
            for c in range(8):
                K.emit("vector", lambda e, c=c: e.tensor_tensor(
                    xsq[:, c * TM:(c + 1) * TM], xt[:, c * TM:(c + 1) * TM], ps_mr[:, 0:TM], ALU.subtract),
                    reads=("a_xt", kmrA), writes=("a_xsq",))
                K.emit("vector", lambda e, c=c: e.tensor_tensor(
                    xn[:, c * TM:(c + 1) * TM], xsq[:, c * TM:(c + 1) * TM], ps_mr[:, TM:2 * TM], ALU.mult),
                    reads=("a_xsq", kmrB), writes=("a_xn%d" % par,))
            for wnm, wt, dst, dk in (("wk", b['wk'], b['kT%d' % par], "a_kT%d" % par),
                                     ("wv", b['wv'], b['vT%d' % par], "a_vT%d" % par),
                                     ("wq", b['wq'], b['qT%d' % par], "a_qT%d" % par)):
                for pc in range(2):
                    for ic in range(8):
                        for nh in range(2):
                            K.emit("tensor", lambda e, pc=pc, ic=ic, nh=nh, wt=wt: e.matmul(
                                ps_mr[:, nh * 512:(nh + 1) * 512],
                                xn[:, ic * TM + pc * 128: ic * TM + (pc + 1) * 128],
                                wt[:, ic * 1024 + nh * 512: ic * 1024 + (nh + 1) * 512],
                                start=(ic == 0), stop=(ic == 7)),
                                reads=("a_xn%d" % par, "a_" + wnm + "T"), writes=(kmrA, kmrB))
                    K.emit("vector", lambda e, pc=pc, dst=dst: e.tensor_copy(
                        dst[:, pc * 1024:(pc + 1) * 1024], ps_mr[:, 0:1024]),
                        reads=(kmrA, kmrB), writes=(dk,))
            for pc in range(2):
                K.emit("scalar", lambda e, pc=pc, par=par: e.activation(
                    b['scr'][:, 0:1024], b['kT%d' % par][:, pc * 1024:(pc + 1) * 1024], AF.Square),
                    reads=("a_kT%d" % par,), writes=("a_scr",))
                K.emit("vector", lambda e: e.tensor_reduce(
                    b['ssum'][:, :].rearrange("p (n o) -> p n o", o=1),
                    b['scr'][:, 0:1024].rearrange("p (n j) -> p n j", j=8), AX.X, ALU.add),
                    reads=("a_scr",), writes=("a_ssum",))
                K.emit("scalar", lambda e: e.activation(b['rr'][:, :], b['ssum'][:, :], AF.Ln),
                       reads=("a_ssum",), writes=("a_rr",))
                K.emit("scalar", lambda e: e.activation(b['rr'][:, :], b['rr'][:, :], AF.Exp, scale=0.5),
                       reads=("a_rr",), writes=("a_rr",))
                K.emit("vector", lambda e: e.tensor_scalar_add(b['rr'][:, :], b['rr'][:, :], 1e-6),
                       reads=("a_rr",), writes=("a_rr",))
                K.emit("vector", lambda e: e.reciprocal(b['rr'][:, :], b['rr'][:, :]),
                       reads=("a_rr",), writes=("a_rr",))
                rrb = b['rr'][:, :].rearrange("p (n o) -> p n o", o=1).broadcast_to([128, 128, 8])
                K.emit("vector", lambda e, pc=pc, par=par: e.tensor_tensor(
                    b['kT%d' % par][:, pc * 1024:(pc + 1) * 1024].rearrange("p (n j) -> p n j", j=8),
                    b['kT%d' % par][:, pc * 1024:(pc + 1) * 1024].rearrange("p (n j) -> p n j", j=8),
                    rrb, ALU.mult), reads=("a_kT%d" % par, "a_rr"), writes=("a_kT%d" % par,))
            for pc in range(2):
                Spc = b['S'][:, pc * 8192:(pc + 1) * 8192]
                knb = b['kT%d' % par][:, pc * 1024:(pc + 1) * 1024]
                for sub in range(2):
                    Ssub = Spc[:, sub * 4096:(sub + 1) * 4096].rearrange("p (n i j) -> p n i j", i=8, j=8)
                    knsub = knb[:, sub * 512:(sub + 1) * 512].rearrange("p (n j) -> p n j", j=8)
                    knsub_bi = knb[:, sub * 512:(sub + 1) * 512].rearrange("p (n o j) -> p n o j", o=1, j=8).broadcast_to([128, 64, 8, 8])
                    K.emit("vector", lambda e, Ssub=Ssub, knsub_bi=knsub_bi: e.tensor_tensor(
                        b['scr'][:, :].rearrange("p (n i j) -> p n i j", i=8, j=8), Ssub, knsub_bi, ALU.mult),
                        reads=("a_S", "a_kT%d" % par), writes=("a_scr",))
                    K.emit("vector", lambda e, sub=sub: e.tensor_reduce(
                        b['ret'][:, sub * 512:(sub + 1) * 512].rearrange("p (f o) -> p f o", o=1),
                        b['scr'][:, :].rearrange("p (f j) -> p f j", j=8), AX.X, ALU.add),
                        reads=("a_scr",), writes=("a_ret",))
                K.emit("vector", lambda e, pc=pc, par=par: e.tensor_tensor(
                    b['ret'][:, :], b['vT%d' % par][:, pc * 1024:(pc + 1) * 1024], b['ret'][:, :], ALU.subtract),
                    reads=("a_vT%d" % par, "a_ret"), writes=("a_ret",))
                for sub in range(2):
                    Ssub = Spc[:, sub * 4096:(sub + 1) * 4096].rearrange("p (n i j) -> p n i j", i=8, j=8)
                    knsub_bi = knb[:, sub * 512:(sub + 1) * 512].rearrange("p (n o j) -> p n o j", o=1, j=8).broadcast_to([128, 64, 8, 8])
                    dsub_bj = b['ret'][:, sub * 512:(sub + 1) * 512].rearrange("p (n i o) -> p n i o", o=1, i=8).broadcast_to([128, 64, 8, 8])
                    K.emit("vector", lambda e, dsub_bj=dsub_bj, knsub_bi=knsub_bi: e.tensor_tensor(
                        b['scr'][:, :].rearrange("p (n i j) -> p n i j", i=8, j=8), dsub_bj, knsub_bi, ALU.mult),
                        reads=("a_ret", "a_kT%d" % par), writes=("a_scr",))
                    K.emit("vector", lambda e, Ssub=Ssub: e.tensor_tensor(
                        Ssub, Ssub, b['scr'][:, :].rearrange("p (n i j) -> p n i j", i=8, j=8), ALU.add),
                        reads=("a_S", "a_scr"), writes=("a_S",))
                    K.emit("scalar", lambda e, Spc=Spc, sub=sub: e.activation(
                        Spc[:, sub * 4096:(sub + 1) * 4096], Spc[:, sub * 4096:(sub + 1) * 4096], AF.Tanh),
                        reads=("a_S",), writes=("a_S",))
                    qsub_bi = b['qT%d' % par][:, pc * 1024 + sub * 512: pc * 1024 + (sub + 1) * 512].rearrange(
                        "p (n o j) -> p n o j", o=1, j=8).broadcast_to([128, 64, 8, 8])
                    K.emit("vector", lambda e, Ssub=Ssub, qsub_bi=qsub_bi: e.tensor_tensor(
                        b['scr'][:, :].rearrange("p (n i j) -> p n i j", i=8, j=8), Ssub, qsub_bi, ALU.mult),
                        reads=("a_S", "a_qT%d" % par), writes=("a_scr",))
                    K.emit("vector", lambda e, sub=sub: e.tensor_reduce(
                        b['ret'][:, sub * 512:(sub + 1) * 512].rearrange("p (f o) -> p f o", o=1),
                        b['scr'][:, :].rearrange("p (f j) -> p f j", j=8), AX.X, ALU.add),
                        reads=("a_scr",), writes=("a_ret",))
                K.emit("scalar", lambda e: e.activation(b['sil'][:, :], b['ret'][:, :], AF.Silu),
                       reads=("a_ret",), writes=("a_sil",))
                K.emit("vector", lambda e: e.tensor_tensor(b['attn'][:, :], b['ret'][:, :], b['sil'][:, :], ALU.mult),
                       reads=("a_ret", "a_sil"), writes=("a_attn",))
                for oc in range(8):
                    K.emit("tensor", lambda e, oc=oc: e.transpose(
                        ps_t[:, :], b['attn'][:, oc * 128:(oc + 1) * 128], b['identb'][:, :]),
                        reads=("a_attn", "a_identb"), writes=(kt_,))
                    K.emit("vector", lambda e, oc=oc, pc=pc: e.tensor_tensor(
                        b['xnew'][:, oc * TM + pc * 128: oc * TM + (pc + 1) * 128],
                        xt[:, oc * TM + pc * 128: oc * TM + (pc + 1) * 128], ps_t[:, :], ALU.add),
                        reads=("a_xt", kt_), writes=("a_xnew",))
            K.emit("gpsimd", lambda e, t=t, h0=h0: e.dma_start(
                out=xdst[t, :, h0:h0 + 256].rearrange("(c p) l -> p c l", p=128),
                in_=b['xnew'][:, :].rearrange("p (c t2) -> p c t2", t2=TM)),
                reads=("a_xnew",), dma=True)


from concourse.bass_utils import run_bass_kernel_spmd


def build_megakernel():
    K = Prog()
    nc = K.nc
    F = F32
    xin = nc.dram_tensor("xin", [16, 1024, LH], F, kind="ExternalInput").ap()
    xout = nc.dram_tensor("xout", [16, 1024, 512], F, kind="ExternalOutput").ap()
    xA = nc.dram_tensor("xA", [16, 1024, LH], F, kind="Internal").ap()
    xB = nc.dram_tensor("xB", [16, 1024, 512], F, kind="Internal").ap()
    xC = nc.dram_tensor("xC", [16, 1024, 512], F, kind="Internal").ap()
    xD = nc.dram_tensor("xD", [16, 1024, 512], F, kind="Internal").ap()
    xE = nc.dram_tensor("xE", [16, 1024, 512], F, kind="Internal").ap()
    hb = {}
    for nm, shape in [("wT", (1024, 1024)), ("bias", (1024,)), ("sin", (1,)), ("sw", (1,)),
                      ("w_in", (1024, 4384)), ("w_op", (2048, 1024)), ("cwt", (128, 72)),
                      ("convb", (128, 18)), ("dtb", (32, 1)), ("A_log", (32, 1)),
                      ("Dcol", (128, 16)), ("nwcol", (128, 16)), ("hflag", (1, 1)),
                      ("tril", (64, 64)), ("triu", (64, 64)), ("esel", (64, 2)),
                      ("ident", (64, 64)), ("ident128", (128, 128)),
                      ("wc", (1024, 128)), ("bc", (128, 1)), ("tie", (128, 512)), ("w1", (128, 8, 128, 32)), ("w2", (128, 32, 1024)),
                      ("b1", (32, 128)), ("b2", (128, 1024)),
                      ("wkT", (1024, 1024)), ("wvT", (1024, 1024)), ("wqT", (1024, 1024))]:
        hb[nm] = nc.dram_tensor(nm, list(shape), F, kind="ExternalInput").ap()
    scratch = {
        'xsel_h': nc.dram_tensor("xsel_h", [128, 256, 1024], BF16, kind="Internal").ap(),
        'r_h': nc.dram_tensor("r_h", [256, 128, 1024], BF16, kind="Internal").ap(),
        'scat_h': nc.dram_tensor("scat_h", [16, 512, 1024], BF16, kind="Internal").ap(),
    }
    C = emit_const_setup(K)
    with K.scope():
        b0 = alloc_p0(K, "p0_")
        emit_p0_setup(K, C, b0, hb["wT"], hb["bias"], hb["sin"], hb["sw"])
        emit_p0(K, C, b0, (xin, xA), LH)
    with K.scope():
        b1 = alloc_p1(K)
        b1['hbw'] = hb["w_in"]
        emit_p1_setup(K, C, b1, hb)
        emit_p1(K, C, b1, xA, xB)
    with K.scope():
        b2 = alloc_p0(K, "p2_")
        emit_p0_setup(K, C, b2, hb["wT"], hb["bias"], hb["sin"], hb["sw"])
        emit_p0(K, C, b2, (xB, xC), 512)
    with K.scope():
        b3 = alloc_p3(K, "m_")
        emit_p3_setup(K, C, b3, hb)
        emit_p3(K, C, b3, xC, xD, hb, scratch)
    with K.scope():
        b4 = alloc_p4(K)
        emit_p4_setup(K, C, b4, hb)
        emit_p4(K, C, b4, xD, xE)
    with K.scope():
        b5 = alloc_p3(K, "n_")
        emit_p3_setup(K, C, b5, hb)
        emit_p3(K, C, b5, xE, xout, hb, scratch)
    K.finish()
    return K.build()


def kernel(**ins):
    x = np.asarray(ins['L0_attn_x'], np.float32)
    w = {
        "wT": np.ascontiguousarray(np.asarray(ins['L0_ffn_weight_fp8'], np.float32).T),
        "bias": np.asarray(ins['L0_ffn_bias'], np.float32),
        "sin": np.asarray(ins['L0_ffn_input_scale'], np.float32),
        "sw": np.asarray(ins['L0_ffn_weight_scale'], np.float32),
        "w_in": np.asarray(ins['L0_attn_in_proj_w'], np.float32),
        "w_op": np.asarray(ins['L0_attn_out_proj_w'], np.float32),
        "cwt": np.ascontiguousarray(np.asarray(ins['L0_attn_conv_w'], np.float32).reshape(4, 18, 128).transpose(2, 1, 0).reshape(128, 72)),
        "convb": np.ascontiguousarray(np.asarray(ins['L0_attn_conv_b'], np.float32).reshape(18, 128).T),
        "dtb": np.asarray(ins['L0_attn_dt_bias'], np.float32).reshape(32, 1),
        "A_log": np.asarray(ins['L0_attn_A_log'], np.float32).reshape(32, 1),
        "Dcol": np.ascontiguousarray(np.repeat(np.asarray(ins['L0_attn_D'], np.float32), 64).reshape(16, 128).T),
        "nwcol": np.ascontiguousarray(np.asarray(ins['L0_attn_norm_w'], np.float32).reshape(16, 128).T),
        "tril": np.triu(np.ones((64, 64), np.float32)),
        "triu": np.triu(np.ones((64, 64), np.float32)),
        "esel": np.ascontiguousarray(np.stack([np.eye(64, 1, k=-31, dtype=np.float32)[:, 0],
                                               np.eye(64, 1, k=-63, dtype=np.float32)[:, 0]], axis=1)),
        "ident": np.eye(64, dtype=np.float32),
        "ident128": np.eye(128, dtype=np.float32),
        "wc": np.ascontiguousarray(np.asarray(ins['L1_ffn_Wc'], np.float32).reshape(1024, 128)),
        "bc": np.ascontiguousarray(np.asarray(ins['L1_ffn_bc'], np.float32).reshape(128, 1)),
        "tie": np.ascontiguousarray(np.broadcast_to(np.tile(np.linspace(0, 1e-6, 32, dtype=np.float32), 16), (128, 512)).copy()),
        "w1": np.ascontiguousarray(np.asarray(ins['L1_ffn_W1'], np.float32).reshape(128, 8, 128, 32)),
        "w2": np.ascontiguousarray(np.asarray(ins['L1_ffn_W2'], np.float32).reshape(128, 1024, 32).transpose(0, 2, 1)),
        "b1": np.ascontiguousarray(np.asarray(ins['L1_ffn_b1'], np.float32).reshape(128, 32).T),
        "b2": np.ascontiguousarray(np.asarray(ins['L1_ffn_b2'], np.float32).reshape(128, 1024)),
        "wkT": np.ascontiguousarray(np.asarray(ins['L1_attn_W_k'], np.float32).T),
        "wvT": np.ascontiguousarray(np.asarray(ins['L1_attn_W_v'], np.float32).T),
        "wqT": np.ascontiguousarray(np.asarray(ins['L1_attn_W_q'], np.float32).T),
    }
    in_maps = []
    for c in range(8):
        lo = 512 * c - HC
        pad = max(0, -lo)
        sl = x[:, max(lo, 0):512 * (c + 1), :]
        xc = np.ascontiguousarray(np.pad(sl, ((0, 0), (pad, 0), (0, 0))).transpose(0, 2, 1))
        im = dict(w)
        im["xin"] = xc
        im["hflag"] = np.array([[0.0 if c == 0 else 1.0]], np.float32)
        in_maps.append(im)
    nc = build_megakernel()
    res = run_bass_kernel_spmd(nc, in_maps, core_ids=list(range(8)))
    out = np.concatenate([np.asarray(r["xout"]).transpose(0, 2, 1) for r in res.results], axis=1)
    return np.ascontiguousarray(out.astype(np.float32))


# version 11
# speedup vs baseline: 1.0597x; 1.0555x over previous
import sys
sys.path.insert(0, '/opt/trn_rl_repo')
import contextlib
import numpy as np
import concourse.bass as bass
import concourse.mybir as mybir

F32 = mybir.dt.float32
F32R = mybir.dt.float32r
BF16 = mybir.dt.bfloat16
FP8 = mybir.dt.float8e4
AF = mybir.ActivationFunctionType
ALU = mybir.AluOpType
AX = mybir.AxisListType

ENGINES = ("sync", "vector", "scalar", "tensor", "gpsimd")

H = 256
HC = H + 4
NPOS = 512
LH = NPOS + HC
Q = 64
EPS = 1e-5


class Prog:
    def __init__(self, name="mk"):
        self.nc = bass.Bass("TRN2", target_bir_lowering=False)
        self.ops = {e: [] for e in ENGINES}
        self.cnt = {e: 0 for e in ENGINES}
        self.sems = {}
        self.last_writer = {}
        self.readers = {}
        self.waited = {e: {e2: 0 for e2 in ENGINES} for e in ENGINES}
        self.stack = contextlib.ExitStack()
        self.tiles = {}
        self._tile_id = 0

    def sbuf(self, name, shape, dtype):
        t = self.stack.enter_context(self.nc.sbuf_tensor(name, list(shape), dtype))
        self.tiles[name] = t
        return t

    def psum(self, name, shape, dtype=F32):
        t = self.stack.enter_context(self.nc.psum_tensor(name, list(shape), dtype))
        self.tiles[name] = t
        return t

    def _wait(self, eng, dep_eng, dep_cnt):
        if self.waited[eng][dep_eng] >= dep_cnt:
            return
        self.waited[eng][dep_eng] = dep_cnt
        sems = self.sems

        def w(e, dep_eng=dep_eng, dep_cnt=dep_cnt):
            e.wait_ge(sems[dep_eng], dep_cnt)
        self.ops[eng].append(w)

    def emit(self, eng, fn, reads=(), writes=(), dma=False, inc=None):
        for r in reads:
            lw = self.last_writer.get(r)
            if lw is not None:
                dep_eng, dep_cnt = lw[0], lw[1]
                if dep_eng == eng and eng == "tensor":
                    pass
                elif dep_eng == eng and not dma and self.cnt[eng] == dep_cnt and False:
                    pass
                else:
                    if not (dep_eng == eng and self.waited[eng][eng] >= dep_cnt):
                        self._wait(eng, dep_eng, dep_cnt)
        for w_ in writes:
            rd = self.readers.get(w_)
            if rd:
                for dep_eng, dep_cnt in rd.items():
                    if dep_eng == eng and eng == "tensor":
                        continue
                    self._wait(eng, dep_eng, dep_cnt)
            lw = self.last_writer.get(w_)
            if lw is not None and not (lw[0] == eng == "tensor"):
                if dma and len(lw) > 2 and lw[2]:
                    pass
                else:
                    self._wait(eng, lw[0], lw[1])
        n = inc if inc is not None else (16 if dma else 1)
        sems = self.sems
        cnt_after = self.cnt[eng] + n
        my = eng

        def run(e, fn=fn, n=n, my=my):
            ins = fn(e)
            ins.then_inc(sems[my], n)
        self.ops[eng].append(run)
        self.cnt[eng] = cnt_after
        if dma:
            cnt_after = self.cnt[eng]
        for w_ in writes:
            self.last_writer[w_] = (eng, cnt_after, dma)
            self.readers.pop(w_, None)
        for r in reads:
            self.readers.setdefault(r, {})
            cur = self.readers[r].get(eng, 0)
            self.readers[r][eng] = max(cur, cnt_after)
        return cnt_after

    def raw(self, eng, fn):
        self.ops[eng].append(fn)

    def barrier(self):
        snap = dict(self.cnt)
        for e in ENGINES:
            for e2 in ENGINES:
                if e2 != e and snap[e2] > 0:
                    self._wait(e, e2, snap[e2])

    @contextlib.contextmanager
    def scope(self):
        outer = self.stack
        inner = contextlib.ExitStack()
        self.stack = inner
        try:
            yield
        finally:
            self.barrier()
            self.stack = outer
            inner.close()

    def build(self):
        nc = self.nc
        for e in ENGINES:
            self.sems[e] = self.stack.enter_context(nc.semaphore("sem_" + e))
        blk = self.stack.enter_context(nc.Block())
        for e in ENGINES:
            ops = self.ops[e]

            def body(engobj, ops=ops):
                for f in ops:
                    f(engobj)
            getattr(blk, e)(body)
        self.stack.close()
        return nc

    def finish(self):
        for e in ENGINES:
            if e == "gpsimd":
                continue
            c = self.cnt[e]
            if c:
                self._wait("gpsimd", e, c)
        c = self.cnt["gpsimd"]
        if c:
            sems = self.sems

            def w(eng, c=c):
                eng.wait_ge(sems["gpsimd"], c)
            self.ops["gpsimd"].append(w)



def ceil_div(a, b):
    return (a + b - 1) // b


def emit_const_setup(K):
    nc = K.nc
    ones_row = K.sbuf("ones_row", [1, 512], BF16)
    ones_col = K.sbuf("ones_col", [128, 1], F32R)
    ones_col32 = K.sbuf("ones_col32", [128, 1], F32)
    onesb_col = K.sbuf("onesb_col", [128, 1], BF16)
    ones_row128 = K.sbuf("ones_row128", [1, 128], F32)
    K.emit("gpsimd", lambda e: e.memset(ones_row[:, :], 1.0), writes=("ones_row",))
    K.emit("gpsimd", lambda e: e.memset(ones_col32[:, :], 1.0), writes=("ones_col32",))
    K.emit("vector", lambda e: e.tensor_copy(ones_col[:, :], ones_col32[:, :]), reads=("ones_col32",), writes=("ones_col",))
    K.emit("gpsimd", lambda e: e.memset(onesb_col[:, :], 1.0), writes=("onesb_col",))
    K.emit("gpsimd", lambda e: e.memset(ones_row128[:, :], 1.0), writes=("ones_row128",))
    cpool = K.sbuf("cpool", [128, 8], F32)
    for i, v in enumerate((1e-5, -0.5, 1e-6, 0.5)):
        K.emit("gpsimd", lambda e, i=i, v=v: e.memset(cpool[:, i:i + 1], v), writes=("cpool",))
        K.nc.const_aps.register(F32, v, cpool[:, i:i + 1])
    return dict(ones_row=ones_row, ones_col=ones_col, onesb_col=onesb_col, ones_row128=ones_row128,
                cpool=cpool)


def emit_ln(K, C, xt, xsq, stat_ps, rep_ps, mvec, T, extra_scale_ap=None,
            tag=""):
    nc = K.nc
    TM = xt.shape[1] // 8
    kx, ksq, kst = tag + "xt", tag + "xsq", tag + "stat"
    kmv = tag + "mvec"
    krepA, krepB = tag + "repA", tag + "repB"
    mA, mB = mvec[0:1, 0:512], mvec[0:1, 512:1024]
    K.emit("scalar", lambda e: e.activation(xsq[:, 0:8 * TM], xt[:, 0:8 * TM], AF.Square),
           reads=(kx,), writes=(ksq,))
    oc = C["ones_col"]
    for c in range(8):
        K.emit("tensor", lambda e, c=c: e.matmul(
            stat_ps[0:1, 0:T], oc[:, :], xt[:, c * TM:c * TM + T],
            start=(c == 0), stop=(c == 7)), reads=(kx, "ones_col"), writes=(kst,))
    for c in range(8):
        K.emit("tensor", lambda e, c=c: e.matmul(
            stat_ps[0:1, 512:512 + T], oc[:, :], xsq[:, c * TM:c * TM + T],
            start=(c == 0), stop=(c == 7)), reads=(ksq, "ones_col"), writes=(kst,))
    K.emit("vector", lambda e: e.tensor_scalar_mul(mA[:, 0:T], stat_ps[0:1, 0:T], 1.0 / 1024.0),
           reads=(kst,), writes=(kmv + "A",))
    K.emit("vector", lambda e: e.tensor_scalar_mul(mB[:, 0:T], stat_ps[0:1, 512:512 + T], 1.0 / 1024.0),
           reads=(kst,), writes=(kmv + "B",))
    orow = C["ones_row128"]
    K.emit("tensor", lambda e: e.matmul(rep_ps[:, 0:T], orow[0:1, :], mA[:, 0:T], start=True, stop=True),
           reads=(kmv + "A", "ones_row128"), writes=(krepA,))
    K.emit("vector", lambda e: e.tensor_tensor(stat_ps[0:1, 0:T], mA[:, 0:T], mA[:, 0:T], ALU.mult),
           reads=(kmv + "A",), writes=(kst,))
    K.emit("vector", lambda e: e.tensor_tensor(mB[:, 0:T], mB[:, 0:T], stat_ps[0:1, 0:T], ALU.subtract),
           reads=(kmv + "B", kst), writes=(kmv + "B",))
    K.emit("scalar", lambda e: e.activation(mA[:, 0:T], mB[:, 0:T], AF.Ln, bias=EPS),
           reads=(kmv + "B",), writes=(kmv + "A",))
    K.emit("scalar", lambda e: e.activation(mA[:, 0:T], mA[:, 0:T], AF.Exp, scale=-0.5),
           reads=(kmv + "A",), writes=(kmv + "A",))
    if extra_scale_ap is not None:
        K.emit("vector", lambda e: e.tensor_tensor(
            mA[:, 0:T], mA[:, 0:T], extra_scale_ap.broadcast_to([1, T]), ALU.mult),
            reads=(kmv + "A", "recip_s"), writes=(kmv + "A",))
    K.emit("tensor", lambda e: e.matmul(rep_ps[:, TM:TM + T], orow[0:1, :], mA[:, 0:T], start=True, stop=True),
           reads=(kmv + "A", "ones_row128"), writes=(krepB,))
    return krepA


def alloc_p0(K, pfx="p0_"):
    b = {}
    b['xt'] = K.sbuf(pfx + "xt", [128, 4096], F32R)
    b['xw'] = K.sbuf(pfx + "xw", [128, 4096], F32)
    b['xsq'] = K.sbuf(pfx + "xsq", [128, 4096], F32R)
    b['carve32'] = K.sbuf(pfx + "carve", [128, 1024], F32)
    b['xb'] = K.sbuf(pfx + "xb", [128, 4096], BF16)
    b['xnew'] = K.sbuf(pfx + "xnew", [128, 4096], F32)
    b['w0b'] = K.sbuf(pfx + "w0b", [128, 8192], BF16)
    b['biasb'] = K.sbuf(pfx + "biasb", [1, 1024], BF16)
    b['bias32'] = K.sbuf(pfx + "bias32", [1, 1024], F32)
    b['sc'] = K.sbuf(pfx + "sc", [1, 8], F32)
    b['c_ap'] = K.sbuf(pfx + "cap", [128, 1], F32)
    b['mvec'] = K.sbuf(pfx + "mvec", [1, 1024], F32)
    b['stat_ps'] = K.psum(pfx + "stat", [1, 1024], F32)
    b['rep_ps'] = K.psum(pfx + "rep", [128, 1024], F32)
    b['out_ps'] = K.psum(pfx + "out", [128, 1024], F32)
    return b


def emit_p0_setup(K, C, b, wT_h, bias_h, sin_h, sw_h):
    nc = K.nc
    w0b, biasb, bias32, sc, c_ap = b['w0b'], b['biasb'], b['bias32'], b['sc'], b['c_ap']
    for ic in range(8):
        K.emit("gpsimd", lambda e, ic=ic: e.dma_start(
            out=w0b[:, ic * 1024:(ic + 1) * 1024], in_=wT_h[ic * 128:(ic + 1) * 128, :]),
            writes=("w0b",), dma=True)
    K.emit("gpsimd", lambda e: e.dma_start(out=bias32[0:1, :], in_=bias_h[None, :]),
           writes=("bias32",), dma=True)
    K.emit("gpsimd", lambda e: e.dma_start(out=sc[0:1, 0:1], in_=sin_h[None, :]),
           writes=("sc",), dma=True)
    K.emit("gpsimd", lambda e: e.dma_start(out=sc[0:1, 1:2], in_=sw_h[None, :]),
           writes=("sc",), dma=True)
    K.emit("vector", lambda e: e.tensor_tensor(sc[0:1, 2:3], sc[0:1, 0:1], sc[0:1, 1:2], ALU.mult),
           reads=("sc",), writes=("sc2",))
    K.emit("vector", lambda e: e.reciprocal(sc[0:1, 3:4], sc[0:1, 2:3]),
           reads=("sc2",), writes=("sc3",))
    K.emit("vector", lambda e: e.tensor_scalar_mul(sc[0:1, 4:5], sc[0:1, 2:3], 0.5),
           reads=("sc2",), writes=("sc4",))
    K.emit("vector", lambda e: e.reciprocal(sc[0:1, 5:6], sc[0:1, 0:1]),
           reads=("sc",), writes=("recip_s",))
    K.emit("vector", lambda e: e.tensor_tensor(biasb[0:1, :], bias32[0:1, :],
                                               sc[0:1, 3:4].broadcast_to([1, 1024]), ALU.mult),
           reads=("bias32", "sc3"), writes=("biasb",))
    K.emit("tensor", lambda e: e.matmul(b['rep_ps'][:, 0:1], C['ones_row128'][0:1, :], sc[0:1, 4:5],
                                        start=True, stop=True),
           reads=("sc4", "ones_row128"), writes=("p0_repA",))
    K.emit("vector", lambda e: e.tensor_copy(c_ap[:, :], b['rep_ps'][:, 0:1]),
           reads=("p0_repA",), writes=("c_ap",))


def emit_p0_tile(K, C, b, xres, bidx, t0, T, tag):
    nc = K.nc
    xsrc, xdst = xres if isinstance(xres, tuple) else (xres, xres)
    TM = 512
    xt, xw, xsq, xb, xnew = b['xt'], b['xw'], b['xsq'], b['xb'], b['xnew']
    carve = b['carve32'].ap().bitcast(FP8)
    kx = "p0_xt"
    if T == TM:
        K.emit("gpsimd", lambda e, t0=t0: e.dma_start(
            out=xt[:, :].rearrange("p (c t) -> p c t", t=TM),
            in_=xsrc[bidx, :, t0:t0 + TM].rearrange("(c p) l -> p c l", p=128)),
            writes=(kx,), dma=True)
    else:
        for c in range(8):
            K.emit("gpsimd", lambda e, c=c: e.dma_start(
                out=xt[:, c * TM:c * TM + T], in_=xsrc[bidx, c * 128:(c + 1) * 128, t0:t0 + T]),
                writes=(kx,), dma=True)
    krep = emit_ln(K, C, xt, xsq, b['stat_ps'], b['rep_ps'], b['mvec'], T,
                   extra_scale_ap=b['sc'][0:1, 5:6], tag="p0_")
    rep = b['rep_ps']
    for c in range(8):
        K.emit("vector", lambda e, c=c: e.tensor_tensor(xw[:, c * TM:c * TM + T], xt[:, c * TM:c * TM + T],
                                                        rep[:, 0:T], ALU.subtract),
               reads=(kx, "p0_repA"), writes=("p0_xw",))
        K.emit("vector", lambda e, c=c: e.tensor_tensor(xw[:, c * TM:c * TM + T], xw[:, c * TM:c * TM + T],
                                                        rep[:, TM:TM + T], ALU.mult),
               reads=("p0_xw", "p0_repB"), writes=("p0_xw",))
        K.emit("vector", lambda e, c=c: e.tensor_scalar(carve[:, c * TM:c * TM + T], xw[:, c * TM:c * TM + T],
                                                        448.0, -448.0, ALU.min, ALU.max),
               reads=("p0_xw",), writes=("p0_carve",))
        K.emit("scalar", lambda e, c=c: e.activation(xb[:, c * TM:c * TM + T], carve[:, c * TM:c * TM + T], AF.Copy),
               reads=("p0_carve",), writes=("p0_xb",))
    out_ps = b['out_ps']
    for oc in range(8):
        pk = "p0_out%d" % (oc % 2)
        K.emit("tensor", lambda e, oc=oc: e.matmul(
            out_ps[:, (oc % 2) * 512:(oc % 2) * 512 + T],
            b['biasb'][0:1, oc * 128:(oc + 1) * 128], C['ones_row'][0:1, 0:T],
            start=True, stop=False), reads=("biasb", "ones_row"), writes=(pk,))
        for ic in range(8):
            K.emit("tensor", lambda e, oc=oc, ic=ic: e.matmul(
                out_ps[:, (oc % 2) * 512:(oc % 2) * 512 + T],
                b['w0b'][:, ic * 1024 + oc * 128:ic * 1024 + (oc + 1) * 128],
                xb[:, ic * TM:ic * TM + T],
                start=False, stop=(ic == 7)), reads=("w0b", "p0_xb"), writes=(pk,))
        K.emit("vector", lambda e, oc=oc: e.scalar_tensor_tensor(
            xnew[:, oc * TM:oc * TM + T], out_ps[:, (oc % 2) * 512:(oc % 2) * 512 + T],
            b['c_ap'][:, :], xt[:, oc * TM:oc * TM + T], ALU.mult, ALU.add),
            reads=(pk, "c_ap", kx), writes=("p0_xnew%d" % oc,))
        K.emit("sync", lambda e, oc=oc: e.dma_start(
            out=xdst[bidx, oc * 128:(oc + 1) * 128, t0:t0 + T], in_=xnew[:, oc * TM:oc * TM + T]),
            reads=("p0_xnew%d" % oc,), dma=True, writes=())


def emit_p0(K, C, b, xres, L):
    tiles = [(0, 512), (512, L - 512)] if L > 512 else [(0, L)]
    assert all(T % 2 == 0 for _, T in tiles)
    for bidx in range(16):
        for (t0, T) in tiles:
            emit_p0_tile(K, C, b, xres, bidx, t0, T, tag="p0_%d_%d_" % (bidx, t0))



def alloc_p1(K):
    b = {}
    b['xt'] = K.sbuf("q_xt", [128, 4096], F32R)
    b['xsq'] = K.sbuf("q_xsq", [128, 4096], F32R)
    b['dts'] = K.sbuf("q_dts2", [32, 772], F32)

    b['mvec'] = K.sbuf("q_mvec", [1, 1024], F32)
    b['w_in0'] = K.sbuf("q_win0", [128, 2048], BF16)
    b['w_in1'] = K.sbuf("q_win1", [128, 2048], BF16)
    b['w_op'] = K.sbuf("q_wop", [128, 2048], BF16)
    b['xbc_pre'] = K.sbuf("q_xbcp", [128, 18 * 772], BF16)
    b['xbc2'] = K.sbuf("q_xbc2", [128, 18 * 768], BF16)
    b['dtraw'] = K.sbuf("q_dtraw", [32, 772], F32)

    b['sdt'] = K.sbuf("q_sdt", [64, 772], F32)
    b['sposatmp'] = K.sbuf("q_sposatmp", [64, 768], F32)
    b['spos'] = b['sdt'][:, 0:384]
    b['atmp'] = b['sdt'][:, 384:768]

    b['s_all'] = K.sbuf("q_sall", [64, 12 * 32], F32)
    b['bmid'] = K.sbuf("q_bmid", [1, 384], F32)
    b['slast'] = K.sbuf("q_slast", [1, 384], F32)
    b['wbase'] = K.sbuf("q_wbase", [1, 384], F32)
    b['g1row'] = K.sbuf("q_g1row", [1, 384], F32)

    b['ubf0'] = K.sbuf("q_ubf0", [64, 96], BF16)
    b['ubf1'] = K.sbuf("q_ubf1", [64, 96], BF16)
    b['uvw0'] = K.sbuf("q_uvw0", [64, 96], F32)
    b['uvw1'] = K.sbuf("q_uvw1", [64, 96], F32)
    b['XT0'] = K.sbuf("q_XT0", [128, 2048], BF16)
    b['BT'] = K.sbuf("q_BT", [128, 768], BF16)
    b['Xu0'] = K.sbuf("q_Xu0", [64, 2048], BF16)
    b['Xu1'] = K.sbuf("q_Xu1", [64, 2048], BF16)
    b['Xw0'] = K.sbuf("q_Xw0", [64, 2048], BF16)
    b['Xw1'] = K.sbuf("q_Xw1", [64, 2048], BF16)
    b['gm0'] = K.sbuf("q_gm0", [64, 64], BF16)
    b['gm1'] = K.sbuf("q_gm1", [64, 64], BF16)
    b['btc0'] = K.sbuf("q_btc0", [64, 128], BF16)
    b['btc1'] = K.sbuf("q_btc1", [64, 128], BF16)
    b['xtc0'] = K.sbuf("q_xtc0", [64, 2048], BF16)
    b['xtc1'] = b['xtc0']
    b['T'] = K.sbuf("q_T", [128, 2048], BF16)
    b['Yf'] = K.sbuf("q_Yf", [128, 8192], BF16)
    b['xn'] = b['Yf']
    b['zg'] = K.sbuf("q_zg", [128, 8192], BF16)
    b['scr1'] = K.sbuf("q_scr1", [128, 512], BF16)
    b['convd'] = K.sbuf("q_convd", [128, 72 * 128], BF16)
    b['convb'] = K.sbuf("q_convb", [128, 18], F32)
    b['cwt'] = K.sbuf("q_cwt", [128, 72], F32)
    b['dtb'] = K.sbuf("q_dtb", [32, 1], F32)
    b['negA'] = K.sbuf("q_negA", [32, 2], F32)
    b['Dcol'] = K.sbuf("q_Dcol", [128, 16], F32)
    b['nwcol'] = K.sbuf("q_nw", [128, 16], F32)
    b['flagc'] = K.sbuf("q_flagc", [128, 1], F32)
    b['tril'] = K.sbuf("q_tril", [64, 64], F32)
    b['triu_m'] = K.sbuf("q_trium", [64, 64], F32)
    b['esel'] = K.sbuf("q_esel", [64, 2], F32)
    b['e31'] = b['esel'][:, 0:1]
    b['e63'] = b['esel'][:, 1:2]
    b['ident'] = K.sbuf("q_ident", [64, 64], F32)
    b['ps_stat'] = K.psum("q_psstat", [1, 1024], F32)
    b['ps_mr'] = K.psum("q_psmr", [128, 1024], F32)
    b['ps_big'] = K.psum("q_psbig", [128, 2048], F32)
    return b


def emit_p1_setup(K, C, b, hb):
    nc = K.nc
    for nm, dst in (("cwt", b['cwt']), ("dtb", b['dtb']), ("Dcol", b['Dcol']),
                    ("nwcol", b['nwcol']), ("tril", b['tril']),
                    ("triu", b['triu_m']), ("esel", b['esel']),
                    ("ident", b['ident'])):
        K.emit("sync", lambda e, nm=nm, dst=dst: e.dma_start(out=dst[:, :], in_=hb[nm]),
               writes=("q_" + nm,), dma=True)
    K.emit("sync", lambda e: e.dma_start(out=b['convb'][:, :], in_=hb["convb"]),
           writes=("q_convb",), dma=True)
    K.emit("sync", lambda e: e.dma_start(out=b['flagc'][0:1, 0:1], in_=hb["hflag"]),
           writes=("q_flagc",), dma=True)
    K.emit("tensor", lambda e: e.matmul(b['ps_mr'][:, 0:1], C['ones_row128'][0:1, :], b['flagc'][0:1, 0:1],
                                        start=True, stop=True), reads=("q_flagc", "ones_row128"), writes=("q_repA",))
    K.emit("vector", lambda e: e.tensor_copy(b['flagc'][:, :], b['ps_mr'][:, 0:1]),
           reads=("q_repA",), writes=("q_flagc",))
    K.emit("gpsimd", lambda e: e.dma_start(out=b['negA'][:, 0:1], in_=hb["A_log"]),
           writes=("q_negA",), dma=True)
    K.emit("scalar", lambda e: e.activation(b['negA'][:, 1:2], b['negA'][:, 0:1], AF.Exp),
           reads=("q_negA",), writes=("q_negA1",))
    K.emit("vector", lambda e: e.tensor_scalar_mul(b['negA'][:, 0:1], b['negA'][:, 1:2], -1.0),
           reads=("q_negA1",), writes=("q_negA0",))
    b['hb_wop'] = hb["w_op"]
    identb = b['convd'][:, 0:128]
    K.emit("gpsimd", lambda e: e.dma_start(out=identb, in_=hb["ident128"]),
           writes=("q_convd",), dma=True)
    pairs = [(k, ch) for k in range(4) for ch in range(18)]
    pairs.remove((0, 0))
    pairs.append((0, 0))
    for k, ch in pairs:
        K.emit("vector", lambda e, k=k, ch=ch: e.tensor_scalar(
            b['convd'][:, (k * 18 + ch) * 128:(k * 18 + ch + 1) * 128],
            b['convd'][:, 0:128], b['cwt'][:, ch * 4 + k:ch * 4 + k + 1], None, ALU.mult),
            reads=("q_convd", "q_cwt"), writes=("q_convd",))


def emit_p1_batch(K, C, b, xA, xB, bidx):
    nc = K.nc
    TM = 512
    xt, xsq, xn = b['xt'], b['xsq'], b['xn']
    ps_mr, ps_stat, ps_big = b['ps_mr'], b['ps_stat'], b['ps_big']
    hbw = b['hbw']
    kbig = "q_big"
    K.emit("gpsimd", lambda e: e.memset(b['T'][:, :], 0.0), writes=("q_T",))

    tiles = [(0, 512), (512, 260)]
    for ti, (t0, T) in enumerate(tiles):
        if T == TM:
            K.emit("gpsimd", lambda e, t0=t0: e.dma_start(
                out=xt[:, :].rearrange("p (c t) -> p c t", t=TM),
                in_=xA[bidx, :, t0:t0 + TM].rearrange("(c p) l -> p c l", p=128)),
                writes=("q_xt",), dma=True)
        else:
            for c in range(8):
                K.emit("gpsimd", lambda e, c=c, t0=t0, T=T: e.dma_start(
                    out=xt[:, c * TM:c * TM + T], in_=xA[bidx, c * 128:(c + 1) * 128, t0:t0 + T]),
                    writes=("q_xt",), dma=True)
        emit_ln(K, C, xt, xsq, ps_stat, ps_mr, b['mvec'], T, tag="q_")
        for c in range(8):
            K.emit("vector", lambda e, c=c, T=T: e.tensor_tensor(
                xsq[:, c * TM:c * TM + T], xt[:, c * TM:c * TM + T], ps_mr[:, 0:T], ALU.subtract),
                reads=("q_xt", "q_repA"), writes=("q_xsq",))
            K.emit("vector", lambda e, c=c, T=T, t0=t0: e.tensor_tensor(
                xn[:, c * 772 + t0: c * 772 + t0 + T], xsq[:, c * TM:c * TM + T], ps_mr[:, TM:TM + T], ALU.mult),
                reads=("q_xsq", "q_repB"), writes=("q_Yf",))
    groups = [(i * 256, 256) for i in range(17)] + [(4352, 32)]
    for gi, (g0, gw) in enumerate(groups):
        wbuf = b['w_in%d' % (gi % 2)]
        wkey = "q_win%d" % (gi % 2)
        K.emit("gpsimd", lambda e, g0=g0, gw=gw, wbuf=wbuf: e.dma_start(
            out=wbuf[:, 0:8 * gw].rearrange("p (ic c) -> p ic c", c=gw),
            in_=hbw[:, g0:g0 + gw].rearrange("(ic p) c -> p ic c", p=128)),
            writes=(wkey,), dma=True)
        nj = (gw + 127) // 128
        for j in range(nj):
            fc_abs = (g0 + j * 128) // 128
            M = min(128, gw - j * 128)
            for ti, (t0, T) in enumerate(tiles):
                half = (2 * fc_abs + ti) % 2
                pk = "q_rep%s" % ("A" if half == 0 else "B")
                for ic in range(8):
                    K.emit("tensor", lambda e, ic=ic, j=j, M=M, half=half, gw=gw, T=T, t0=t0, wbuf=wbuf: e.matmul(
                        ps_mr[0:M, half * 512:half * 512 + T],
                        wbuf[:, ic * gw + j * 128: ic * gw + j * 128 + M],
                        xn[:, ic * 772 + t0: ic * 772 + t0 + T],
                        start=(ic == 0), stop=(ic == 7)),
                        reads=(wkey, "q_Yf"), writes=(pk,))
                if fc_abs < 16:
                    cc = fc_abs
                    if ti == 0:
                        K.emit("scalar", lambda e, cc=cc, half=half: e.activation(
                            b['zg'][:, cc * 512: cc * 512 + 252],
                            ps_mr[:, half * 512 + 260: half * 512 + 512], AF.Copy),
                            reads=(pk,), writes=("q_zg",))
                    else:
                        K.emit("scalar", lambda e, cc=cc, half=half: e.activation(
                            b['zg'][:, cc * 512 + 252: cc * 512 + 512],
                            ps_mr[:, half * 512: half * 512 + 260], AF.Copy),
                            reads=(pk,), writes=("q_zg",))
                elif fc_abs < 34:
                    ch = fc_abs - 16
                    if ti == 0:
                        K.emit("scalar", lambda e, ch=ch, half=half: e.activation(
                            b['xbc_pre'][:, ch * 772: ch * 772 + 512],
                            ps_mr[:, half * 512: half * 512 + 512], AF.Copy),
                            reads=(pk,), writes=("q_xbcp",))
                    else:
                        K.emit("scalar", lambda e, ch=ch, half=half: e.activation(
                            b['xbc_pre'][:, ch * 772 + 512: ch * 772 + 772],
                            ps_mr[:, half * 512: half * 512 + 260], AF.Copy),
                            reads=(pk,), writes=("q_xbcp",))
                else:
                    K.emit("vector", lambda e, half=half, t0=t0, T=T: e.tensor_copy(
                        b['dtraw'][0:32, t0:t0 + T], ps_mr[0:32, half * 512: half * 512 + T]),
                        reads=(pk,), writes=("q_dtraw",))
    for ti in range(2):
        if ti == 0:
            o0, OT, rs = 0, 508, 1
        else:
            o0, OT, rs = 508, 260, 509
        for ch in range(18):
            for k in range(4):
                K.emit("tensor", lambda e, ch=ch, k=k, OT=OT, rs=rs: e.matmul(
                    ps_mr[:, 0:OT], b['convd'][:, ((3 - k) * 18 + ch) * 128:((3 - k) * 18 + ch + 1) * 128],
                    b['xbc_pre'][:, ch * 772 + rs + k: ch * 772 + rs + k + OT],
                    start=(k == 0), stop=(k == 3)), reads=("q_convd", "q_xbcp"), writes=("q_repA",))
            K.emit("scalar", lambda e, ch=ch, o0=o0, OT=OT: e.activation(
                b['xbc2'][:, ch * 768 + o0: ch * 768 + o0 + OT], ps_mr[:, 0:OT], AF.Silu,
                bias=b['convb'][:, ch:ch + 1]),
                reads=("q_repA", "q_convb"), writes=("q_xbc2",))
    dts = b['dts']
    dA = dts[0:32, :]
    dR = b['dtraw'][0:32, :]
    K.emit("vector", lambda e: e.tensor_scalar(dA, dR, b['dtb'][:, :], None, ALU.add),
           reads=("q_dtraw", "q_dtb"), writes=("q_xsq",))
    K.emit("scalar", lambda e: e.activation(dR, dA, AF.Abs),
           reads=("q_xsq",), writes=("q_dtraw",))
    K.emit("scalar", lambda e: e.activation(dR, dR, AF.Exp, scale=-1.0),
           reads=("q_dtraw",), writes=("q_dtraw",))
    K.emit("scalar", lambda e: e.activation(dR, dR, AF.Ln, bias=1.0),
           reads=("q_dtraw",), writes=("q_dtraw",))
    K.emit("vector", lambda e: e.scalar_tensor_tensor(dA, dA, 0.0, dR, ALU.max, ALU.add),
           reads=("q_xsq", "q_dtraw"), writes=("q_xsq",))
    K.emit("vector", lambda e: e.tensor_copy(b['sdt'][32:64, 0:772], dts[0:32, :]),
           reads=("q_xsq",), writes=("q_sdt",))
    K.emit("vector", lambda e: e.tensor_scalar(b['sdt'][32:64, 0:260], b['sdt'][32:64, 0:260],
                                               b['flagc'][32:64, :], None, ALU.mult),
           reads=("q_sdt", "q_flagc"), writes=("q_sdt",))
    K.emit("vector", lambda e: e.tensor_scalar(b['sdt'][0:32, 0:772], dts[0:32, :], b['negA'][:, 0:1], None, ALU.mult),
           reads=("q_xsq", "q_negA0"), writes=("q_sdt",))
    for w in range(3):
        for i in range(4):
            c = w * 4 + i
            K.emit("tensor", lambda e, c=c, i=i: e.transpose(
                ps_big[0:64, i * 512: i * 512 + 64],
                b['sdt'][:, 4 + 64 * c: 68 + 64 * c], b['ident'][:, :]),
                reads=("q_sdt", "q_ident"), writes=(kbig,))
        for i in range(4):
            c = w * 4 + i
            K.emit("vector", lambda e, c=c, i=i: e.tensor_copy(
                b['sposatmp'][:, 384 + c * 32: 384 + (c + 1) * 32], ps_big[0:64, i * 512: i * 512 + 32]),
                reads=(kbig,), writes=("q_sdt",))
            K.emit("vector", lambda e, c=c, i=i: e.tensor_copy(
                b['sposatmp'][:, c * 32:(c + 1) * 32], ps_big[0:64, i * 512 + 32: i * 512 + 64]),
                reads=(kbig,), writes=("q_sdt",))
    K.emit("tensor", lambda e: e.matmul(ps_big[0:64, 0:384], b['tril'][:, :],
                                        b['sposatmp'][:, 384:768], start=True, stop=True),
           reads=("q_tril", "q_sdt"), writes=(kbig,))
    K.emit("vector", lambda e: e.tensor_copy(b['s_all'][:, 0:384], ps_big[0:64, 0:384]),
           reads=(kbig,), writes=("q_sall",))
    K.emit("tensor", lambda e: e.matmul(ps_big[0:1, 0:384], b['esel'][:, 0:1],
                                        b['s_all'][:, 0:384], start=True, stop=True),
           reads=("q_esel", "q_sall"), writes=(kbig,))
    K.emit("vector", lambda e: e.tensor_copy(b['bmid'][0:1, 0:384], ps_big[0:1, 0:384]),
           reads=(kbig,), writes=("q_bmid",))
    K.emit("tensor", lambda e: e.matmul(ps_big[0:1, 0:384], b['esel'][:, 1:2],
                                        b['s_all'][:, 0:384], start=True, stop=True),
           reads=("q_esel", "q_sall"), writes=(kbig,))
    K.emit("vector", lambda e: e.tensor_copy(b['slast'][0:1, 0:384], ps_big[0:1, 0:384]),
           reads=(kbig,), writes=("q_slast",))
    K.emit("vector", lambda e: e.tensor_tensor(b['wbase'][0:1, 0:352], b['slast'][0:1, 0:352],
                                               b['bmid'][0:1, 32:384], ALU.add),
           reads=("q_slast", "q_bmid"), writes=("q_wbase",))
    K.emit("vector", lambda e: e.tensor_copy(b['wbase'][0:1, 352:384], b['slast'][0:1, 352:384]),
           reads=("q_slast",), writes=("q_wbase",))
    K.emit("vector", lambda e: e.tensor_tensor(b['g1row'][0:1, 0:384], b['wbase'][0:1, 0:384],
                                               b['bmid'][0:1, 0:384], ALU.subtract),
           reads=("q_wbase", "q_bmid"), writes=("q_g1row",))
    K.emit("scalar", lambda e: e.activation(b['g1row'][0:1, 0:384], b['g1row'][0:1, 0:384], AF.Exp),
           reads=("q_g1row",), writes=("q_g1row",))
    for cp in range(6):
        K.emit("sync", lambda e, cp=cp: e.dma_start_transpose(
            b['BT'][:, cp * 128:(cp + 1) * 128], b['xbc2'][:, 16 * 768 + cp * 128: 16 * 768 + (cp + 1) * 128]),
            reads=("q_xbc2",), writes=("q_BT",), dma=True)
    def prep(c):
        q = c % 2
        cp, off = c // 2, (c % 2) * 64
        uvw, ubf = b['uvw%d' % q], b['ubf%d' % q]
        kuvw, kubf = "q_uvw%d" % q, "q_ubf%d" % q
        kXT, kxtc = "q_XT0", "q_xtc0"
        kXu, kXw = "q_Xu%d" % q, "q_Xw%d" % q
        if c % 2 == 0:
            for cc in range(16):
                K.emit("sync", lambda e, cc=cc, cp=cp: e.dma_start_transpose(
                    b['XT0'][:, cc * 128:(cc + 1) * 128],
                    b['xbc2'][:, cc * 768 + cp * 128: cc * 768 + (cp + 1) * 128]),
                    reads=("q_xbc2",), writes=("q_XT0",), dma=True)
        K.emit("tensor", lambda e, c=c: e.matmul(ps_mr[0:64, 576:608], C['ones_row128'][0:1, 0:64],
                                                 b['bmid'][0:1, c * 32:(c + 1) * 32], start=True, stop=True),
               reads=("ones_row128", "q_bmid"), writes=("q_prep",))
        K.emit("vector", lambda e, c=c, uvw=uvw: e.tensor_tensor(uvw[:, 0:32], ps_mr[0:64, 576:608],
                                                        b['s_all'][:, c * 32:(c + 1) * 32], ALU.subtract),
               reads=("q_prep", "q_sall"), writes=(kuvw,))
        K.emit("vector", lambda e, c=c, uvw=uvw: e.tensor_tensor(uvw[:, 32:64], b['s_all'][:, c * 32:(c + 1) * 32],
                                                        ps_mr[0:64, 576:608], ALU.subtract),
               reads=("q_prep", "q_sall"), writes=(kuvw,))
        K.emit("tensor", lambda e, c=c: e.matmul(ps_mr[0:64, 608:640], C['ones_row128'][0:1, 0:64],
                                                 b['wbase'][0:1, c * 32:(c + 1) * 32], start=True, stop=True),
               reads=("ones_row128", "q_wbase"), writes=("q_prep",))
        K.emit("vector", lambda e, c=c, uvw=uvw: e.tensor_tensor(uvw[:, 64:96], ps_mr[0:64, 608:640],
                                                        b['s_all'][:, c * 32:(c + 1) * 32], ALU.subtract),
               reads=("q_prep", "q_sall"), writes=(kuvw,))
        K.emit("scalar", lambda e, uvw=uvw: e.activation(uvw[:, 0:96], uvw[:, 0:96], AF.Exp),
               reads=(kuvw,), writes=(kuvw,))
        K.emit("vector", lambda e, c=c, uvw=uvw, ubf=ubf: e.tensor_tensor(ubf[:, 0:32], uvw[:, 0:32],
                                                        b['sposatmp'][:, c * 32:(c + 1) * 32], ALU.mult),
               reads=(kuvw, "q_sdt"), writes=(kubf,))
        K.emit("vector", lambda e, uvw=uvw, ubf=ubf: e.tensor_copy(ubf[:, 32:64], uvw[:, 32:64]),
               reads=(kuvw,), writes=(kubf,))
        K.emit("vector", lambda e, c=c, uvw=uvw, ubf=ubf: e.tensor_tensor(ubf[:, 64:96], uvw[:, 64:96],
                                                        b['sposatmp'][:, c * 32:(c + 1) * 32], ALU.mult),
               reads=(kuvw, "q_sdt"), writes=(kubf,))
        ub = ubf[:, 0:32].rearrange("p (h q) -> p h q", q=1).broadcast_to([64, 32, 64])
        wb = ubf[:, 64:96].rearrange("p (h q) -> p h q", q=1).broadcast_to([64, 32, 64])
        xtc = b['xtc0']
        K.emit("vector", lambda e, off=off, xtc=xtc: e.tensor_copy(xtc[:, :], b['XT0'][off:off + 64, :]),
               reads=("q_XT0",), writes=(kxtc,))
        xtv = xtc[:, :].rearrange("p (h q) -> p h q", q=64)
        K.emit("vector", lambda e, q2=q: e.tensor_tensor(b['Xu%d' % q2][:, :].rearrange("p (h q) -> p h q", q=64), xtv, ub, ALU.mult),
               reads=(kxtc, kubf), writes=(kXu,))
        K.emit("vector", lambda e, q2=q: e.tensor_tensor(b['Xw%d' % q2][:, :].rearrange("p (h q) -> p h q", q=64), xtv, wb, ALU.mult),
               reads=(kxtc, kubf), writes=(kXw,))
        K.emit("vector", lambda e, cp=cp, off=off, q2=q: e.tensor_copy(
            b['btc%d' % q2][:, :], b['BT'][off:off + 64, cp * 128:(cp + 1) * 128]),
            reads=("q_BT",), writes=("q_btc%d" % q,))
        if c >= 4:
            K.emit("tensor", lambda e, c=c: e.matmul(ps_mr[0:64, 960:1024],
                                                     b['xbc2'][:, 16 * 768 + c * 64: 16 * 768 + (c + 1) * 64],
                                                     b['xbc2'][:, 17 * 768 + c * 64: 17 * 768 + (c + 1) * 64],
                                                     start=True, stop=True),
                   reads=("q_xbc2",), writes=("q_prep",))
            K.emit("vector", lambda e, q2=q: e.tensor_tensor(b['gm%d' % q2][:, :], ps_mr[0:64, 960:1024], b['triu_m'][:, :], ALU.mult),
                   reads=("q_prep", "q_trium"), writes=("q_gm%d" % q,))

    def main(c):
        q = c % 2
        is_blk = c >= 4
        kXu, kXw = "q_Xu%d" % q, "q_Xw%d" % q
        if is_blk:
            for nb in range(4):
                K.emit("tensor", lambda e, nb=nb, q2=q: e.matmul(
                    ps_big[0:64, nb * 512:(nb + 1) * 512], b['gm%d' % q2][:, :], b['Xu%d' % q2][:, nb * 512:(nb + 1) * 512],
                    start=True, stop=False), reads=("q_gm%d" % q, kXu), writes=(kbig,))
                K.emit("tensor", lambda e, nb=nb, c=c: e.matmul(
                    ps_big[0:64, nb * 512:(nb + 1) * 512],
                    b['xbc2'][:, 17 * 768 + c * 64: 17 * 768 + (c + 1) * 64],
                    b['T'][:, nb * 512:(nb + 1) * 512],
                    start=False, stop=True), reads=("q_xbc2", "q_T"), writes=(kbig,))
            vb = b['ubf%d' % q][:, 32:64].rearrange("p (h q) -> p h q", q=1).broadcast_to([64, 32, 64])
            K.emit("vector", lambda e, q2=q: e.tensor_tensor(b['Xu%d' % q2][:, :].rearrange("p (h q) -> p h q", q=64),
                                                       ps_big[0:64, 0:2048].rearrange("p (h q) -> p h q", q=64),
                                                       vb, ALU.mult),
                   reads=(kbig, "q_ubf%d" % q, kXu), writes=(kXu,))
            for cc in range(16):
                K.emit("sync", lambda e, cc=cc, c=c, q2=q: e.dma_start_transpose(
                    b['Yf'][:, cc * 512 + (c - 4) * 64: cc * 512 + (c - 4) * 64 + 64],
                    b['Xu%d' % q2][:, cc * 128:(cc + 1) * 128]),
                    reads=(kXu,), writes=("q_Yf",), dma=True)
        K.emit("vector", lambda e, q2=q: e.tensor_copy(b['btc%d' % q2][:, :], b['btc%d' % q2][:, :]),
               reads=("q_btc%d" % q,), writes=("q_btc%d" % q,)) if False else None
        for nb in range(4):
            K.emit("tensor", lambda e, nb=nb, q2=q: e.matmul(
                ps_big[0:128, nb * 512:(nb + 1) * 512], b['btc%d' % q2][:, :],
                b['Xw%d' % q2][:, nb * 512:(nb + 1) * 512],
                start=True, stop=True), reads=("q_btc%d" % q, kXw), writes=(kbig,))
        K.emit("tensor", lambda e, c=c: e.matmul(ps_mr[0:128, 0:32], C['ones_row128'][0:1, :],
                                                 b['g1row'][0:1, c * 32:(c + 1) * 32], start=True, stop=True),
               reads=("ones_row128", "q_g1row"), writes=("q_g1",))
        g1v = ps_mr[0:128, 0:32].rearrange("p (h q) -> p h q", q=1).broadcast_to([128, 32, 64])
        K.emit("vector", lambda e: e.tensor_tensor(b['T'][:, :].rearrange("p (h q) -> p h q", q=64),
                                                   b['T'][:, :].rearrange("p (h q) -> p h q", q=64), g1v, ALU.mult),
               reads=("q_T", "q_g1"), writes=("q_T",))
        K.emit("vector", lambda e: e.tensor_tensor(b['T'][:, :], b['T'][:, :], ps_big[0:128, 0:2048], ALU.add),
               reads=("q_T", kbig), writes=("q_T",))

    prep(0)
    for c in range(12):
        if c + 1 < 12:
            prep(c + 1)
        main(c)
    for cc in range(16):
        K.emit("scalar", lambda e, cc=cc: e.activation(b['scr1'][:, :], b['zg'][:, cc * 512:(cc + 1) * 512], AF.Silu),
               reads=("q_zg",), writes=("q_scr1",))
        K.emit("vector", lambda e, cc=cc: e.tensor_tensor(
            b['Yf'][:, cc * 512:(cc + 1) * 512], b['Yf'][:, cc * 512:(cc + 1) * 512], b['scr1'][:, :], ALU.mult),
            reads=("q_Yf", "q_scr1"), writes=("q_Yf",))
        K.emit("vector", lambda e, cc=cc: e.scalar_tensor_tensor(
            b['scr1'][:, :], b['xbc2'][:, cc * 768 + 256: cc * 768 + 768], b['Dcol'][:, cc:cc + 1],
            b['scr1'][:, :], ALU.mult, ALU.mult),
            reads=("q_xbc2", "q_Dcol", "q_scr1"), writes=("q_scr1",))
        K.emit("vector", lambda e, cc=cc: e.tensor_tensor(
            b['Yf'][:, cc * 512:(cc + 1) * 512], b['Yf'][:, cc * 512:(cc + 1) * 512], b['scr1'][:, :], ALU.add),
            reads=("q_Yf", "q_scr1"), writes=("q_Yf",))
        K.emit("scalar", lambda e, cc=cc: e.activation(b['scr1'][:, :], b['Yf'][:, cc * 512:(cc + 1) * 512], AF.Square),
               reads=("q_Yf",), writes=("q_scr1",))
        K.emit("tensor", lambda e, cc=cc: e.matmul(ps_stat[0:1, 0:512], C['onesb_col'][:, :], b['scr1'][:, :],
                                                   start=(cc == 0), stop=(cc == 15)),
               reads=("q_scr1", "onesb_col"), writes=("q_stat",))
    K.emit("vector", lambda e: e.tensor_scalar_mul(b['mvec'][0:1, 0:512], ps_stat[0:1, 0:512], 1.0 / 2048.0),
           reads=("q_stat",), writes=("q_mvecA",))
    K.emit("scalar", lambda e: e.activation(b['mvec'][0:1, 0:512], b['mvec'][0:1, 0:512], AF.Ln, bias=EPS),
           reads=("q_mvecA",), writes=("q_mvecA",))
    K.emit("scalar", lambda e: e.activation(b['mvec'][0:1, 0:512], b['mvec'][0:1, 0:512], AF.Exp, scale=-0.5),
           reads=("q_mvecA",), writes=("q_mvecA",))
    K.emit("tensor", lambda e: e.matmul(ps_mr[:, 0:512], C['ones_row128'][0:1, :], b['mvec'][0:1, 0:512],
                                        start=True, stop=True), reads=("ones_row128", "q_mvecA"), writes=("q_repA",))
    for cc in range(16):
        K.emit("vector", lambda e, cc=cc: e.scalar_tensor_tensor(
            b['zg'][:, cc * 512:(cc + 1) * 512], b['Yf'][:, cc * 512:(cc + 1) * 512],
            b['nwcol'][:, cc:cc + 1], ps_mr[:, 0:512], ALU.mult, ALU.mult),
            reads=("q_Yf", "q_nw", "q_repA"), writes=("q_zg",))
    K.emit("gpsimd", lambda e: e.dma_start(
        out=xt[:, :].rearrange("p (c t) -> p c t", t=TM),
        in_=xA[bidx, :, 260:772].rearrange("(c p) l -> p c l", p=128)),
        writes=("q_xt",), dma=True)
    for oc in range(8):
        K.emit("gpsimd", lambda e, oc=oc: e.dma_start(
            out=b['w_op'][:, :].rearrange("p (kc c) -> p kc c", c=128),
            in_=b['hb_wop'][:, oc * 128:(oc + 1) * 128].rearrange("(kc p) c -> p kc c", p=128)),
            writes=("q_wop",), dma=True)
        for kc in range(16):
            K.emit("tensor", lambda e, oc=oc, kc=kc: e.matmul(
                ps_mr[:, (oc % 2) * 512:(oc % 2) * 512 + 512],
                b['w_op'][:, kc * 128:(kc + 1) * 128],
                b['zg'][:, kc * 512:(kc + 1) * 512],
                start=(kc == 0), stop=(kc == 15)), reads=("q_wop", "q_zg"),
                writes=("q_rep%s" % ("A" if oc % 2 == 0 else "B"),))
        K.emit("vector", lambda e, oc=oc: e.tensor_tensor(
            xsq[:, oc * TM:(oc + 1) * TM], ps_mr[:, (oc % 2) * 512:(oc % 2) * 512 + 512],
            xt[:, oc * TM:(oc + 1) * TM], ALU.add),
            reads=("q_rep%s" % ("A" if oc % 2 == 0 else "B"), "q_xt"), writes=("q_xsq",))
        K.emit("gpsimd", lambda e, oc=oc: e.dma_start(
            out=xB[bidx, oc * 128:(oc + 1) * 128, 0:512], in_=xsq[:, oc * TM:(oc + 1) * TM]),
            reads=("q_xsq",), dma=True)


def emit_p1(K, C, b, xA, xB):
    for bidx in range(16):
        emit_p1_batch(K, C, b, xA, xB, bidx)



def alloc_p3(K, pfx="m_"):
    b = {'pfx': pfx}
    S = lambda n, sh, dt: K.sbuf(pfx + n, sh, dt)
    b['xt'] = S("xt", [128, 4096], F32R)
    b['xsq'] = S("xsq", [128, 4096], F32R)
    b['xn'] = S("xn", [128, 4096], BF16)
    b['xnr'] = S("xnr", [128, 4096], F32R)
    b['mvec'] = S("mvec", [1, 1024], F32)
    b['wc'] = S("wc", [128, 1024], F32R)
    b['mx'] = S("mx", [128, 16], F32)
    b['maskb'] = S("maskb", [128, 8192], BF16)
    b['maskT'] = S("maskT", [64, 2048], BF16)
    b['xnT'] = S("xnT", [64, 8192], BF16)
    b['xselsb'] = S("xselsb", [128, 1024], BF16)
    for p_ in (0, 1):
        b['w1sb%d' % p_] = S("w1sb%d" % p_, [128, 256], BF16)
        b['w2sb%d' % p_] = S("w2sb%d" % p_, [32, 1024], BF16)
        b['b2sb%d' % p_] = S("b2sb%d" % p_, [1, 1024], BF16)
        b['innersb%d' % p_] = S("innersb%d" % p_, [32, 256], BF16)
        b['xsel_es%d' % p_] = S("xsel_es%d" % p_, [128, 2048], BF16)
        b['rsb%d' % p_] = S("rsb%d" % p_, [128, 4096], BF16)
    b['b1sb'] = S("b1sb", [32, 128], F32)
    b['scat64'] = S("scat64", [64, 8192], BF16)
    b['scatT'] = S("scatT", [128, 512], BF16)
    b['ps_stat'] = K.psum(pfx + "psstat", [1, 1024], F32)
    b['ps_mr'] = K.psum(pfx + "psmr", [128, 1024], F32)
    b['ps_a'] = K.psum(pfx + "psa", [128, 1024], F32)
    b['ps_t'] = K.psum(pfx + "pst", [128, 128], BF16)
    b['ps_i'] = K.psum(pfx + "psi", [32, 256], F32)
    b['identb'] = S("identb", [128, 128], BF16)
    b['bcc'] = S("bcc", [128, 1], F32)
    b['ties'] = S("ties", [128, 512], F32)
    b['lgsb'] = S("lgsb", [128, 512], F32)
    return b


def emit_p3_setup(K, C, b, hb):
    pfx = b['pfx']
    for ic in range(8):
        K.emit("gpsimd", lambda e, ic=ic: e.dma_start(
            out=b['wc'][:, ic * 128:(ic + 1) * 128], in_=hb["wc"][ic * 128:(ic + 1) * 128, :]),
            writes=(pfx + "wc",), dma=True)
    K.emit("gpsimd", lambda e: e.dma_start(out=b['identb'][:, :], in_=hb["ident128"]),
           writes=(pfx + "identb",), dma=True)
    K.emit("sync", lambda e: e.dma_start(out=b['bcc'][:, :], in_=hb["bc"]),
           writes=(pfx + "bcc",), dma=True)
    K.emit("sync", lambda e: e.dma_start(out=b['ties'][:, :], in_=hb["tie"]),
           writes=(pfx + "ties",), dma=True)


def emit_p3(K, C, b, xsrc, xdst, hb, scratch):
    pfx = b['pfx']
    TM = 512
    xt, xsq, xn, xnr = b['xt'], b['xsq'], b['xn'], b['xnr']
    ps_mr, ps_stat, ps_a, ps_t, ps_i = b['ps_mr'], b['ps_stat'], b['ps_a'], b['ps_t'], b['ps_i']
    kmrA, kmrB = pfx + "repA", pfx + "repB"
    ka, kt, ki = pfx + "psa", pfx + "pst", pfx + "psi"
    xsel_h, r_h, scat_h = scratch['xsel_h'], scratch['r_h'], scratch['scat_h']

    for bidx in range(16):
        K.emit("gpsimd", lambda e, bidx=bidx: e.dma_start(
            out=xt[:, :].rearrange("p (c t) -> p c t", t=TM),
            in_=xsrc[bidx].rearrange("(c p) l -> p c l", p=128)),
            writes=(pfx + "xt",), dma=True)
        emit_ln(K, C, xt, xsq, ps_stat, ps_mr, b['mvec'], TM, tag=pfx)
        for c in range(8):
            K.emit("vector", lambda e, c=c: e.tensor_tensor(
                xnr[:, c * TM:(c + 1) * TM], xt[:, c * TM:(c + 1) * TM], ps_mr[:, 0:TM], ALU.subtract),
                reads=(pfx + "xt", kmrA), writes=(pfx + "xnr",))
            K.emit("vector", lambda e, c=c: e.tensor_tensor(
                xnr[:, c * TM:(c + 1) * TM], xnr[:, c * TM:(c + 1) * TM], ps_mr[:, TM:2 * TM], ALU.mult),
                reads=(pfx + "xnr", kmrB), writes=(pfx + "xnr",))
            K.emit("scalar", lambda e, c=c: e.activation(
                xn[:, c * TM:(c + 1) * TM], xnr[:, c * TM:(c + 1) * TM], AF.Copy),
                reads=(pfx + "xnr",), writes=(pfx + "xn",))
        for ic in range(8):
            K.emit("tensor", lambda e, ic=ic: e.matmul(
                ps_a[:, 0:512], b['wc'][:, ic * 128:(ic + 1) * 128], xnr[:, ic * TM:(ic + 1) * TM],
                start=(ic == 0), stop=(ic == 7)), reads=(pfx + "wc", pfx + "xnr"), writes=(ka,))
        K.emit("vector", lambda e: e.tensor_scalar(
            b['lgsb'][:, :], ps_a[:, 0:512], b['bcc'][:, :], None, ALU.add),
            reads=(ka, pfx + "bcc"), writes=(pfx + "lgsb",))
        K.emit("vector", lambda e: e.tensor_tensor(
            b['lgsb'][:, :], b['lgsb'][:, :], b['ties'][:, :], ALU.add),
            reads=(pfx + "lgsb", pfx + "ties"), writes=(pfx + "lgsb",))
        K.emit("vector", lambda e: e.tensor_reduce(
            b['mx'][:, :].rearrange("p (g o) -> p g o", o=1),
            b['lgsb'][:, :].rearrange("p (g t) -> p g t", t=32), AX.X, ALU.max),
            reads=(pfx + "lgsb",), writes=(pfx + "mx",))
        mxb = b['mx'][:, :].rearrange("p (g o) -> p g o", o=1).broadcast_to([128, 16, 32])
        K.emit("vector", lambda e, bidx=bidx: e.tensor_tensor(
            b['maskb'][:, bidx * 512:(bidx + 1) * 512].rearrange("p (g t) -> p g t", t=32),
            b['lgsb'][:, :].rearrange("p (g t) -> p g t", t=32), mxb, ALU.is_equal),
            reads=(pfx + "lgsb", pfx + "mx"), writes=(pfx + "maskb",))
        for g in range(16):
            K.emit("tensor", lambda e, g=g, bidx=bidx: e.transpose(
                ps_t[0:32, 0:128], b['maskb'][:, bidx * 512 + g * 32: bidx * 512 + (g + 1) * 32],
                b['identb'][:, :]), reads=(pfx + "maskb", pfx + "identb"), writes=(kt,))
            K.emit("vector", lambda e, g=g: e.tensor_copy(
                b['maskT'][(g % 2) * 32:(g % 2) * 32 + 32, (g // 2) * 128:(g // 2 + 1) * 128],
                ps_t[0:32, 0:128]), reads=(kt,), writes=(pfx + "maskT",))
        for pc in range(4):
            for ic in range(8):
                K.emit("tensor", lambda e, pc=pc, ic=ic: e.transpose(
                    ps_t[:, :], xn[:, ic * TM + pc * 128: ic * TM + (pc + 1) * 128], b['identb'][:, :]),
                    reads=(pfx + "xn", pfx + "identb"), writes=(kt,))
                K.emit("vector", lambda e, pc=pc, ic=ic: e.tensor_copy(
                    b['xnT'][0:64, (2 * pc) * 1024 + ic * 128:(2 * pc) * 1024 + (ic + 1) * 128],
                    ps_t[0:64, :]), reads=(kt,), writes=(pfx + "xnT",))
                K.emit("vector", lambda e, pc=pc, ic=ic: e.tensor_copy(
                    b['xnT'][0:64, (2 * pc + 1) * 1024 + ic * 128:(2 * pc + 1) * 1024 + (ic + 1) * 128],
                    ps_t[64:128, :]), reads=(kt,), writes=(pfx + "xnT",))
        for g in range(16):
            base = (g % 2) * 32
            for nh in range(2):
                K.emit("tensor", lambda e, g=g, nh=nh, base=base: e.matmul(
                    ps_a[:, nh * 512:(nh + 1) * 512],
                    b['maskT'][base:base + 32, (g // 2) * 128:(g // 2 + 1) * 128],
                    b['xnT'][base:base + 32, (g // 2) * 1024 + nh * 512:(g // 2) * 1024 + (nh + 1) * 512],
                    start=True, stop=True), reads=(pfx + "maskT", pfx + "xnT"), writes=(ka,))
            K.emit("vector", lambda e: e.tensor_copy(b['xselsb'][:, :], ps_a[:, 0:1024]),
                   reads=(ka,), writes=(pfx + "xselsb",))
            K.emit("sync", lambda e, g=g, bidx=bidx: e.dma_start(
                out=xsel_h[:, bidx * 16 + g, :], in_=b['xselsb'][:, :]),
                reads=(pfx + "xselsb",), writes=(pfx + "h_xsel",), dma=True)

    for es in range(128):
        p_ = es % 2
        w1sb, w2sb, b2sb = b['w1sb%d' % p_], b['w2sb%d' % p_], b['b2sb%d' % p_]
        innersb, xsel_es = b['innersb%d' % p_], b['xsel_es%d' % p_]
        kw1, kw2, kb2 = pfx + "w1sb%d" % p_, pfx + "w2sb%d" % p_, pfx + "b2sb%d" % p_
        kin, kxe = pfx + "innersb%d" % p_, pfx + "xsel_es%d" % p_
        kap = ka + str(p_)
        psa = ps_a[:, (p_ % 2) * 0:1024] if False else ps_a
        K.emit("gpsimd", lambda e, es=es, w1sb=w1sb: e.dma_start(
            out=w1sb[:, :].rearrange("p (dc f) -> p dc f", f=32),
            in_=hb["w1"][es].rearrange("dc p f -> p dc f")),
            writes=(kw1,), dma=True)
        K.emit("gpsimd", lambda e, es=es, w2sb=w2sb: e.dma_start(out=w2sb[:, :], in_=hb["w2"][es]),
               writes=(kw2,), dma=True)
        K.emit("gpsimd", lambda e, es=es, b2sb=b2sb: e.dma_start(out=b2sb[0:1, :], in_=hb["b2"][es:es + 1, :]),
               writes=(kb2,), dma=True)
        if es == 0:
            K.emit("sync", lambda e: e.dma_start(out=b['b1sb'][:, :], in_=hb["b1"]),
                   writes=(pfx + "b1sb",), dma=True)
        for dc in range(8):
            K.emit("sync", lambda e, es=es, dc=dc, xsel_es=xsel_es: e.dma_start_transpose(
                xsel_es[:, dc * 256:(dc + 1) * 256], xsel_h[es, :, dc * 128:(dc + 1) * 128]),
                reads=(pfx + "h_xsel",), writes=(kxe,), dma=True)
        for dc in range(8):
            K.emit("tensor", lambda e, dc=dc, w1sb=w1sb, xsel_es=xsel_es: e.matmul(
                ps_i[0:32, (p_ % 2) * 0:256] if False else ps_i[:, :], w1sb[:, dc * 32:(dc + 1) * 32], xsel_es[:, dc * 256:(dc + 1) * 256],
                start=(dc == 0), stop=(dc == 7)), reads=(kw1, kxe), writes=(ki,))
        K.emit("scalar", lambda e, es=es, innersb=innersb: e.activation(
            innersb[:, :], ps_i[:, :], AF.Relu, bias=b['b1sb'][:, es:es + 1]),
            reads=(ki, pfx + "b1sb"), writes=(kin,))
        for gc in range(2):
            for nh in range(2):
                K.emit("tensor", lambda e, nh=nh, b2sb=b2sb: e.matmul(
                    ps_a[:, nh * 512:(nh + 1) * 512], C['ones_row'][0:1, 0:128],
                    b2sb[0:1, nh * 512:(nh + 1) * 512],
                    start=True, stop=False), reads=(kb2, "ones_row"), writes=(ka,))
                K.emit("tensor", lambda e, gc=gc, nh=nh, innersb=innersb, w2sb=w2sb: e.matmul(
                    ps_a[:, nh * 512:(nh + 1) * 512],
                    innersb[:, gc * 128:(gc + 1) * 128],
                    w2sb[:, nh * 512:(nh + 1) * 512],
                    start=False, stop=True), reads=(kin, kw2), writes=(ka,))
            K.emit("vector", lambda e: e.tensor_copy(b['xselsb'][:, :], ps_a[:, 0:1024]),
                   reads=(ka,), writes=(pfx + "xselsb",))
            K.emit("sync", lambda e, es=es, gc=gc: e.dma_start(
                out=r_h[gc * 128:(gc + 1) * 128, es, :], in_=b['xselsb'][:, :]),
                reads=(pfx + "xselsb",), writes=(pfx + "h_r",), dma=True)

    for bidx in range(16):
        for gq in range(4):
            rsb = b['rsb%d' % (gq % 2)]
            krsb = pfx + "rsb%d" % (gq % 2)
            K.emit("gpsimd", lambda e, bidx=bidx, gq=gq, rsb=rsb: e.dma_start(
                out=rsb[:, :].rearrange("p (g d) -> p g d", d=1024),
                in_=r_h[bidx * 16 + gq * 4: bidx * 16 + gq * 4 + 4, :, :].rearrange("g p d -> p g d")),
                reads=(pfx + "h_r",), writes=(krsb,), dma=True)
            for gi in range(4):
                g = gq * 4 + gi
                for nh in range(2):
                    K.emit("tensor", lambda e, g=g, gi=gi, nh=nh, bidx=bidx, rsb=rsb: e.matmul(
                        ps_a[0:32, nh * 512:(nh + 1) * 512],
                        b['maskb'][:, bidx * 512 + g * 32: bidx * 512 + (g + 1) * 32],
                        rsb[:, gi * 1024 + nh * 512: gi * 1024 + (nh + 1) * 512],
                        start=True, stop=True), reads=(pfx + "maskb", krsb), writes=(ka,))
                K.emit("vector", lambda e, g=g: e.tensor_copy(
                    b['scat64'][(g % 2) * 32:(g % 2) * 32 + 32, (g // 2) * 1024:(g // 2 + 1) * 1024],
                    ps_a[0:32, 0:1024]), reads=(ka,), writes=(pfx + "scat64",))
        K.emit("sync", lambda e, bidx=bidx: e.dma_start(
            out=scat_h[bidx].rearrange("(gh gm t) d -> (gm t) gh d", gm=2, t=32),
            in_=b['scat64'][:, :].rearrange("p (gh d) -> p gh d", d=1024)),
            reads=(pfx + "scat64",), writes=(pfx + "h_scat",), dma=True)
        K.emit("gpsimd", lambda e, bidx=bidx: e.dma_start(
            out=xt[:, :].rearrange("p (c t) -> p c t", t=TM),
            in_=xsrc[bidx].rearrange("(c p) l -> p c l", p=128)),
            writes=(pfx + "xt",), dma=True)
        for dc in range(8):
            K.emit("sync", lambda e, dc=dc, bidx=bidx: e.dma_start_transpose(
                b['scatT'][:, :], scat_h[bidx, :, dc * 128:(dc + 1) * 128]),
                reads=(pfx + "h_scat",), writes=(pfx + "scatT",), dma=True)
            K.emit("vector", lambda e, dc=dc: e.scalar_tensor_tensor(
                xsq[:, dc * TM:(dc + 1) * TM], b['scatT'][:, :], 0.5, xt[:, dc * TM:(dc + 1) * TM],
                ALU.mult, ALU.add), reads=(pfx + "scatT", pfx + "xt"), writes=(pfx + "xsq",))
            K.emit("gpsimd", lambda e, dc=dc, bidx=bidx: e.dma_start(
                out=xdst[bidx, dc * 128:(dc + 1) * 128, :], in_=xsq[:, dc * TM:(dc + 1) * TM]),
                reads=(pfx + "xsq",), dma=True)



def alloc_p4(K):
    b = {}
    S = lambda n, sh, dt: K.sbuf("a_" + n, sh, dt)
    b['xt0'] = S("xt0", [128, 2048], F32R)
    b['xt1'] = b['xt0']
    b['xsq'] = S("xsq", [128, 2048], F32R)
    b['xn0'] = S("xn0", [128, 2048], BF16)
    b['xn1'] = S("xn1", [128, 2048], BF16)
    b['xnew'] = S("xnew", [128, 2048], F32)
    b['mvec'] = S("mvec", [1, 1024], F32)
    b['wk'] = S("wk", [128, 8192], BF16)
    b['wv'] = S("wv", [128, 8192], BF16)
    b['wq'] = S("wq", [128, 8192], BF16)
    b['S'] = S("S", [128, 16384], F32)
    b['scr'] = S("scr", [128, 4096], F32)
    b['kT0'] = S("kT0", [128, 2048], BF16)
    b['vT0'] = S("vT0", [128, 2048], BF16)
    b['qT0'] = S("qT0", [128, 2048], BF16)
    b['kT1'] = S("kT1", [128, 2048], BF16)
    b['vT1'] = S("vT1", [128, 2048], BF16)
    b['qT1'] = S("qT1", [128, 2048], BF16)
    b['ret'] = S("ret", [128, 1024], F32)
    b['sil'] = S("sil", [128, 1024], F32)
    b['attn'] = S("attn", [128, 1024], BF16)
    b['ssum'] = S("ssum", [128, 128], F32)
    b['rr'] = S("rr", [128, 128], F32)
    b['ps_stat'] = K.psum("a_psstat", [1, 1024], F32)
    b['ps_mr'] = K.psum("a_psmr", [128, 1024], F32)
    b['ps_t'] = K.psum("a_pst", [128, 128], BF16)
    b['identb'] = S("identb", [128, 128], BF16)
    return b


def emit_p4_setup(K, C, b, hb):
    for nm, dst in (("wkT", b['wk']), ("wvT", b['wv']), ("wqT", b['wq'])):
        for ic in range(8):
            K.emit("gpsimd", lambda e, nm=nm, dst=dst, ic=ic: e.dma_start(
                out=dst[:, ic * 1024:(ic + 1) * 1024], in_=hb[nm][ic * 128:(ic + 1) * 128, :]),
                writes=("a_" + nm,), dma=True)
    K.emit("gpsimd", lambda e: e.dma_start(out=b['identb'][:, :], in_=hb["ident128"]),
           writes=("a_identb",), dma=True)


def emit_p4(K, C, b, xsrc, xdst):
    TM = 256
    xsq = b['xsq']
    ps_mr, ps_stat, ps_t = b['ps_mr'], b['ps_stat'], b['ps_t']
    kmrA, kmrB, kt_ = "a_repA", "a_repB", "a_pst"
    for half in range(2):
        h0 = half * 256
        K.emit("gpsimd", lambda e: e.memset(b['S'][:, :], 0.0), writes=("a_S",))
        for t in range(16):
            par = t % 2
            xt, xn = b['xt0'], b['xn%d' % par]
            K.emit("gpsimd", lambda e, t=t, h0=h0: e.dma_start(
                out=xt[:, :].rearrange("p (c t2) -> p c t2", t2=TM),
                in_=xsrc[t, :, h0:h0 + 256].rearrange("(c p) l -> p c l", p=128)),
                writes=("a_xt",), dma=True)
            emit_ln(K, C, xt, xsq, ps_stat, ps_mr, b['mvec'], TM, tag="a_")
            for c in range(8):
                K.emit("vector", lambda e, c=c: e.tensor_tensor(
                    xsq[:, c * TM:(c + 1) * TM], xt[:, c * TM:(c + 1) * TM], ps_mr[:, 0:TM], ALU.subtract),
                    reads=("a_xt", kmrA), writes=("a_xsq",))
                K.emit("vector", lambda e, c=c: e.tensor_tensor(
                    xn[:, c * TM:(c + 1) * TM], xsq[:, c * TM:(c + 1) * TM], ps_mr[:, TM:2 * TM], ALU.mult),
                    reads=("a_xsq", kmrB), writes=("a_xn%d" % par,))
            for wnm, wt, dst, dk in (("wk", b['wk'], b['kT%d' % par], "a_kT%d" % par),
                                     ("wv", b['wv'], b['vT%d' % par], "a_vT%d" % par),
                                     ("wq", b['wq'], b['qT%d' % par], "a_qT%d" % par)):
                for pc in range(2):
                    for ic in range(8):
                        for nh in range(2):
                            K.emit("tensor", lambda e, pc=pc, ic=ic, nh=nh, wt=wt: e.matmul(
                                ps_mr[:, nh * 512:(nh + 1) * 512],
                                xn[:, ic * TM + pc * 128: ic * TM + (pc + 1) * 128],
                                wt[:, ic * 1024 + nh * 512: ic * 1024 + (nh + 1) * 512],
                                start=(ic == 0), stop=(ic == 7)),
                                reads=("a_xn%d" % par, "a_" + wnm + "T"), writes=(kmrA, kmrB))
                    K.emit("vector", lambda e, pc=pc, dst=dst: e.tensor_copy(
                        dst[:, pc * 1024:(pc + 1) * 1024], ps_mr[:, 0:1024]),
                        reads=(kmrA, kmrB), writes=(dk,))
            for pc in range(2):
                K.emit("scalar", lambda e, pc=pc, par=par: e.activation(
                    b['scr'][:, 0:1024], b['kT%d' % par][:, pc * 1024:(pc + 1) * 1024], AF.Square),
                    reads=("a_kT%d" % par,), writes=("a_scr",))
                K.emit("vector", lambda e: e.tensor_reduce(
                    b['ssum'][:, :].rearrange("p (n o) -> p n o", o=1),
                    b['scr'][:, 0:1024].rearrange("p (n j) -> p n j", j=8), AX.X, ALU.add),
                    reads=("a_scr",), writes=("a_ssum",))
                K.emit("scalar", lambda e: e.activation(b['rr'][:, :], b['ssum'][:, :], AF.Ln),
                       reads=("a_ssum",), writes=("a_rr",))
                K.emit("scalar", lambda e: e.activation(b['rr'][:, :], b['rr'][:, :], AF.Exp, scale=0.5),
                       reads=("a_rr",), writes=("a_rr",))
                K.emit("vector", lambda e: e.tensor_scalar_add(b['rr'][:, :], b['rr'][:, :], 1e-6),
                       reads=("a_rr",), writes=("a_rr",))
                K.emit("vector", lambda e: e.reciprocal(b['rr'][:, :], b['rr'][:, :]),
                       reads=("a_rr",), writes=("a_rr",))
                rrb = b['rr'][:, :].rearrange("p (n o) -> p n o", o=1).broadcast_to([128, 128, 8])
                K.emit("vector", lambda e, pc=pc, par=par: e.tensor_tensor(
                    b['kT%d' % par][:, pc * 1024:(pc + 1) * 1024].rearrange("p (n j) -> p n j", j=8),
                    b['kT%d' % par][:, pc * 1024:(pc + 1) * 1024].rearrange("p (n j) -> p n j", j=8),
                    rrb, ALU.mult), reads=("a_kT%d" % par, "a_rr"), writes=("a_kT%d" % par,))
            for pc in range(2):
                Spc = b['S'][:, pc * 8192:(pc + 1) * 8192]
                knb = b['kT%d' % par][:, pc * 1024:(pc + 1) * 1024]
                for sub in range(2):
                    Ssub = Spc[:, sub * 4096:(sub + 1) * 4096].rearrange("p (n i j) -> p n i j", i=8, j=8)
                    knsub = knb[:, sub * 512:(sub + 1) * 512].rearrange("p (n j) -> p n j", j=8)
                    knsub_bi = knb[:, sub * 512:(sub + 1) * 512].rearrange("p (n o j) -> p n o j", o=1, j=8).broadcast_to([128, 64, 8, 8])
                    K.emit("vector", lambda e, Ssub=Ssub, knsub_bi=knsub_bi: e.tensor_tensor(
                        b['scr'][:, :].rearrange("p (n i j) -> p n i j", i=8, j=8), Ssub, knsub_bi, ALU.mult),
                        reads=("a_S", "a_kT%d" % par), writes=("a_scr",))
                    K.emit("vector", lambda e, sub=sub: e.tensor_reduce(
                        b['ret'][:, sub * 512:(sub + 1) * 512].rearrange("p (f o) -> p f o", o=1),
                        b['scr'][:, :].rearrange("p (f j) -> p f j", j=8), AX.X, ALU.add),
                        reads=("a_scr",), writes=("a_ret",))
                K.emit("vector", lambda e, pc=pc, par=par: e.tensor_tensor(
                    b['ret'][:, :], b['vT%d' % par][:, pc * 1024:(pc + 1) * 1024], b['ret'][:, :], ALU.subtract),
                    reads=("a_vT%d" % par, "a_ret"), writes=("a_ret",))
                for sub in range(2):
                    Ssub = Spc[:, sub * 4096:(sub + 1) * 4096].rearrange("p (n i j) -> p n i j", i=8, j=8)
                    knsub_bi = knb[:, sub * 512:(sub + 1) * 512].rearrange("p (n o j) -> p n o j", o=1, j=8).broadcast_to([128, 64, 8, 8])
                    dsub_bj = b['ret'][:, sub * 512:(sub + 1) * 512].rearrange("p (n i o) -> p n i o", o=1, i=8).broadcast_to([128, 64, 8, 8])
                    K.emit("vector", lambda e, dsub_bj=dsub_bj, knsub_bi=knsub_bi: e.tensor_tensor(
                        b['scr'][:, :].rearrange("p (n i j) -> p n i j", i=8, j=8), dsub_bj, knsub_bi, ALU.mult),
                        reads=("a_ret", "a_kT%d" % par), writes=("a_scr",))
                    K.emit("vector", lambda e, Ssub=Ssub: e.tensor_tensor(
                        Ssub, Ssub, b['scr'][:, :].rearrange("p (n i j) -> p n i j", i=8, j=8), ALU.add),
                        reads=("a_S", "a_scr"), writes=("a_S",))
                    K.emit("scalar", lambda e, Spc=Spc, sub=sub: e.activation(
                        Spc[:, sub * 4096:(sub + 1) * 4096], Spc[:, sub * 4096:(sub + 1) * 4096], AF.Tanh),
                        reads=("a_S",), writes=("a_S",))
                    qsub_bi = b['qT%d' % par][:, pc * 1024 + sub * 512: pc * 1024 + (sub + 1) * 512].rearrange(
                        "p (n o j) -> p n o j", o=1, j=8).broadcast_to([128, 64, 8, 8])
                    K.emit("vector", lambda e, Ssub=Ssub, qsub_bi=qsub_bi: e.tensor_tensor(
                        b['scr'][:, :].rearrange("p (n i j) -> p n i j", i=8, j=8), Ssub, qsub_bi, ALU.mult),
                        reads=("a_S", "a_qT%d" % par), writes=("a_scr",))
                    K.emit("vector", lambda e, sub=sub: e.tensor_reduce(
                        b['ret'][:, sub * 512:(sub + 1) * 512].rearrange("p (f o) -> p f o", o=1),
                        b['scr'][:, :].rearrange("p (f j) -> p f j", j=8), AX.X, ALU.add),
                        reads=("a_scr",), writes=("a_ret",))
                K.emit("scalar", lambda e: e.activation(b['sil'][:, :], b['ret'][:, :], AF.Silu),
                       reads=("a_ret",), writes=("a_sil",))
                K.emit("vector", lambda e: e.tensor_tensor(b['attn'][:, :], b['ret'][:, :], b['sil'][:, :], ALU.mult),
                       reads=("a_ret", "a_sil"), writes=("a_attn",))
                for oc in range(8):
                    K.emit("tensor", lambda e, oc=oc: e.transpose(
                        ps_t[:, :], b['attn'][:, oc * 128:(oc + 1) * 128], b['identb'][:, :]),
                        reads=("a_attn", "a_identb"), writes=(kt_,))
                    K.emit("vector", lambda e, oc=oc, pc=pc: e.tensor_tensor(
                        b['xnew'][:, oc * TM + pc * 128: oc * TM + (pc + 1) * 128],
                        xt[:, oc * TM + pc * 128: oc * TM + (pc + 1) * 128], ps_t[:, :], ALU.add),
                        reads=("a_xt", kt_), writes=("a_xnew",))
            K.emit("gpsimd", lambda e, t=t, h0=h0: e.dma_start(
                out=xdst[t, :, h0:h0 + 256].rearrange("(c p) l -> p c l", p=128),
                in_=b['xnew'][:, :].rearrange("p (c t2) -> p c t2", t2=TM)),
                reads=("a_xnew",), dma=True)


from concourse.bass_utils import run_bass_kernel_spmd


def build_megakernel():
    K = Prog()
    nc = K.nc
    F = F32
    xin = nc.dram_tensor("xin", [16, 1024, LH], F, kind="ExternalInput").ap()
    xout = nc.dram_tensor("xout", [16, 1024, 512], F, kind="ExternalOutput").ap()
    xA = nc.dram_tensor("xA", [16, 1024, LH], F, kind="Internal").ap()
    xB = nc.dram_tensor("xB", [16, 1024, 512], F, kind="Internal").ap()
    xC = nc.dram_tensor("xC", [16, 1024, 512], F, kind="Internal").ap()
    xD = nc.dram_tensor("xD", [16, 1024, 512], F, kind="Internal").ap()
    xE = nc.dram_tensor("xE", [16, 1024, 512], F, kind="Internal").ap()
    hb = {}
    for nm, shape in [("wT", (1024, 1024)), ("bias", (1024,)), ("sin", (1,)), ("sw", (1,)),
                      ("w_in", (1024, 4384)), ("w_op", (2048, 1024)), ("cwt", (128, 72)),
                      ("convb", (128, 18)), ("dtb", (32, 1)), ("A_log", (32, 1)),
                      ("Dcol", (128, 16)), ("nwcol", (128, 16)), ("hflag", (1, 1)),
                      ("tril", (64, 64)), ("triu", (64, 64)), ("esel", (64, 2)),
                      ("ident", (64, 64)), ("ident128", (128, 128)),
                      ("wc", (1024, 128)), ("bc", (128, 1)), ("tie", (128, 512)), ("w1", (128, 8, 128, 32)), ("w2", (128, 32, 1024)),
                      ("b1", (32, 128)), ("b2", (128, 1024)),
                      ("wkT", (1024, 1024)), ("wvT", (1024, 1024)), ("wqT", (1024, 1024))]:
        hb[nm] = nc.dram_tensor(nm, list(shape), F, kind="ExternalInput").ap()
    scratch = {
        'xsel_h': nc.dram_tensor("xsel_h", [128, 256, 1024], BF16, kind="Internal").ap(),
        'r_h': nc.dram_tensor("r_h", [256, 128, 1024], BF16, kind="Internal").ap(),
        'scat_h': nc.dram_tensor("scat_h", [16, 512, 1024], BF16, kind="Internal").ap(),
    }
    C = emit_const_setup(K)
    with K.scope():
        b0 = alloc_p0(K, "p0_")
        emit_p0_setup(K, C, b0, hb["wT"], hb["bias"], hb["sin"], hb["sw"])
        emit_p0(K, C, b0, (xin, xA), LH)
    with K.scope():
        b1 = alloc_p1(K)
        b1['hbw'] = hb["w_in"]
        emit_p1_setup(K, C, b1, hb)
        emit_p1(K, C, b1, xA, xB)
    with K.scope():
        b2 = alloc_p0(K, "p2_")
        emit_p0_setup(K, C, b2, hb["wT"], hb["bias"], hb["sin"], hb["sw"])
        emit_p0(K, C, b2, (xB, xC), 512)
    with K.scope():
        b3 = alloc_p3(K, "m_")
        emit_p3_setup(K, C, b3, hb)
        emit_p3(K, C, b3, xC, xD, hb, scratch)
    with K.scope():
        b4 = alloc_p4(K)
        emit_p4_setup(K, C, b4, hb)
        emit_p4(K, C, b4, xD, xE)
    with K.scope():
        b5 = alloc_p3(K, "n_")
        emit_p3_setup(K, C, b5, hb)
        emit_p3(K, C, b5, xE, xout, hb, scratch)
    K.finish()
    return K.build()


def kernel(**ins):
    x = np.asarray(ins['L0_attn_x'], np.float32)
    w = {
        "wT": np.ascontiguousarray(np.asarray(ins['L0_ffn_weight_fp8'], np.float32).T),
        "bias": np.asarray(ins['L0_ffn_bias'], np.float32),
        "sin": np.asarray(ins['L0_ffn_input_scale'], np.float32),
        "sw": np.asarray(ins['L0_ffn_weight_scale'], np.float32),
        "w_in": np.asarray(ins['L0_attn_in_proj_w'], np.float32),
        "w_op": np.asarray(ins['L0_attn_out_proj_w'], np.float32),
        "cwt": np.ascontiguousarray(np.asarray(ins['L0_attn_conv_w'], np.float32).reshape(4, 18, 128).transpose(2, 1, 0).reshape(128, 72)),
        "convb": np.ascontiguousarray(np.asarray(ins['L0_attn_conv_b'], np.float32).reshape(18, 128).T),
        "dtb": np.asarray(ins['L0_attn_dt_bias'], np.float32).reshape(32, 1),
        "A_log": np.asarray(ins['L0_attn_A_log'], np.float32).reshape(32, 1),
        "Dcol": np.ascontiguousarray(np.repeat(np.asarray(ins['L0_attn_D'], np.float32), 64).reshape(16, 128).T),
        "nwcol": np.ascontiguousarray(np.asarray(ins['L0_attn_norm_w'], np.float32).reshape(16, 128).T),
        "tril": np.triu(np.ones((64, 64), np.float32)),
        "triu": np.triu(np.ones((64, 64), np.float32)),
        "esel": np.ascontiguousarray(np.stack([np.eye(64, 1, k=-31, dtype=np.float32)[:, 0],
                                               np.eye(64, 1, k=-63, dtype=np.float32)[:, 0]], axis=1)),
        "ident": np.eye(64, dtype=np.float32),
        "ident128": np.eye(128, dtype=np.float32),
        "wc": np.ascontiguousarray(np.asarray(ins['L1_ffn_Wc'], np.float32).reshape(1024, 128)),
        "bc": np.ascontiguousarray(np.asarray(ins['L1_ffn_bc'], np.float32).reshape(128, 1)),
        "tie": np.ascontiguousarray(np.broadcast_to(np.tile(np.linspace(0, 1e-6, 32, dtype=np.float32), 16), (128, 512)).copy()),
        "w1": np.ascontiguousarray(np.asarray(ins['L1_ffn_W1'], np.float32).reshape(128, 8, 128, 32)),
        "w2": np.ascontiguousarray(np.asarray(ins['L1_ffn_W2'], np.float32).reshape(128, 1024, 32).transpose(0, 2, 1)),
        "b1": np.ascontiguousarray(np.asarray(ins['L1_ffn_b1'], np.float32).reshape(128, 32).T),
        "b2": np.ascontiguousarray(np.asarray(ins['L1_ffn_b2'], np.float32).reshape(128, 1024)),
        "wkT": np.ascontiguousarray(np.asarray(ins['L1_attn_W_k'], np.float32).T),
        "wvT": np.ascontiguousarray(np.asarray(ins['L1_attn_W_v'], np.float32).T),
        "wqT": np.ascontiguousarray(np.asarray(ins['L1_attn_W_q'], np.float32).T),
    }
    in_maps = []
    for c in range(8):
        lo = 512 * c - HC
        pad = max(0, -lo)
        sl = x[:, max(lo, 0):512 * (c + 1), :]
        xc = np.ascontiguousarray(np.pad(sl, ((0, 0), (pad, 0), (0, 0))).transpose(0, 2, 1))
        im = dict(w)
        im["xin"] = xc
        im["hflag"] = np.array([[0.0 if c == 0 else 1.0]], np.float32)
        in_maps.append(im)
    nc = build_megakernel()
    res = run_bass_kernel_spmd(nc, in_maps, core_ids=list(range(8)))
    out = np.concatenate([np.asarray(r["xout"]).transpose(0, 2, 1) for r in res.results], axis=1)
    return np.ascontiguousarray(out.astype(np.float32))


# version 12 (final)
# speedup vs baseline: 1.0684x; 1.0082x over previous
import sys
sys.path.insert(0, '/opt/trn_rl_repo')
import contextlib
import numpy as np
import concourse.bass as bass
import concourse.mybir as mybir

F32 = mybir.dt.float32
F32R = mybir.dt.float32r
BF16 = mybir.dt.bfloat16
FP8 = mybir.dt.float8e4
AF = mybir.ActivationFunctionType
ALU = mybir.AluOpType
AX = mybir.AxisListType

ENGINES = ("sync", "vector", "scalar", "tensor", "gpsimd")

H = 256
HC = H + 4
NPOS = 512
LH = NPOS + HC
Q = 64
EPS = 1e-5


class Prog:
    def __init__(self, name="mk"):
        self.nc = bass.Bass("TRN2", target_bir_lowering=False)
        self.ops = {e: [] for e in ENGINES}
        self.cnt = {e: 0 for e in ENGINES}
        self.sems = {}
        self.last_writer = {}
        self.readers = {}
        self.waited = {e: {e2: 0 for e2 in ENGINES} for e in ENGINES}
        self.stack = contextlib.ExitStack()
        self.tiles = {}
        self._tile_id = 0

    def sbuf(self, name, shape, dtype):
        t = self.stack.enter_context(self.nc.sbuf_tensor(name, list(shape), dtype))
        self.tiles[name] = t
        return t

    def psum(self, name, shape, dtype=F32):
        t = self.stack.enter_context(self.nc.psum_tensor(name, list(shape), dtype))
        self.tiles[name] = t
        return t

    def _wait(self, eng, dep_eng, dep_cnt):
        if self.waited[eng][dep_eng] >= dep_cnt:
            return
        self.waited[eng][dep_eng] = dep_cnt
        sems = self.sems

        def w(e, dep_eng=dep_eng, dep_cnt=dep_cnt):
            e.wait_ge(sems[dep_eng], dep_cnt)
        self.ops[eng].append(w)

    def emit(self, eng, fn, reads=(), writes=(), dma=False, inc=None):
        for r in reads:
            lw = self.last_writer.get(r)
            if lw is not None:
                dep_eng, dep_cnt = lw[0], lw[1]
                if dep_eng == eng and eng == "tensor":
                    pass
                elif dep_eng == eng and not dma and self.cnt[eng] == dep_cnt and False:
                    pass
                else:
                    if not (dep_eng == eng and self.waited[eng][eng] >= dep_cnt):
                        self._wait(eng, dep_eng, dep_cnt)
        for w_ in writes:
            rd = self.readers.get(w_)
            if rd:
                for dep_eng, dep_cnt in rd.items():
                    if dep_eng == eng and eng == "tensor":
                        continue
                    self._wait(eng, dep_eng, dep_cnt)
            lw = self.last_writer.get(w_)
            if lw is not None and not (lw[0] == eng == "tensor"):
                if dma and len(lw) > 2 and lw[2]:
                    pass
                else:
                    self._wait(eng, lw[0], lw[1])
        n = inc if inc is not None else (16 if dma else 1)
        sems = self.sems
        cnt_after = self.cnt[eng] + n
        my = eng

        def run(e, fn=fn, n=n, my=my):
            ins = fn(e)
            ins.then_inc(sems[my], n)
        self.ops[eng].append(run)
        self.cnt[eng] = cnt_after
        if dma:
            cnt_after = self.cnt[eng]
        for w_ in writes:
            self.last_writer[w_] = (eng, cnt_after, dma)
            self.readers.pop(w_, None)
        for r in reads:
            self.readers.setdefault(r, {})
            cur = self.readers[r].get(eng, 0)
            self.readers[r][eng] = max(cur, cnt_after)
        return cnt_after

    def raw(self, eng, fn):
        self.ops[eng].append(fn)

    def barrier(self):
        snap = dict(self.cnt)
        for e in ENGINES:
            for e2 in ENGINES:
                if e2 != e and snap[e2] > 0:
                    self._wait(e, e2, snap[e2])

    @contextlib.contextmanager
    def scope(self):
        outer = self.stack
        inner = contextlib.ExitStack()
        self.stack = inner
        try:
            yield
        finally:
            self.barrier()
            self.stack = outer
            inner.close()

    def build(self):
        nc = self.nc
        for e in ENGINES:
            self.sems[e] = self.stack.enter_context(nc.semaphore("sem_" + e))
        blk = self.stack.enter_context(nc.Block())
        for e in ENGINES:
            ops = self.ops[e]

            def body(engobj, ops=ops):
                for f in ops:
                    f(engobj)
            getattr(blk, e)(body)
        self.stack.close()
        return nc

    def finish(self):
        for e in ENGINES:
            if e == "gpsimd":
                continue
            c = self.cnt[e]
            if c:
                self._wait("gpsimd", e, c)
        c = self.cnt["gpsimd"]
        if c:
            sems = self.sems

            def w(eng, c=c):
                eng.wait_ge(sems["gpsimd"], c)
            self.ops["gpsimd"].append(w)



def ceil_div(a, b):
    return (a + b - 1) // b


def emit_const_setup(K):
    nc = K.nc
    ones_row = K.sbuf("ones_row", [1, 512], BF16)
    ones_col = K.sbuf("ones_col", [128, 1], F32R)
    ones_col32 = K.sbuf("ones_col32", [128, 1], F32)
    onesb_col = K.sbuf("onesb_col", [128, 1], BF16)
    ones_row128 = K.sbuf("ones_row128", [1, 128], F32)
    K.emit("gpsimd", lambda e: e.memset(ones_row[:, :], 1.0), writes=("ones_row",))
    K.emit("gpsimd", lambda e: e.memset(ones_col32[:, :], 1.0), writes=("ones_col32",))
    K.emit("vector", lambda e: e.tensor_copy(ones_col[:, :], ones_col32[:, :]), reads=("ones_col32",), writes=("ones_col",))
    K.emit("gpsimd", lambda e: e.memset(onesb_col[:, :], 1.0), writes=("onesb_col",))
    K.emit("gpsimd", lambda e: e.memset(ones_row128[:, :], 1.0), writes=("ones_row128",))
    cpool = K.sbuf("cpool", [128, 8], F32)
    for i, v in enumerate((1e-5, -0.5, 1e-6, 0.5)):
        K.emit("gpsimd", lambda e, i=i, v=v: e.memset(cpool[:, i:i + 1], v), writes=("cpool",))
        K.nc.const_aps.register(F32, v, cpool[:, i:i + 1])
    return dict(ones_row=ones_row, ones_col=ones_col, onesb_col=onesb_col, ones_row128=ones_row128,
                cpool=cpool)


def emit_ln(K, C, xt, xsq, stat_ps, rep_ps, mvec, T, extra_scale_ap=None,
            tag=""):
    nc = K.nc
    TM = xt.shape[1] // 8
    kx, ksq, kst = tag + "xt", tag + "xsq", tag + "stat"
    kmv = tag + "mvec"
    krepA, krepB = tag + "repA", tag + "repB"
    mA, mB = mvec[0:1, 0:512], mvec[0:1, 512:1024]
    K.emit("scalar", lambda e: e.activation(xsq[:, 0:8 * TM], xt[:, 0:8 * TM], AF.Square),
           reads=(kx,), writes=(ksq,))
    oc = C["ones_col"]
    for c in range(8):
        K.emit("tensor", lambda e, c=c: e.matmul(
            stat_ps[0:1, 0:T], oc[:, :], xt[:, c * TM:c * TM + T],
            start=(c == 0), stop=(c == 7)), reads=(kx, "ones_col"), writes=(kst,))
    for c in range(8):
        K.emit("tensor", lambda e, c=c: e.matmul(
            stat_ps[0:1, 512:512 + T], oc[:, :], xsq[:, c * TM:c * TM + T],
            start=(c == 0), stop=(c == 7)), reads=(ksq, "ones_col"), writes=(kst,))
    K.emit("vector", lambda e: e.tensor_scalar_mul(mA[:, 0:T], stat_ps[0:1, 0:T], 1.0 / 1024.0),
           reads=(kst,), writes=(kmv + "A",))
    K.emit("vector", lambda e: e.tensor_scalar_mul(mB[:, 0:T], stat_ps[0:1, 512:512 + T], 1.0 / 1024.0),
           reads=(kst,), writes=(kmv + "B",))
    orow = C["ones_row128"]
    K.emit("tensor", lambda e: e.matmul(rep_ps[:, 0:T], orow[0:1, :], mA[:, 0:T], start=True, stop=True),
           reads=(kmv + "A", "ones_row128"), writes=(krepA,))
    K.emit("vector", lambda e: e.tensor_tensor(stat_ps[0:1, 0:T], mA[:, 0:T], mA[:, 0:T], ALU.mult),
           reads=(kmv + "A",), writes=(kst,))
    K.emit("vector", lambda e: e.tensor_tensor(mB[:, 0:T], mB[:, 0:T], stat_ps[0:1, 0:T], ALU.subtract),
           reads=(kmv + "B", kst), writes=(kmv + "B",))
    K.emit("scalar", lambda e: e.activation(mA[:, 0:T], mB[:, 0:T], AF.Ln, bias=EPS),
           reads=(kmv + "B",), writes=(kmv + "A",))
    K.emit("scalar", lambda e: e.activation(mA[:, 0:T], mA[:, 0:T], AF.Exp, scale=-0.5),
           reads=(kmv + "A",), writes=(kmv + "A",))
    if extra_scale_ap is not None:
        K.emit("vector", lambda e: e.tensor_tensor(
            mA[:, 0:T], mA[:, 0:T], extra_scale_ap.broadcast_to([1, T]), ALU.mult),
            reads=(kmv + "A", "recip_s"), writes=(kmv + "A",))
    K.emit("tensor", lambda e: e.matmul(rep_ps[:, TM:TM + T], orow[0:1, :], mA[:, 0:T], start=True, stop=True),
           reads=(kmv + "A", "ones_row128"), writes=(krepB,))
    return krepA


def alloc_p0(K, pfx="p0_"):
    b = {}
    b['xt'] = K.sbuf(pfx + "xt", [128, 4096], F32R)
    b['xw'] = K.sbuf(pfx + "xw", [128, 4096], F32)
    b['xsq'] = K.sbuf(pfx + "xsq", [128, 4096], F32R)
    b['carve32'] = K.sbuf(pfx + "carve", [128, 1024], F32)
    b['xb'] = K.sbuf(pfx + "xb", [128, 4096], BF16)
    b['xnew'] = K.sbuf(pfx + "xnew", [128, 4096], F32)
    b['w0b'] = K.sbuf(pfx + "w0b", [128, 8192], BF16)
    b['biasb'] = K.sbuf(pfx + "biasb", [1, 1024], BF16)
    b['bias32'] = K.sbuf(pfx + "bias32", [1, 1024], F32)
    b['sc'] = K.sbuf(pfx + "sc", [1, 8], F32)
    b['c_ap'] = K.sbuf(pfx + "cap", [128, 1], F32)
    b['mvec'] = K.sbuf(pfx + "mvec", [1, 1024], F32)
    b['stat_ps'] = K.psum(pfx + "stat", [1, 1024], F32)
    b['rep_ps'] = K.psum(pfx + "rep", [128, 1024], F32)
    b['out_ps'] = K.psum(pfx + "out", [128, 1024], F32)
    return b


def emit_p0_setup(K, C, b, wT_h, bias_h, sin_h, sw_h):
    nc = K.nc
    w0b, biasb, bias32, sc, c_ap = b['w0b'], b['biasb'], b['bias32'], b['sc'], b['c_ap']
    for ic in range(8):
        K.emit("gpsimd", lambda e, ic=ic: e.dma_start(
            out=w0b[:, ic * 1024:(ic + 1) * 1024], in_=wT_h[ic * 128:(ic + 1) * 128, :]),
            writes=("w0b",), dma=True)
    K.emit("gpsimd", lambda e: e.dma_start(out=bias32[0:1, :], in_=bias_h[None, :]),
           writes=("bias32",), dma=True)
    K.emit("gpsimd", lambda e: e.dma_start(out=sc[0:1, 0:1], in_=sin_h[None, :]),
           writes=("sc",), dma=True)
    K.emit("gpsimd", lambda e: e.dma_start(out=sc[0:1, 1:2], in_=sw_h[None, :]),
           writes=("sc",), dma=True)
    K.emit("vector", lambda e: e.tensor_tensor(sc[0:1, 2:3], sc[0:1, 0:1], sc[0:1, 1:2], ALU.mult),
           reads=("sc",), writes=("sc2",))
    K.emit("vector", lambda e: e.reciprocal(sc[0:1, 3:4], sc[0:1, 2:3]),
           reads=("sc2",), writes=("sc3",))
    K.emit("vector", lambda e: e.tensor_scalar_mul(sc[0:1, 4:5], sc[0:1, 2:3], 0.5),
           reads=("sc2",), writes=("sc4",))
    K.emit("vector", lambda e: e.reciprocal(sc[0:1, 5:6], sc[0:1, 0:1]),
           reads=("sc",), writes=("recip_s",))
    K.emit("vector", lambda e: e.tensor_tensor(biasb[0:1, :], bias32[0:1, :],
                                               sc[0:1, 3:4].broadcast_to([1, 1024]), ALU.mult),
           reads=("bias32", "sc3"), writes=("biasb",))
    K.emit("tensor", lambda e: e.matmul(b['rep_ps'][:, 0:1], C['ones_row128'][0:1, :], sc[0:1, 4:5],
                                        start=True, stop=True),
           reads=("sc4", "ones_row128"), writes=("p0_repA",))
    K.emit("vector", lambda e: e.tensor_copy(c_ap[:, :], b['rep_ps'][:, 0:1]),
           reads=("p0_repA",), writes=("c_ap",))


def emit_p0_tile(K, C, b, xres, bidx, t0, T, tag):
    nc = K.nc
    xsrc, xdst = xres if isinstance(xres, tuple) else (xres, xres)
    TM = 512
    xt, xw, xsq, xb, xnew = b['xt'], b['xw'], b['xsq'], b['xb'], b['xnew']
    carve = b['carve32'].ap().bitcast(FP8)
    kx = "p0_xt"
    if T == TM:
        K.emit("gpsimd", lambda e, t0=t0: e.dma_start(
            out=xt[:, :].rearrange("p (c t) -> p c t", t=TM),
            in_=xsrc[bidx, :, t0:t0 + TM].rearrange("(c p) l -> p c l", p=128)),
            writes=(kx,), dma=True)
    else:
        for c in range(8):
            K.emit("gpsimd", lambda e, c=c: e.dma_start(
                out=xt[:, c * TM:c * TM + T], in_=xsrc[bidx, c * 128:(c + 1) * 128, t0:t0 + T]),
                writes=(kx,), dma=True)
    krep = emit_ln(K, C, xt, xsq, b['stat_ps'], b['rep_ps'], b['mvec'], T,
                   extra_scale_ap=b['sc'][0:1, 5:6], tag="p0_")
    rep = b['rep_ps']
    for c in range(8):
        K.emit("vector", lambda e, c=c: e.tensor_tensor(xw[:, c * TM:c * TM + T], xt[:, c * TM:c * TM + T],
                                                        rep[:, 0:T], ALU.subtract),
               reads=(kx, "p0_repA"), writes=("p0_xw",))
        K.emit("vector", lambda e, c=c: e.tensor_tensor(xw[:, c * TM:c * TM + T], xw[:, c * TM:c * TM + T],
                                                        rep[:, TM:TM + T], ALU.mult),
               reads=("p0_xw", "p0_repB"), writes=("p0_xw",))
        K.emit("vector", lambda e, c=c: e.tensor_scalar(carve[:, c * TM:c * TM + T], xw[:, c * TM:c * TM + T],
                                                        448.0, -448.0, ALU.min, ALU.max),
               reads=("p0_xw",), writes=("p0_carve",))
        K.emit("scalar", lambda e, c=c: e.activation(xb[:, c * TM:c * TM + T], carve[:, c * TM:c * TM + T], AF.Copy),
               reads=("p0_carve",), writes=("p0_xb",))
    out_ps = b['out_ps']
    for oc in range(8):
        pk = "p0_out%d" % (oc % 2)
        K.emit("tensor", lambda e, oc=oc: e.matmul(
            out_ps[:, (oc % 2) * 512:(oc % 2) * 512 + T],
            b['biasb'][0:1, oc * 128:(oc + 1) * 128], C['ones_row'][0:1, 0:T],
            start=True, stop=False), reads=("biasb", "ones_row"), writes=(pk,))
        for ic in range(8):
            K.emit("tensor", lambda e, oc=oc, ic=ic: e.matmul(
                out_ps[:, (oc % 2) * 512:(oc % 2) * 512 + T],
                b['w0b'][:, ic * 1024 + oc * 128:ic * 1024 + (oc + 1) * 128],
                xb[:, ic * TM:ic * TM + T],
                start=False, stop=(ic == 7)), reads=("w0b", "p0_xb"), writes=(pk,))
        K.emit("vector", lambda e, oc=oc: e.scalar_tensor_tensor(
            xnew[:, oc * TM:oc * TM + T], out_ps[:, (oc % 2) * 512:(oc % 2) * 512 + T],
            b['c_ap'][:, :], xt[:, oc * TM:oc * TM + T], ALU.mult, ALU.add),
            reads=(pk, "c_ap", kx), writes=("p0_xnew%d" % oc,))
        K.emit("sync", lambda e, oc=oc: e.dma_start(
            out=xdst[bidx, oc * 128:(oc + 1) * 128, t0:t0 + T], in_=xnew[:, oc * TM:oc * TM + T]),
            reads=("p0_xnew%d" % oc,), dma=True, writes=())


def emit_p0(K, C, b, xres, L):
    tiles = [(0, 512), (512, L - 512)] if L > 512 else [(0, L)]
    assert all(T % 2 == 0 for _, T in tiles)
    for bidx in range(16):
        for (t0, T) in tiles:
            emit_p0_tile(K, C, b, xres, bidx, t0, T, tag="p0_%d_%d_" % (bidx, t0))



def alloc_p1(K):
    b = {}
    b['xt'] = K.sbuf("q_xt", [128, 4096], F32R)
    b['xsq'] = K.sbuf("q_xsq", [128, 4096], F32R)
    b['dts'] = K.sbuf("q_dts2", [32, 772], F32)

    b['mvec'] = K.sbuf("q_mvec", [1, 1024], F32)
    b['w_in0'] = K.sbuf("q_win0", [128, 2048], BF16)
    b['w_in1'] = K.sbuf("q_win1", [128, 2048], BF16)
    b['w_op'] = K.sbuf("q_wop", [128, 2048], BF16)
    b['w_op1'] = b['w_op']
    b['xbc_pre'] = K.sbuf("q_xbcp", [128, 18 * 772], BF16)
    b['xbc2'] = K.sbuf("q_xbc2", [128, 18 * 768], BF16)
    b['dtraw'] = K.sbuf("q_dtraw", [32, 772], F32)

    b['sdt'] = K.sbuf("q_sdt", [64, 772], F32)
    b['sposatmp'] = K.sbuf("q_sposatmp", [64, 768], F32)
    b['spos'] = b['sdt'][:, 0:384]
    b['atmp'] = b['sdt'][:, 384:768]

    b['s_all'] = K.sbuf("q_sall", [64, 12 * 32], F32)
    b['bmid'] = K.sbuf("q_bmid", [1, 384], F32)
    b['slast'] = K.sbuf("q_slast", [1, 384], F32)
    b['wbase'] = K.sbuf("q_wbase", [1, 384], F32)
    b['g1row'] = K.sbuf("q_g1row", [1, 384], F32)

    b['ubf0'] = K.sbuf("q_ubf0", [64, 96], BF16)
    b['ubf1'] = K.sbuf("q_ubf1", [64, 96], BF16)
    b['uvw0'] = K.sbuf("q_uvw0", [64, 96], F32)
    b['uvw1'] = K.sbuf("q_uvw1", [64, 96], F32)
    b['XT0'] = K.sbuf("q_XT0", [128, 2048], BF16)
    b['BT'] = K.sbuf("q_BT", [128, 768], BF16)
    b['Xu0'] = K.sbuf("q_Xu0", [64, 2048], BF16)
    b['Xu1'] = K.sbuf("q_Xu1", [64, 2048], BF16)
    b['Xw0'] = K.sbuf("q_Xw0", [64, 2048], BF16)
    b['Xw1'] = K.sbuf("q_Xw1", [64, 2048], BF16)
    b['gm0'] = K.sbuf("q_gm0", [64, 64], BF16)
    b['gm1'] = K.sbuf("q_gm1", [64, 64], BF16)
    b['btc0'] = K.sbuf("q_btc0", [64, 128], BF16)
    b['btc1'] = K.sbuf("q_btc1", [64, 128], BF16)
    b['xtc0'] = K.sbuf("q_xtc0", [64, 2048], BF16)
    b['xtc1'] = b['xtc0']
    b['T'] = K.sbuf("q_T", [128, 2048], BF16)
    b['Yf'] = K.sbuf("q_Yf", [128, 8192], BF16)
    b['xn'] = b['Yf']
    b['zg'] = K.sbuf("q_zg", [128, 8192], BF16)
    b['scr1'] = K.sbuf("q_scr1", [128, 512], BF16)
    b['convd'] = K.sbuf("q_convd", [128, 72 * 128], BF16)
    b['convb'] = K.sbuf("q_convb", [128, 18], F32)
    b['cwt'] = K.sbuf("q_cwt", [128, 72], F32)
    b['dtb'] = K.sbuf("q_dtb", [32, 1], F32)
    b['negA'] = K.sbuf("q_negA", [32, 2], F32)
    b['Dcol'] = K.sbuf("q_Dcol", [128, 16], F32)
    b['nwcol'] = K.sbuf("q_nw", [128, 16], F32)
    b['flagc'] = K.sbuf("q_flagc", [128, 1], F32)
    b['tril'] = K.sbuf("q_tril", [64, 64], F32)
    b['triu_m'] = K.sbuf("q_trium", [64, 64], F32)
    b['esel'] = K.sbuf("q_esel", [64, 2], F32)
    b['e31'] = b['esel'][:, 0:1]
    b['e63'] = b['esel'][:, 1:2]
    b['ident'] = K.sbuf("q_ident", [64, 64], F32)
    b['ps_stat'] = K.psum("q_psstat", [1, 1024], F32)
    b['ps_mr'] = K.psum("q_psmr", [128, 1024], F32)
    b['ps_big'] = K.psum("q_psbig", [128, 2048], F32)
    return b


def emit_p1_setup(K, C, b, hb):
    nc = K.nc
    for nm, dst in (("cwt", b['cwt']), ("dtb", b['dtb']), ("Dcol", b['Dcol']),
                    ("nwcol", b['nwcol']), ("tril", b['tril']),
                    ("triu", b['triu_m']), ("esel", b['esel']),
                    ("ident", b['ident'])):
        K.emit("sync", lambda e, nm=nm, dst=dst: e.dma_start(out=dst[:, :], in_=hb[nm]),
               writes=("q_" + nm,), dma=True)
    K.emit("sync", lambda e: e.dma_start(out=b['convb'][:, :], in_=hb["convb"]),
           writes=("q_convb",), dma=True)
    K.emit("sync", lambda e: e.dma_start(out=b['flagc'][0:1, 0:1], in_=hb["hflag"]),
           writes=("q_flagc",), dma=True)
    K.emit("tensor", lambda e: e.matmul(b['ps_mr'][:, 0:1], C['ones_row128'][0:1, :], b['flagc'][0:1, 0:1],
                                        start=True, stop=True), reads=("q_flagc", "ones_row128"), writes=("q_repA",))
    K.emit("vector", lambda e: e.tensor_copy(b['flagc'][:, :], b['ps_mr'][:, 0:1]),
           reads=("q_repA",), writes=("q_flagc",))
    K.emit("gpsimd", lambda e: e.dma_start(out=b['negA'][:, 0:1], in_=hb["A_log"]),
           writes=("q_negA",), dma=True)
    K.emit("scalar", lambda e: e.activation(b['negA'][:, 1:2], b['negA'][:, 0:1], AF.Exp),
           reads=("q_negA",), writes=("q_negA1",))
    K.emit("vector", lambda e: e.tensor_scalar_mul(b['negA'][:, 0:1], b['negA'][:, 1:2], -1.0),
           reads=("q_negA1",), writes=("q_negA0",))
    b['hb_wop'] = hb["w_op"]
    identb = b['convd'][:, 0:128]
    K.emit("gpsimd", lambda e: e.dma_start(out=identb, in_=hb["ident128"]),
           writes=("q_convd",), dma=True)
    pairs = [(k, ch) for k in range(4) for ch in range(18)]
    pairs.remove((0, 0))
    pairs.append((0, 0))
    for k, ch in pairs:
        K.emit("vector", lambda e, k=k, ch=ch: e.tensor_scalar(
            b['convd'][:, (k * 18 + ch) * 128:(k * 18 + ch + 1) * 128],
            b['convd'][:, 0:128], b['cwt'][:, ch * 4 + k:ch * 4 + k + 1], None, ALU.mult),
            reads=("q_convd", "q_cwt"), writes=("q_convd",))
    b['hbw_bf'] = hb["w_in_bf"]
    b['hb_wop_bf'] = hb["w_op_bf"]
    groups = [(i * 256, 256) for i in range(17)] + [(4352, 32)]
    for gi, (g0, gw) in enumerate(groups):
        wbuf = b['w_in%d' % (gi % 2)]
        wkey = "q_win%d" % (gi % 2)
        K.emit("gpsimd", lambda e, g0=g0, gw=gw, wbuf=wbuf: e.dma_start(
            out=wbuf[:, 0:8 * gw].rearrange("p (ic c) -> p ic c", c=gw),
            in_=hb["w_in"][:, g0:g0 + gw].rearrange("(ic p) c -> p ic c", p=128)),
            writes=(wkey,), dma=True)
        K.emit("sync", lambda e, g0=g0, gw=gw, wbuf=wbuf: e.dma_start(
            out=b['hbw_bf'][:, g0:g0 + gw].rearrange("(ic p) c -> p ic c", p=128),
            in_=wbuf[:, 0:8 * gw].rearrange("p (ic c) -> p ic c", c=gw)),
            reads=(wkey,), writes=("h_winbf",), dma=True)
    for oc in range(8):
        wbuf = b['w_op']
        wkey = "q_wop"
        K.emit("gpsimd", lambda e, oc=oc, wbuf=wbuf: e.dma_start(
            out=wbuf[:, :].rearrange("p (kc c) -> p kc c", c=128),
            in_=hb["w_op"][:, oc * 128:(oc + 1) * 128].rearrange("(kc p) c -> p kc c", p=128)),
            writes=(wkey,), dma=True)
        K.emit("sync", lambda e, oc=oc, wbuf=wbuf: e.dma_start(
            out=b['hb_wop_bf'][:, oc * 128:(oc + 1) * 128].rearrange("(kc p) c -> p kc c", p=128),
            in_=wbuf[:, :].rearrange("p (kc c) -> p kc c", c=128)),
            reads=(wkey,), writes=("h_wopbf",), dma=True)


def emit_p1_batch(K, C, b, xA, xB, bidx):
    nc = K.nc
    TM = 512
    xt, xsq, xn = b['xt'], b['xsq'], b['xn']
    ps_mr, ps_stat, ps_big = b['ps_mr'], b['ps_stat'], b['ps_big']
    hbw = b['hbw']
    kbig = "q_big"
    K.emit("gpsimd", lambda e: e.memset(b['T'][:, :], 0.0), writes=("q_T",))

    tiles = [(0, 512), (512, 260)]
    for ti, (t0, T) in enumerate(tiles):
        if T == TM:
            K.emit("gpsimd", lambda e, t0=t0: e.dma_start(
                out=xt[:, :].rearrange("p (c t) -> p c t", t=TM),
                in_=xA[bidx, :, t0:t0 + TM].rearrange("(c p) l -> p c l", p=128)),
                writes=("q_xt",), dma=True)
        else:
            for c in range(8):
                K.emit("gpsimd", lambda e, c=c, t0=t0, T=T: e.dma_start(
                    out=xt[:, c * TM:c * TM + T], in_=xA[bidx, c * 128:(c + 1) * 128, t0:t0 + T]),
                    writes=("q_xt",), dma=True)
        emit_ln(K, C, xt, xsq, ps_stat, ps_mr, b['mvec'], T, tag="q_")
        for c in range(8):
            K.emit("vector", lambda e, c=c, T=T: e.tensor_tensor(
                xsq[:, c * TM:c * TM + T], xt[:, c * TM:c * TM + T], ps_mr[:, 0:T], ALU.subtract),
                reads=("q_xt", "q_repA"), writes=("q_xsq",))
            K.emit("vector", lambda e, c=c, T=T, t0=t0: e.tensor_tensor(
                xn[:, c * 772 + t0: c * 772 + t0 + T], xsq[:, c * TM:c * TM + T], ps_mr[:, TM:TM + T], ALU.mult),
                reads=("q_xsq", "q_repB"), writes=("q_Yf",))
    groups = [(i * 256, 256) for i in range(17)] + [(4352, 32)]
    for gi, (g0, gw) in enumerate(groups):
        wbuf = b['w_in%d' % (gi % 2)]
        wkey = "q_win%d" % (gi % 2)
        K.emit("sync" if gi % 2 == 0 else "gpsimd", lambda e, g0=g0, gw=gw, wbuf=wbuf: e.dma_start(
            out=wbuf[:, 0:8 * gw].rearrange("p (ic c) -> p ic c", c=gw),
            in_=b['hbw_bf'][:, g0:g0 + gw].rearrange("(ic p) c -> p ic c", p=128)),
            reads=("h_winbf",), writes=(wkey,), dma=True)
        nj = (gw + 127) // 128
        for j in range(nj):
            fc_abs = (g0 + j * 128) // 128
            M = min(128, gw - j * 128)
            for ti, (t0, T) in enumerate(tiles):
                half = (2 * fc_abs + ti) % 2
                pk = "q_rep%s" % ("A" if half == 0 else "B")
                for ic in range(8):
                    K.emit("tensor", lambda e, ic=ic, j=j, M=M, half=half, gw=gw, T=T, t0=t0, wbuf=wbuf: e.matmul(
                        ps_mr[0:M, half * 512:half * 512 + T],
                        wbuf[:, ic * gw + j * 128: ic * gw + j * 128 + M],
                        xn[:, ic * 772 + t0: ic * 772 + t0 + T],
                        start=(ic == 0), stop=(ic == 7)),
                        reads=(wkey, "q_Yf"), writes=(pk,))
                if fc_abs < 16:
                    cc = fc_abs
                    if ti == 0:
                        K.emit("scalar", lambda e, cc=cc, half=half: e.activation(
                            b['zg'][:, cc * 512: cc * 512 + 252],
                            ps_mr[:, half * 512 + 260: half * 512 + 512], AF.Copy),
                            reads=(pk,), writes=("q_zg",))
                    else:
                        K.emit("scalar", lambda e, cc=cc, half=half: e.activation(
                            b['zg'][:, cc * 512 + 252: cc * 512 + 512],
                            ps_mr[:, half * 512: half * 512 + 260], AF.Copy),
                            reads=(pk,), writes=("q_zg",))
                elif fc_abs < 34:
                    ch = fc_abs - 16
                    if ti == 0:
                        K.emit("scalar", lambda e, ch=ch, half=half: e.activation(
                            b['xbc_pre'][:, ch * 772: ch * 772 + 512],
                            ps_mr[:, half * 512: half * 512 + 512], AF.Copy),
                            reads=(pk,), writes=("q_xbcp",))
                    else:
                        K.emit("scalar", lambda e, ch=ch, half=half: e.activation(
                            b['xbc_pre'][:, ch * 772 + 512: ch * 772 + 772],
                            ps_mr[:, half * 512: half * 512 + 260], AF.Copy),
                            reads=(pk,), writes=("q_xbcp",))
                else:
                    K.emit("vector", lambda e, half=half, t0=t0, T=T: e.tensor_copy(
                        b['dtraw'][0:32, t0:t0 + T], ps_mr[0:32, half * 512: half * 512 + T]),
                        reads=(pk,), writes=("q_dtraw",))
    for ti in range(2):
        if ti == 0:
            o0, OT, rs = 0, 508, 1
        else:
            o0, OT, rs = 508, 260, 509
        for ch in range(18):
            for k in range(4):
                K.emit("tensor", lambda e, ch=ch, k=k, OT=OT, rs=rs: e.matmul(
                    ps_mr[:, 0:OT], b['convd'][:, ((3 - k) * 18 + ch) * 128:((3 - k) * 18 + ch + 1) * 128],
                    b['xbc_pre'][:, ch * 772 + rs + k: ch * 772 + rs + k + OT],
                    start=(k == 0), stop=(k == 3)), reads=("q_convd", "q_xbcp"), writes=("q_repA",))
            K.emit("scalar", lambda e, ch=ch, o0=o0, OT=OT: e.activation(
                b['xbc2'][:, ch * 768 + o0: ch * 768 + o0 + OT], ps_mr[:, 0:OT], AF.Silu,
                bias=b['convb'][:, ch:ch + 1]),
                reads=("q_repA", "q_convb"), writes=("q_xbc2",))
    dts = b['dts']
    dA = dts[0:32, :]
    dR = b['dtraw'][0:32, :]
    K.emit("vector", lambda e: e.tensor_scalar(dA, dR, b['dtb'][:, :], None, ALU.add),
           reads=("q_dtraw", "q_dtb"), writes=("q_xsq",))
    K.emit("scalar", lambda e: e.activation(dR, dA, AF.Abs),
           reads=("q_xsq",), writes=("q_dtraw",))
    K.emit("scalar", lambda e: e.activation(dR, dR, AF.Exp, scale=-1.0),
           reads=("q_dtraw",), writes=("q_dtraw",))
    K.emit("scalar", lambda e: e.activation(dR, dR, AF.Ln, bias=1.0),
           reads=("q_dtraw",), writes=("q_dtraw",))
    K.emit("vector", lambda e: e.scalar_tensor_tensor(dA, dA, 0.0, dR, ALU.max, ALU.add),
           reads=("q_xsq", "q_dtraw"), writes=("q_xsq",))
    K.emit("vector", lambda e: e.tensor_copy(b['sdt'][32:64, 0:772], dts[0:32, :]),
           reads=("q_xsq",), writes=("q_sdt",))
    K.emit("vector", lambda e: e.tensor_scalar(b['sdt'][32:64, 0:260], b['sdt'][32:64, 0:260],
                                               b['flagc'][32:64, :], None, ALU.mult),
           reads=("q_sdt", "q_flagc"), writes=("q_sdt",))
    K.emit("vector", lambda e: e.tensor_scalar(b['sdt'][0:32, 0:772], dts[0:32, :], b['negA'][:, 0:1], None, ALU.mult),
           reads=("q_xsq", "q_negA0"), writes=("q_sdt",))
    for w in range(3):
        for i in range(4):
            c = w * 4 + i
            K.emit("tensor", lambda e, c=c, i=i: e.transpose(
                ps_big[0:64, i * 512: i * 512 + 64],
                b['sdt'][:, 4 + 64 * c: 68 + 64 * c], b['ident'][:, :]),
                reads=("q_sdt", "q_ident"), writes=(kbig,))
        for i in range(4):
            c = w * 4 + i
            K.emit("vector", lambda e, c=c, i=i: e.tensor_copy(
                b['sposatmp'][:, 384 + c * 32: 384 + (c + 1) * 32], ps_big[0:64, i * 512: i * 512 + 32]),
                reads=(kbig,), writes=("q_sdt",))
            K.emit("vector", lambda e, c=c, i=i: e.tensor_copy(
                b['sposatmp'][:, c * 32:(c + 1) * 32], ps_big[0:64, i * 512 + 32: i * 512 + 64]),
                reads=(kbig,), writes=("q_sdt",))
    K.emit("tensor", lambda e: e.matmul(ps_big[0:64, 0:384], b['tril'][:, :],
                                        b['sposatmp'][:, 384:768], start=True, stop=True),
           reads=("q_tril", "q_sdt"), writes=(kbig,))
    K.emit("vector", lambda e: e.tensor_copy(b['s_all'][:, 0:384], ps_big[0:64, 0:384]),
           reads=(kbig,), writes=("q_sall",))
    K.emit("tensor", lambda e: e.matmul(ps_big[0:1, 0:384], b['esel'][:, 0:1],
                                        b['s_all'][:, 0:384], start=True, stop=True),
           reads=("q_esel", "q_sall"), writes=(kbig,))
    K.emit("vector", lambda e: e.tensor_copy(b['bmid'][0:1, 0:384], ps_big[0:1, 0:384]),
           reads=(kbig,), writes=("q_bmid",))
    K.emit("tensor", lambda e: e.matmul(ps_big[0:1, 0:384], b['esel'][:, 1:2],
                                        b['s_all'][:, 0:384], start=True, stop=True),
           reads=("q_esel", "q_sall"), writes=(kbig,))
    K.emit("vector", lambda e: e.tensor_copy(b['slast'][0:1, 0:384], ps_big[0:1, 0:384]),
           reads=(kbig,), writes=("q_slast",))
    K.emit("vector", lambda e: e.tensor_tensor(b['wbase'][0:1, 0:352], b['slast'][0:1, 0:352],
                                               b['bmid'][0:1, 32:384], ALU.add),
           reads=("q_slast", "q_bmid"), writes=("q_wbase",))
    K.emit("vector", lambda e: e.tensor_copy(b['wbase'][0:1, 352:384], b['slast'][0:1, 352:384]),
           reads=("q_slast",), writes=("q_wbase",))
    K.emit("vector", lambda e: e.tensor_tensor(b['g1row'][0:1, 0:384], b['wbase'][0:1, 0:384],
                                               b['bmid'][0:1, 0:384], ALU.subtract),
           reads=("q_wbase", "q_bmid"), writes=("q_g1row",))
    K.emit("scalar", lambda e: e.activation(b['g1row'][0:1, 0:384], b['g1row'][0:1, 0:384], AF.Exp),
           reads=("q_g1row",), writes=("q_g1row",))
    for cp in range(6):
        K.emit("sync", lambda e, cp=cp: e.dma_start_transpose(
            b['BT'][:, cp * 128:(cp + 1) * 128], b['xbc2'][:, 16 * 768 + cp * 128: 16 * 768 + (cp + 1) * 128]),
            reads=("q_xbc2",), writes=("q_BT",), dma=True)
    def prep(c):
        q = c % 2
        cp, off = c // 2, (c % 2) * 64
        uvw, ubf = b['uvw%d' % q], b['ubf%d' % q]
        kuvw, kubf = "q_uvw%d" % q, "q_ubf%d" % q
        kXT, kxtc = "q_XT0", "q_xtc0"
        kXu, kXw = "q_Xu%d" % q, "q_Xw%d" % q
        if c % 2 == 0:
            for cc in range(16):
                K.emit("sync", lambda e, cc=cc, cp=cp: e.dma_start_transpose(
                    b['XT0'][:, cc * 128:(cc + 1) * 128],
                    b['xbc2'][:, cc * 768 + cp * 128: cc * 768 + (cp + 1) * 128]),
                    reads=("q_xbc2",), writes=("q_XT0",), dma=True)
        K.emit("tensor", lambda e, c=c: e.matmul(ps_mr[0:64, 576:608], C['ones_row128'][0:1, 0:64],
                                                 b['bmid'][0:1, c * 32:(c + 1) * 32], start=True, stop=True),
               reads=("ones_row128", "q_bmid"), writes=("q_prep",))
        K.emit("vector", lambda e, c=c, uvw=uvw: e.tensor_tensor(uvw[:, 0:32], ps_mr[0:64, 576:608],
                                                        b['s_all'][:, c * 32:(c + 1) * 32], ALU.subtract),
               reads=("q_prep", "q_sall"), writes=(kuvw,))
        K.emit("vector", lambda e, c=c, uvw=uvw: e.tensor_tensor(uvw[:, 32:64], b['s_all'][:, c * 32:(c + 1) * 32],
                                                        ps_mr[0:64, 576:608], ALU.subtract),
               reads=("q_prep", "q_sall"), writes=(kuvw,))
        K.emit("tensor", lambda e, c=c: e.matmul(ps_mr[0:64, 608:640], C['ones_row128'][0:1, 0:64],
                                                 b['wbase'][0:1, c * 32:(c + 1) * 32], start=True, stop=True),
               reads=("ones_row128", "q_wbase"), writes=("q_prep",))
        K.emit("vector", lambda e, c=c, uvw=uvw: e.tensor_tensor(uvw[:, 64:96], ps_mr[0:64, 608:640],
                                                        b['s_all'][:, c * 32:(c + 1) * 32], ALU.subtract),
               reads=("q_prep", "q_sall"), writes=(kuvw,))
        K.emit("scalar", lambda e, uvw=uvw: e.activation(uvw[:, 0:96], uvw[:, 0:96], AF.Exp),
               reads=(kuvw,), writes=(kuvw,))
        K.emit("vector", lambda e, c=c, uvw=uvw, ubf=ubf: e.tensor_tensor(ubf[:, 0:32], uvw[:, 0:32],
                                                        b['sposatmp'][:, c * 32:(c + 1) * 32], ALU.mult),
               reads=(kuvw, "q_sdt"), writes=(kubf,))
        K.emit("vector", lambda e, uvw=uvw, ubf=ubf: e.tensor_copy(ubf[:, 32:64], uvw[:, 32:64]),
               reads=(kuvw,), writes=(kubf,))
        K.emit("vector", lambda e, c=c, uvw=uvw, ubf=ubf: e.tensor_tensor(ubf[:, 64:96], uvw[:, 64:96],
                                                        b['sposatmp'][:, c * 32:(c + 1) * 32], ALU.mult),
               reads=(kuvw, "q_sdt"), writes=(kubf,))
        ub = ubf[:, 0:32].rearrange("p (h q) -> p h q", q=1).broadcast_to([64, 32, 64])
        wb = ubf[:, 64:96].rearrange("p (h q) -> p h q", q=1).broadcast_to([64, 32, 64])
        xtc = b['xtc0']
        K.emit("vector", lambda e, off=off, xtc=xtc: e.tensor_copy(xtc[:, :], b['XT0'][off:off + 64, :]),
               reads=("q_XT0",), writes=(kxtc,))
        xtv = xtc[:, :].rearrange("p (h q) -> p h q", q=64)
        K.emit("vector", lambda e, q2=q: e.tensor_tensor(b['Xu%d' % q2][:, :].rearrange("p (h q) -> p h q", q=64), xtv, ub, ALU.mult),
               reads=(kxtc, kubf), writes=(kXu,))
        K.emit("vector", lambda e, q2=q: e.tensor_tensor(b['Xw%d' % q2][:, :].rearrange("p (h q) -> p h q", q=64), xtv, wb, ALU.mult),
               reads=(kxtc, kubf), writes=(kXw,))
        K.emit("vector", lambda e, cp=cp, off=off, q2=q: e.tensor_copy(
            b['btc%d' % q2][:, :], b['BT'][off:off + 64, cp * 128:(cp + 1) * 128]),
            reads=("q_BT",), writes=("q_btc%d" % q,))
        if c >= 4:
            K.emit("tensor", lambda e, c=c: e.matmul(ps_mr[0:64, 960:1024],
                                                     b['xbc2'][:, 16 * 768 + c * 64: 16 * 768 + (c + 1) * 64],
                                                     b['xbc2'][:, 17 * 768 + c * 64: 17 * 768 + (c + 1) * 64],
                                                     start=True, stop=True),
                   reads=("q_xbc2",), writes=("q_prep",))
            K.emit("vector", lambda e, q2=q: e.tensor_tensor(b['gm%d' % q2][:, :], ps_mr[0:64, 960:1024], b['triu_m'][:, :], ALU.mult),
                   reads=("q_prep", "q_trium"), writes=("q_gm%d" % q,))

    def main(c):
        q = c % 2
        is_blk = c >= 4
        kXu, kXw = "q_Xu%d" % q, "q_Xw%d" % q
        if is_blk:
            for nb in range(4):
                K.emit("tensor", lambda e, nb=nb, q2=q: e.matmul(
                    ps_big[0:64, nb * 512:(nb + 1) * 512], b['gm%d' % q2][:, :], b['Xu%d' % q2][:, nb * 512:(nb + 1) * 512],
                    start=True, stop=False), reads=("q_gm%d" % q, kXu), writes=(kbig,))
                K.emit("tensor", lambda e, nb=nb, c=c: e.matmul(
                    ps_big[0:64, nb * 512:(nb + 1) * 512],
                    b['xbc2'][:, 17 * 768 + c * 64: 17 * 768 + (c + 1) * 64],
                    b['T'][:, nb * 512:(nb + 1) * 512],
                    start=False, stop=True), reads=("q_xbc2", "q_T"), writes=(kbig,))
            vb = b['ubf%d' % q][:, 32:64].rearrange("p (h q) -> p h q", q=1).broadcast_to([64, 32, 64])
            K.emit("vector", lambda e, q2=q: e.tensor_tensor(b['Xu%d' % q2][:, :].rearrange("p (h q) -> p h q", q=64),
                                                       ps_big[0:64, 0:2048].rearrange("p (h q) -> p h q", q=64),
                                                       vb, ALU.mult),
                   reads=(kbig, "q_ubf%d" % q, kXu), writes=(kXu,))
            for cc in range(16):
                K.emit("sync", lambda e, cc=cc, c=c, q2=q: e.dma_start_transpose(
                    b['Yf'][:, cc * 512 + (c - 4) * 64: cc * 512 + (c - 4) * 64 + 64],
                    b['Xu%d' % q2][:, cc * 128:(cc + 1) * 128]),
                    reads=(kXu,), writes=("q_Yf",), dma=True)
        K.emit("vector", lambda e, q2=q: e.tensor_copy(b['btc%d' % q2][:, :], b['btc%d' % q2][:, :]),
               reads=("q_btc%d" % q,), writes=("q_btc%d" % q,)) if False else None
        for nb in range(4):
            K.emit("tensor", lambda e, nb=nb, q2=q: e.matmul(
                ps_big[0:128, nb * 512:(nb + 1) * 512], b['btc%d' % q2][:, :],
                b['Xw%d' % q2][:, nb * 512:(nb + 1) * 512],
                start=True, stop=True), reads=("q_btc%d" % q, kXw), writes=(kbig,))
        K.emit("tensor", lambda e, c=c: e.matmul(ps_mr[0:128, 0:32], C['ones_row128'][0:1, :],
                                                 b['g1row'][0:1, c * 32:(c + 1) * 32], start=True, stop=True),
               reads=("ones_row128", "q_g1row"), writes=("q_g1",))
        g1v = ps_mr[0:128, 0:32].rearrange("p (h q) -> p h q", q=1).broadcast_to([128, 32, 64])
        K.emit("vector", lambda e: e.tensor_tensor(b['T'][:, :].rearrange("p (h q) -> p h q", q=64),
                                                   b['T'][:, :].rearrange("p (h q) -> p h q", q=64), g1v, ALU.mult),
               reads=("q_T", "q_g1"), writes=("q_T",))
        K.emit("vector", lambda e: e.tensor_tensor(b['T'][:, :], b['T'][:, :], ps_big[0:128, 0:2048], ALU.add),
               reads=("q_T", kbig), writes=("q_T",))

    prep(0)
    for c in range(12):
        if c + 1 < 12:
            prep(c + 1)
        main(c)
    for cc in range(16):
        K.emit("scalar", lambda e, cc=cc: e.activation(b['scr1'][:, :], b['zg'][:, cc * 512:(cc + 1) * 512], AF.Silu),
               reads=("q_zg",), writes=("q_scr1",))
        K.emit("vector", lambda e, cc=cc: e.tensor_tensor(
            b['Yf'][:, cc * 512:(cc + 1) * 512], b['Yf'][:, cc * 512:(cc + 1) * 512], b['scr1'][:, :], ALU.mult),
            reads=("q_Yf", "q_scr1"), writes=("q_Yf",))
        K.emit("vector", lambda e, cc=cc: e.scalar_tensor_tensor(
            b['scr1'][:, :], b['xbc2'][:, cc * 768 + 256: cc * 768 + 768], b['Dcol'][:, cc:cc + 1],
            b['scr1'][:, :], ALU.mult, ALU.mult),
            reads=("q_xbc2", "q_Dcol", "q_scr1"), writes=("q_scr1",))
        K.emit("vector", lambda e, cc=cc: e.tensor_tensor(
            b['Yf'][:, cc * 512:(cc + 1) * 512], b['Yf'][:, cc * 512:(cc + 1) * 512], b['scr1'][:, :], ALU.add),
            reads=("q_Yf", "q_scr1"), writes=("q_Yf",))
        K.emit("scalar", lambda e, cc=cc: e.activation(b['scr1'][:, :], b['Yf'][:, cc * 512:(cc + 1) * 512], AF.Square),
               reads=("q_Yf",), writes=("q_scr1",))
        K.emit("tensor", lambda e, cc=cc: e.matmul(ps_stat[0:1, 0:512], C['onesb_col'][:, :], b['scr1'][:, :],
                                                   start=(cc == 0), stop=(cc == 15)),
               reads=("q_scr1", "onesb_col"), writes=("q_stat",))
    K.emit("vector", lambda e: e.tensor_scalar_mul(b['mvec'][0:1, 0:512], ps_stat[0:1, 0:512], 1.0 / 2048.0),
           reads=("q_stat",), writes=("q_mvecA",))
    K.emit("scalar", lambda e: e.activation(b['mvec'][0:1, 0:512], b['mvec'][0:1, 0:512], AF.Ln, bias=EPS),
           reads=("q_mvecA",), writes=("q_mvecA",))
    K.emit("scalar", lambda e: e.activation(b['mvec'][0:1, 0:512], b['mvec'][0:1, 0:512], AF.Exp, scale=-0.5),
           reads=("q_mvecA",), writes=("q_mvecA",))
    K.emit("tensor", lambda e: e.matmul(ps_mr[:, 0:512], C['ones_row128'][0:1, :], b['mvec'][0:1, 0:512],
                                        start=True, stop=True), reads=("ones_row128", "q_mvecA"), writes=("q_repA",))
    for cc in range(16):
        K.emit("vector", lambda e, cc=cc: e.scalar_tensor_tensor(
            b['zg'][:, cc * 512:(cc + 1) * 512], b['Yf'][:, cc * 512:(cc + 1) * 512],
            b['nwcol'][:, cc:cc + 1], ps_mr[:, 0:512], ALU.mult, ALU.mult),
            reads=("q_Yf", "q_nw", "q_repA"), writes=("q_zg",))
    K.emit("gpsimd", lambda e: e.dma_start(
        out=xt[:, :].rearrange("p (c t) -> p c t", t=TM),
        in_=xA[bidx, :, 260:772].rearrange("(c p) l -> p c l", p=128)),
        writes=("q_xt",), dma=True)
    for oc in range(8):
        wopb = b['w_op']
        wopk = "q_wop"
        K.emit("sync", lambda e, oc=oc, wopb=wopb: e.dma_start(
            out=wopb[:, :].rearrange("p (kc c) -> p kc c", c=128),
            in_=b['hb_wop_bf'][:, oc * 128:(oc + 1) * 128].rearrange("(kc p) c -> p kc c", p=128)),
            reads=("h_wopbf",), writes=(wopk,), dma=True)
        for kc in range(16):
            K.emit("tensor", lambda e, oc=oc, kc=kc, wopb=wopb: e.matmul(
                ps_mr[:, (oc % 2) * 512:(oc % 2) * 512 + 512],
                wopb[:, kc * 128:(kc + 1) * 128],
                b['zg'][:, kc * 512:(kc + 1) * 512],
                start=(kc == 0), stop=(kc == 15)), reads=(wopk, "q_zg"),
                writes=("q_rep%s" % ("A" if oc % 2 == 0 else "B"),))
        K.emit("vector", lambda e, oc=oc: e.tensor_tensor(
            xsq[:, oc * TM:(oc + 1) * TM], ps_mr[:, (oc % 2) * 512:(oc % 2) * 512 + 512],
            xt[:, oc * TM:(oc + 1) * TM], ALU.add),
            reads=("q_rep%s" % ("A" if oc % 2 == 0 else "B"), "q_xt"), writes=("q_xsq",))
        K.emit("gpsimd", lambda e, oc=oc: e.dma_start(
            out=xB[bidx, oc * 128:(oc + 1) * 128, 0:512], in_=xsq[:, oc * TM:(oc + 1) * TM]),
            reads=("q_xsq",), dma=True)


def emit_p1(K, C, b, xA, xB):
    for bidx in range(16):
        emit_p1_batch(K, C, b, xA, xB, bidx)



def alloc_p3(K, pfx="m_"):
    b = {'pfx': pfx}
    S = lambda n, sh, dt: K.sbuf(pfx + n, sh, dt)
    b['xt'] = S("xt", [128, 4096], F32R)
    b['xsq'] = S("xsq", [128, 4096], F32R)
    b['xn'] = S("xn", [128, 4096], BF16)
    b['xnr'] = S("xnr", [128, 4096], F32R)
    b['mvec'] = S("mvec", [1, 1024], F32)
    b['wc'] = S("wc", [128, 1024], F32R)
    b['mx'] = S("mx", [128, 16], F32)
    b['maskb'] = S("maskb", [128, 8192], BF16)
    b['maskT'] = S("maskT", [64, 2048], BF16)
    b['xnT'] = S("xnT", [64, 8192], BF16)
    b['xselsb'] = S("xselsb", [128, 1024], BF16)
    for p_ in (0, 1):
        b['w1sb%d' % p_] = S("w1sb%d" % p_, [128, 256], BF16)
        b['w2sb%d' % p_] = S("w2sb%d" % p_, [32, 1024], BF16)
        b['b2sb%d' % p_] = S("b2sb%d" % p_, [1, 1024], BF16)
        b['innersb%d' % p_] = S("innersb%d" % p_, [32, 256], BF16)
        b['xsel_es%d' % p_] = S("xsel_es%d" % p_, [128, 2048], BF16)
        b['rsb%d' % p_] = S("rsb%d" % p_, [128, 4096], BF16)
    b['b1sb'] = S("b1sb", [32, 128], F32)
    b['scat64'] = S("scat64", [64, 8192], BF16)
    b['scatT'] = S("scatT", [128, 512], BF16)
    b['ps_stat'] = K.psum(pfx + "psstat", [1, 1024], F32)
    b['ps_mr'] = K.psum(pfx + "psmr", [128, 1024], F32)
    b['ps_a'] = K.psum(pfx + "psa", [128, 1024], F32)
    b['ps_t'] = K.psum(pfx + "pst", [128, 128], BF16)
    b['ps_i'] = K.psum(pfx + "psi", [32, 256], F32)
    b['identb'] = S("identb", [128, 128], BF16)
    b['bcc'] = S("bcc", [128, 1], F32)
    b['ties'] = S("ties", [128, 512], F32)
    b['lgsb'] = S("lgsb", [128, 512], F32)
    return b


def emit_p3_setup(K, C, b, hb):
    pfx = b['pfx']
    for ic in range(8):
        K.emit("gpsimd", lambda e, ic=ic: e.dma_start(
            out=b['wc'][:, ic * 128:(ic + 1) * 128], in_=hb["wc"][ic * 128:(ic + 1) * 128, :]),
            writes=(pfx + "wc",), dma=True)
    K.emit("gpsimd", lambda e: e.dma_start(out=b['identb'][:, :], in_=hb["ident128"]),
           writes=(pfx + "identb",), dma=True)
    K.emit("sync", lambda e: e.dma_start(out=b['bcc'][:, :], in_=hb["bc"]),
           writes=(pfx + "bcc",), dma=True)
    K.emit("sync", lambda e: e.dma_start(out=b['ties'][:, :], in_=hb["tie"]),
           writes=(pfx + "ties",), dma=True)


def emit_p3(K, C, b, xsrc, xdst, hb, scratch):
    pfx = b['pfx']
    TM = 512
    xt, xsq, xn, xnr = b['xt'], b['xsq'], b['xn'], b['xnr']
    ps_mr, ps_stat, ps_a, ps_t, ps_i = b['ps_mr'], b['ps_stat'], b['ps_a'], b['ps_t'], b['ps_i']
    kmrA, kmrB = pfx + "repA", pfx + "repB"
    ka, kt, ki = pfx + "psa", pfx + "pst", pfx + "psi"
    xsel_h, r_h, scat_h = scratch['xsel_h'], scratch['r_h'], scratch['scat_h']

    for bidx in range(16):
        K.emit("gpsimd", lambda e, bidx=bidx: e.dma_start(
            out=xt[:, :].rearrange("p (c t) -> p c t", t=TM),
            in_=xsrc[bidx].rearrange("(c p) l -> p c l", p=128)),
            writes=(pfx + "xt",), dma=True)
        emit_ln(K, C, xt, xsq, ps_stat, ps_mr, b['mvec'], TM, tag=pfx)
        for c in range(8):
            K.emit("vector", lambda e, c=c: e.tensor_tensor(
                xnr[:, c * TM:(c + 1) * TM], xt[:, c * TM:(c + 1) * TM], ps_mr[:, 0:TM], ALU.subtract),
                reads=(pfx + "xt", kmrA), writes=(pfx + "xnr",))
            K.emit("vector", lambda e, c=c: e.tensor_tensor(
                xnr[:, c * TM:(c + 1) * TM], xnr[:, c * TM:(c + 1) * TM], ps_mr[:, TM:2 * TM], ALU.mult),
                reads=(pfx + "xnr", kmrB), writes=(pfx + "xnr",))
            K.emit("scalar", lambda e, c=c: e.activation(
                xn[:, c * TM:(c + 1) * TM], xnr[:, c * TM:(c + 1) * TM], AF.Copy),
                reads=(pfx + "xnr",), writes=(pfx + "xn",))
        for ic in range(8):
            K.emit("tensor", lambda e, ic=ic: e.matmul(
                ps_a[:, 0:512], b['wc'][:, ic * 128:(ic + 1) * 128], xnr[:, ic * TM:(ic + 1) * TM],
                start=(ic == 0), stop=(ic == 7)), reads=(pfx + "wc", pfx + "xnr"), writes=(ka,))
        K.emit("vector", lambda e: e.tensor_scalar(
            b['lgsb'][:, :], ps_a[:, 0:512], b['bcc'][:, :], None, ALU.add),
            reads=(ka, pfx + "bcc"), writes=(pfx + "lgsb",))
        K.emit("vector", lambda e: e.tensor_tensor(
            b['lgsb'][:, :], b['lgsb'][:, :], b['ties'][:, :], ALU.add),
            reads=(pfx + "lgsb", pfx + "ties"), writes=(pfx + "lgsb",))
        K.emit("vector", lambda e: e.tensor_reduce(
            b['mx'][:, :].rearrange("p (g o) -> p g o", o=1),
            b['lgsb'][:, :].rearrange("p (g t) -> p g t", t=32), AX.X, ALU.max),
            reads=(pfx + "lgsb",), writes=(pfx + "mx",))
        mxb = b['mx'][:, :].rearrange("p (g o) -> p g o", o=1).broadcast_to([128, 16, 32])
        K.emit("vector", lambda e, bidx=bidx: e.tensor_tensor(
            b['maskb'][:, bidx * 512:(bidx + 1) * 512].rearrange("p (g t) -> p g t", t=32),
            b['lgsb'][:, :].rearrange("p (g t) -> p g t", t=32), mxb, ALU.is_equal),
            reads=(pfx + "lgsb", pfx + "mx"), writes=(pfx + "maskb",))
        for g in range(16):
            K.emit("tensor", lambda e, g=g, bidx=bidx: e.transpose(
                ps_t[0:32, 0:128], b['maskb'][:, bidx * 512 + g * 32: bidx * 512 + (g + 1) * 32],
                b['identb'][:, :]), reads=(pfx + "maskb", pfx + "identb"), writes=(kt,))
            K.emit("vector", lambda e, g=g: e.tensor_copy(
                b['maskT'][(g % 2) * 32:(g % 2) * 32 + 32, (g // 2) * 128:(g // 2 + 1) * 128],
                ps_t[0:32, 0:128]), reads=(kt,), writes=(pfx + "maskT",))
        for pc in range(4):
            for ic in range(8):
                K.emit("tensor", lambda e, pc=pc, ic=ic: e.transpose(
                    ps_t[:, :], xn[:, ic * TM + pc * 128: ic * TM + (pc + 1) * 128], b['identb'][:, :]),
                    reads=(pfx + "xn", pfx + "identb"), writes=(kt,))
                K.emit("vector", lambda e, pc=pc, ic=ic: e.tensor_copy(
                    b['xnT'][0:64, (2 * pc) * 1024 + ic * 128:(2 * pc) * 1024 + (ic + 1) * 128],
                    ps_t[0:64, :]), reads=(kt,), writes=(pfx + "xnT",))
                K.emit("vector", lambda e, pc=pc, ic=ic: e.tensor_copy(
                    b['xnT'][0:64, (2 * pc + 1) * 1024 + ic * 128:(2 * pc + 1) * 1024 + (ic + 1) * 128],
                    ps_t[64:128, :]), reads=(kt,), writes=(pfx + "xnT",))
        for g in range(16):
            base = (g % 2) * 32
            for nh in range(2):
                K.emit("tensor", lambda e, g=g, nh=nh, base=base: e.matmul(
                    ps_a[:, nh * 512:(nh + 1) * 512],
                    b['maskT'][base:base + 32, (g // 2) * 128:(g // 2 + 1) * 128],
                    b['xnT'][base:base + 32, (g // 2) * 1024 + nh * 512:(g // 2) * 1024 + (nh + 1) * 512],
                    start=True, stop=True), reads=(pfx + "maskT", pfx + "xnT"), writes=(ka,))
            K.emit("vector", lambda e: e.tensor_copy(b['xselsb'][:, :], ps_a[:, 0:1024]),
                   reads=(ka,), writes=(pfx + "xselsb",))
            K.emit("sync", lambda e, g=g, bidx=bidx: e.dma_start(
                out=xsel_h[:, bidx * 16 + g, :], in_=b['xselsb'][:, :]),
                reads=(pfx + "xselsb",), writes=(pfx + "h_xsel",), dma=True)

    for es in range(128):
        p_ = es % 2
        w1sb, w2sb, b2sb = b['w1sb%d' % p_], b['w2sb%d' % p_], b['b2sb%d' % p_]
        innersb, xsel_es = b['innersb%d' % p_], b['xsel_es%d' % p_]
        kw1, kw2, kb2 = pfx + "w1sb%d" % p_, pfx + "w2sb%d" % p_, pfx + "b2sb%d" % p_
        kin, kxe = pfx + "innersb%d" % p_, pfx + "xsel_es%d" % p_
        kap = ka + str(p_)
        psa = ps_a[:, (p_ % 2) * 0:1024] if False else ps_a
        K.emit("gpsimd", lambda e, es=es, w1sb=w1sb: e.dma_start(
            out=w1sb[:, :].rearrange("p (dc f) -> p dc f", f=32),
            in_=hb["w1"][es].rearrange("dc p f -> p dc f")),
            writes=(kw1,), dma=True)
        K.emit("gpsimd", lambda e, es=es, w2sb=w2sb: e.dma_start(out=w2sb[:, :], in_=hb["w2"][es]),
               writes=(kw2,), dma=True)
        K.emit("gpsimd", lambda e, es=es, b2sb=b2sb: e.dma_start(out=b2sb[0:1, :], in_=hb["b2"][es:es + 1, :]),
               writes=(kb2,), dma=True)
        if es == 0:
            K.emit("sync", lambda e: e.dma_start(out=b['b1sb'][:, :], in_=hb["b1"]),
                   writes=(pfx + "b1sb",), dma=True)
        for dc in range(8):
            K.emit("sync", lambda e, es=es, dc=dc, xsel_es=xsel_es: e.dma_start_transpose(
                xsel_es[:, dc * 256:(dc + 1) * 256], xsel_h[es, :, dc * 128:(dc + 1) * 128]),
                reads=(pfx + "h_xsel",), writes=(kxe,), dma=True)
        for dc in range(8):
            K.emit("tensor", lambda e, dc=dc, w1sb=w1sb, xsel_es=xsel_es: e.matmul(
                ps_i[0:32, (p_ % 2) * 0:256] if False else ps_i[:, :], w1sb[:, dc * 32:(dc + 1) * 32], xsel_es[:, dc * 256:(dc + 1) * 256],
                start=(dc == 0), stop=(dc == 7)), reads=(kw1, kxe), writes=(ki,))
        K.emit("scalar", lambda e, es=es, innersb=innersb: e.activation(
            innersb[:, :], ps_i[:, :], AF.Relu, bias=b['b1sb'][:, es:es + 1]),
            reads=(ki, pfx + "b1sb"), writes=(kin,))
        for gc in range(2):
            for nh in range(2):
                K.emit("tensor", lambda e, nh=nh, b2sb=b2sb: e.matmul(
                    ps_a[:, nh * 512:(nh + 1) * 512], C['ones_row'][0:1, 0:128],
                    b2sb[0:1, nh * 512:(nh + 1) * 512],
                    start=True, stop=False), reads=(kb2, "ones_row"), writes=(ka,))
                K.emit("tensor", lambda e, gc=gc, nh=nh, innersb=innersb, w2sb=w2sb: e.matmul(
                    ps_a[:, nh * 512:(nh + 1) * 512],
                    innersb[:, gc * 128:(gc + 1) * 128],
                    w2sb[:, nh * 512:(nh + 1) * 512],
                    start=False, stop=True), reads=(kin, kw2), writes=(ka,))
            K.emit("vector", lambda e: e.tensor_copy(b['xselsb'][:, :], ps_a[:, 0:1024]),
                   reads=(ka,), writes=(pfx + "xselsb",))
            K.emit("sync", lambda e, es=es, gc=gc: e.dma_start(
                out=r_h[gc * 128:(gc + 1) * 128, es, :], in_=b['xselsb'][:, :]),
                reads=(pfx + "xselsb",), writes=(pfx + "h_r",), dma=True)

    for bidx in range(16):
        for gq in range(4):
            rsb = b['rsb%d' % (gq % 2)]
            krsb = pfx + "rsb%d" % (gq % 2)
            K.emit("gpsimd", lambda e, bidx=bidx, gq=gq, rsb=rsb: e.dma_start(
                out=rsb[:, :].rearrange("p (g d) -> p g d", d=1024),
                in_=r_h[bidx * 16 + gq * 4: bidx * 16 + gq * 4 + 4, :, :].rearrange("g p d -> p g d")),
                reads=(pfx + "h_r",), writes=(krsb,), dma=True)
            for gi in range(4):
                g = gq * 4 + gi
                for nh in range(2):
                    K.emit("tensor", lambda e, g=g, gi=gi, nh=nh, bidx=bidx, rsb=rsb: e.matmul(
                        ps_a[0:32, nh * 512:(nh + 1) * 512],
                        b['maskb'][:, bidx * 512 + g * 32: bidx * 512 + (g + 1) * 32],
                        rsb[:, gi * 1024 + nh * 512: gi * 1024 + (nh + 1) * 512],
                        start=True, stop=True), reads=(pfx + "maskb", krsb), writes=(ka,))
                K.emit("vector", lambda e, g=g: e.tensor_copy(
                    b['scat64'][(g % 2) * 32:(g % 2) * 32 + 32, (g // 2) * 1024:(g // 2 + 1) * 1024],
                    ps_a[0:32, 0:1024]), reads=(ka,), writes=(pfx + "scat64",))
        K.emit("sync", lambda e, bidx=bidx: e.dma_start(
            out=scat_h[bidx].rearrange("(gh gm t) d -> (gm t) gh d", gm=2, t=32),
            in_=b['scat64'][:, :].rearrange("p (gh d) -> p gh d", d=1024)),
            reads=(pfx + "scat64",), writes=(pfx + "h_scat",), dma=True)
        K.emit("gpsimd", lambda e, bidx=bidx: e.dma_start(
            out=xt[:, :].rearrange("p (c t) -> p c t", t=TM),
            in_=xsrc[bidx].rearrange("(c p) l -> p c l", p=128)),
            writes=(pfx + "xt",), dma=True)
        for dc in range(8):
            K.emit("sync", lambda e, dc=dc, bidx=bidx: e.dma_start_transpose(
                b['scatT'][:, :], scat_h[bidx, :, dc * 128:(dc + 1) * 128]),
                reads=(pfx + "h_scat",), writes=(pfx + "scatT",), dma=True)
            K.emit("vector", lambda e, dc=dc: e.scalar_tensor_tensor(
                xsq[:, dc * TM:(dc + 1) * TM], b['scatT'][:, :], 0.5, xt[:, dc * TM:(dc + 1) * TM],
                ALU.mult, ALU.add), reads=(pfx + "scatT", pfx + "xt"), writes=(pfx + "xsq",))
            K.emit("gpsimd", lambda e, dc=dc, bidx=bidx: e.dma_start(
                out=xdst[bidx, dc * 128:(dc + 1) * 128, :], in_=xsq[:, dc * TM:(dc + 1) * TM]),
                reads=(pfx + "xsq",), dma=True)



def alloc_p4(K):
    b = {}
    S = lambda n, sh, dt: K.sbuf("a_" + n, sh, dt)
    b['xt0'] = S("xt0", [128, 2048], F32R)
    b['xt1'] = b['xt0']
    b['xsq'] = S("xsq", [128, 2048], F32R)
    b['xn0'] = S("xn0", [128, 2048], BF16)
    b['xn1'] = S("xn1", [128, 2048], BF16)
    b['xnew'] = S("xnew", [128, 2048], F32)
    b['mvec'] = S("mvec", [1, 1024], F32)
    b['wk'] = S("wk", [128, 8192], BF16)
    b['wv'] = S("wv", [128, 8192], BF16)
    b['wq'] = S("wq", [128, 8192], BF16)
    b['S'] = S("S", [128, 16384], F32)
    b['scr'] = S("scr", [128, 4096], F32)
    b['kT0'] = S("kT0", [128, 2048], BF16)
    b['vT0'] = S("vT0", [128, 2048], BF16)
    b['qT0'] = S("qT0", [128, 2048], BF16)
    b['kT1'] = S("kT1", [128, 2048], BF16)
    b['vT1'] = S("vT1", [128, 2048], BF16)
    b['qT1'] = S("qT1", [128, 2048], BF16)
    b['ret'] = S("ret", [128, 1024], F32)
    b['sil'] = S("sil", [128, 1024], F32)
    b['attn'] = S("attn", [128, 1024], BF16)
    b['ssum'] = S("ssum", [128, 128], F32)
    b['rr'] = S("rr", [128, 128], F32)
    b['ps_stat'] = K.psum("a_psstat", [1, 1024], F32)
    b['ps_mr'] = K.psum("a_psmr", [128, 1024], F32)
    b['ps_t'] = K.psum("a_pst", [128, 128], BF16)
    b['identb'] = S("identb", [128, 128], BF16)
    return b


def emit_p4_setup(K, C, b, hb):
    for nm, dst in (("wkT", b['wk']), ("wvT", b['wv']), ("wqT", b['wq'])):
        for ic in range(8):
            K.emit("gpsimd", lambda e, nm=nm, dst=dst, ic=ic: e.dma_start(
                out=dst[:, ic * 1024:(ic + 1) * 1024], in_=hb[nm][ic * 128:(ic + 1) * 128, :]),
                writes=("a_" + nm,), dma=True)
    K.emit("gpsimd", lambda e: e.dma_start(out=b['identb'][:, :], in_=hb["ident128"]),
           writes=("a_identb",), dma=True)


def emit_p4(K, C, b, xsrc, xdst):
    TM = 256
    xsq = b['xsq']
    ps_mr, ps_stat, ps_t = b['ps_mr'], b['ps_stat'], b['ps_t']
    kmrA, kmrB, kt_ = "a_repA", "a_repB", "a_pst"
    for half in range(2):
        h0 = half * 256
        K.emit("gpsimd", lambda e: e.memset(b['S'][:, :], 0.0), writes=("a_S",))
        for t in range(16):
            par = t % 2
            xt, xn = b['xt0'], b['xn%d' % par]
            K.emit("gpsimd", lambda e, t=t, h0=h0: e.dma_start(
                out=xt[:, :].rearrange("p (c t2) -> p c t2", t2=TM),
                in_=xsrc[t, :, h0:h0 + 256].rearrange("(c p) l -> p c l", p=128)),
                writes=("a_xt",), dma=True)
            emit_ln(K, C, xt, xsq, ps_stat, ps_mr, b['mvec'], TM, tag="a_")
            for c in range(8):
                K.emit("vector", lambda e, c=c: e.tensor_tensor(
                    xsq[:, c * TM:(c + 1) * TM], xt[:, c * TM:(c + 1) * TM], ps_mr[:, 0:TM], ALU.subtract),
                    reads=("a_xt", kmrA), writes=("a_xsq",))
                K.emit("vector", lambda e, c=c: e.tensor_tensor(
                    xn[:, c * TM:(c + 1) * TM], xsq[:, c * TM:(c + 1) * TM], ps_mr[:, TM:2 * TM], ALU.mult),
                    reads=("a_xsq", kmrB), writes=("a_xn%d" % par,))
            for wnm, wt, dst, dk in (("wk", b['wk'], b['kT%d' % par], "a_kT%d" % par),
                                     ("wv", b['wv'], b['vT%d' % par], "a_vT%d" % par),
                                     ("wq", b['wq'], b['qT%d' % par], "a_qT%d" % par)):
                for pc in range(2):
                    for ic in range(8):
                        for nh in range(2):
                            K.emit("tensor", lambda e, pc=pc, ic=ic, nh=nh, wt=wt: e.matmul(
                                ps_mr[:, nh * 512:(nh + 1) * 512],
                                xn[:, ic * TM + pc * 128: ic * TM + (pc + 1) * 128],
                                wt[:, ic * 1024 + nh * 512: ic * 1024 + (nh + 1) * 512],
                                start=(ic == 0), stop=(ic == 7)),
                                reads=("a_xn%d" % par, "a_" + wnm + "T"), writes=(kmrA, kmrB))
                    K.emit("vector", lambda e, pc=pc, dst=dst: e.tensor_copy(
                        dst[:, pc * 1024:(pc + 1) * 1024], ps_mr[:, 0:1024]),
                        reads=(kmrA, kmrB), writes=(dk,))
            for pc in range(2):
                K.emit("scalar", lambda e, pc=pc, par=par: e.activation(
                    b['scr'][:, 0:1024], b['kT%d' % par][:, pc * 1024:(pc + 1) * 1024], AF.Square),
                    reads=("a_kT%d" % par,), writes=("a_scr",))
                K.emit("vector", lambda e: e.tensor_reduce(
                    b['ssum'][:, :].rearrange("p (n o) -> p n o", o=1),
                    b['scr'][:, 0:1024].rearrange("p (n j) -> p n j", j=8), AX.X, ALU.add),
                    reads=("a_scr",), writes=("a_ssum",))
                K.emit("scalar", lambda e: e.activation(b['rr'][:, :], b['ssum'][:, :], AF.Ln),
                       reads=("a_ssum",), writes=("a_rr",))
                K.emit("scalar", lambda e: e.activation(b['rr'][:, :], b['rr'][:, :], AF.Exp, scale=0.5),
                       reads=("a_rr",), writes=("a_rr",))
                K.emit("vector", lambda e: e.tensor_scalar_add(b['rr'][:, :], b['rr'][:, :], 1e-6),
                       reads=("a_rr",), writes=("a_rr",))
                K.emit("vector", lambda e: e.reciprocal(b['rr'][:, :], b['rr'][:, :]),
                       reads=("a_rr",), writes=("a_rr",))
                rrb = b['rr'][:, :].rearrange("p (n o) -> p n o", o=1).broadcast_to([128, 128, 8])
                K.emit("vector", lambda e, pc=pc, par=par: e.tensor_tensor(
                    b['kT%d' % par][:, pc * 1024:(pc + 1) * 1024].rearrange("p (n j) -> p n j", j=8),
                    b['kT%d' % par][:, pc * 1024:(pc + 1) * 1024].rearrange("p (n j) -> p n j", j=8),
                    rrb, ALU.mult), reads=("a_kT%d" % par, "a_rr"), writes=("a_kT%d" % par,))
            for pc in range(2):
                Spc = b['S'][:, pc * 8192:(pc + 1) * 8192]
                knb = b['kT%d' % par][:, pc * 1024:(pc + 1) * 1024]
                for sub in range(2):
                    Ssub = Spc[:, sub * 4096:(sub + 1) * 4096].rearrange("p (n i j) -> p n i j", i=8, j=8)
                    knsub = knb[:, sub * 512:(sub + 1) * 512].rearrange("p (n j) -> p n j", j=8)
                    knsub_bi = knb[:, sub * 512:(sub + 1) * 512].rearrange("p (n o j) -> p n o j", o=1, j=8).broadcast_to([128, 64, 8, 8])
                    K.emit("vector", lambda e, Ssub=Ssub, knsub_bi=knsub_bi: e.tensor_tensor(
                        b['scr'][:, :].rearrange("p (n i j) -> p n i j", i=8, j=8), Ssub, knsub_bi, ALU.mult),
                        reads=("a_S", "a_kT%d" % par), writes=("a_scr",))
                    K.emit("vector", lambda e, sub=sub: e.tensor_reduce(
                        b['ret'][:, sub * 512:(sub + 1) * 512].rearrange("p (f o) -> p f o", o=1),
                        b['scr'][:, :].rearrange("p (f j) -> p f j", j=8), AX.X, ALU.add),
                        reads=("a_scr",), writes=("a_ret",))
                K.emit("vector", lambda e, pc=pc, par=par: e.tensor_tensor(
                    b['ret'][:, :], b['vT%d' % par][:, pc * 1024:(pc + 1) * 1024], b['ret'][:, :], ALU.subtract),
                    reads=("a_vT%d" % par, "a_ret"), writes=("a_ret",))
                for sub in range(2):
                    Ssub = Spc[:, sub * 4096:(sub + 1) * 4096].rearrange("p (n i j) -> p n i j", i=8, j=8)
                    knsub_bi = knb[:, sub * 512:(sub + 1) * 512].rearrange("p (n o j) -> p n o j", o=1, j=8).broadcast_to([128, 64, 8, 8])
                    dsub_bj = b['ret'][:, sub * 512:(sub + 1) * 512].rearrange("p (n i o) -> p n i o", o=1, i=8).broadcast_to([128, 64, 8, 8])
                    K.emit("vector", lambda e, dsub_bj=dsub_bj, knsub_bi=knsub_bi: e.tensor_tensor(
                        b['scr'][:, :].rearrange("p (n i j) -> p n i j", i=8, j=8), dsub_bj, knsub_bi, ALU.mult),
                        reads=("a_ret", "a_kT%d" % par), writes=("a_scr",))
                    K.emit("vector", lambda e, Ssub=Ssub: e.tensor_tensor(
                        Ssub, Ssub, b['scr'][:, :].rearrange("p (n i j) -> p n i j", i=8, j=8), ALU.add),
                        reads=("a_S", "a_scr"), writes=("a_S",))
                    K.emit("scalar", lambda e, Spc=Spc, sub=sub: e.activation(
                        Spc[:, sub * 4096:(sub + 1) * 4096], Spc[:, sub * 4096:(sub + 1) * 4096], AF.Tanh),
                        reads=("a_S",), writes=("a_S",))
                    qsub_bi = b['qT%d' % par][:, pc * 1024 + sub * 512: pc * 1024 + (sub + 1) * 512].rearrange(
                        "p (n o j) -> p n o j", o=1, j=8).broadcast_to([128, 64, 8, 8])
                    K.emit("vector", lambda e, Ssub=Ssub, qsub_bi=qsub_bi: e.tensor_tensor(
                        b['scr'][:, :].rearrange("p (n i j) -> p n i j", i=8, j=8), Ssub, qsub_bi, ALU.mult),
                        reads=("a_S", "a_qT%d" % par), writes=("a_scr",))
                    K.emit("vector", lambda e, sub=sub: e.tensor_reduce(
                        b['ret'][:, sub * 512:(sub + 1) * 512].rearrange("p (f o) -> p f o", o=1),
                        b['scr'][:, :].rearrange("p (f j) -> p f j", j=8), AX.X, ALU.add),
                        reads=("a_scr",), writes=("a_ret",))
                K.emit("scalar", lambda e: e.activation(b['sil'][:, :], b['ret'][:, :], AF.Silu),
                       reads=("a_ret",), writes=("a_sil",))
                K.emit("vector", lambda e: e.tensor_tensor(b['attn'][:, :], b['ret'][:, :], b['sil'][:, :], ALU.mult),
                       reads=("a_ret", "a_sil"), writes=("a_attn",))
                for oc in range(8):
                    K.emit("tensor", lambda e, oc=oc: e.transpose(
                        ps_t[:, :], b['attn'][:, oc * 128:(oc + 1) * 128], b['identb'][:, :]),
                        reads=("a_attn", "a_identb"), writes=(kt_,))
                    K.emit("vector", lambda e, oc=oc, pc=pc: e.tensor_tensor(
                        b['xnew'][:, oc * TM + pc * 128: oc * TM + (pc + 1) * 128],
                        xt[:, oc * TM + pc * 128: oc * TM + (pc + 1) * 128], ps_t[:, :], ALU.add),
                        reads=("a_xt", kt_), writes=("a_xnew",))
            K.emit("gpsimd", lambda e, t=t, h0=h0: e.dma_start(
                out=xdst[t, :, h0:h0 + 256].rearrange("(c p) l -> p c l", p=128),
                in_=b['xnew'][:, :].rearrange("p (c t2) -> p c t2", t2=TM)),
                reads=("a_xnew",), dma=True)


from concourse.bass_utils import run_bass_kernel_spmd


def build_megakernel():
    K = Prog()
    nc = K.nc
    F = F32
    xin = nc.dram_tensor("xin", [16, 1024, LH], F, kind="ExternalInput").ap()
    xout = nc.dram_tensor("xout", [16, 1024, 512], F, kind="ExternalOutput").ap()
    xA = nc.dram_tensor("xA", [16, 1024, LH], F, kind="Internal").ap()
    xB = nc.dram_tensor("xB", [16, 1024, 512], F, kind="Internal").ap()
    xC = nc.dram_tensor("xC", [16, 1024, 512], F, kind="Internal").ap()
    xD = nc.dram_tensor("xD", [16, 1024, 512], F, kind="Internal").ap()
    xE = nc.dram_tensor("xE", [16, 1024, 512], F, kind="Internal").ap()
    hb = {}
    for nm, shape in [("wT", (1024, 1024)), ("bias", (1024,)), ("sin", (1,)), ("sw", (1,)),
                      ("w_in", (1024, 4384)), ("w_op", (2048, 1024)), ("cwt", (128, 72)),
                      ("convb", (128, 18)), ("dtb", (32, 1)), ("A_log", (32, 1)),
                      ("Dcol", (128, 16)), ("nwcol", (128, 16)), ("hflag", (1, 1)),
                      ("tril", (64, 64)), ("triu", (64, 64)), ("esel", (64, 2)),
                      ("ident", (64, 64)), ("ident128", (128, 128)),
                      ("wc", (1024, 128)), ("bc", (128, 1)), ("tie", (128, 512)), ("w1", (128, 8, 128, 32)), ("w2", (128, 32, 1024)),
                      ("b1", (32, 128)), ("b2", (128, 1024)),
                      ("wkT", (1024, 1024)), ("wvT", (1024, 1024)), ("wqT", (1024, 1024))]:
        hb[nm] = nc.dram_tensor(nm, list(shape), F, kind="ExternalInput").ap()
    hb["w_in_bf"] = nc.dram_tensor("w_in_bf", [1024, 4384], BF16, kind="Internal").ap()
    hb["w_op_bf"] = nc.dram_tensor("w_op_bf", [2048, 1024], BF16, kind="Internal").ap()
    scratch = {
        'xsel_h': nc.dram_tensor("xsel_h", [128, 256, 1024], BF16, kind="Internal").ap(),
        'r_h': nc.dram_tensor("r_h", [256, 128, 1024], BF16, kind="Internal").ap(),
        'scat_h': nc.dram_tensor("scat_h", [16, 512, 1024], BF16, kind="Internal").ap(),
    }
    C = emit_const_setup(K)
    with K.scope():
        b0 = alloc_p0(K, "p0_")
        emit_p0_setup(K, C, b0, hb["wT"], hb["bias"], hb["sin"], hb["sw"])
        emit_p0(K, C, b0, (xin, xA), LH)
    with K.scope():
        b1 = alloc_p1(K)
        b1['hbw'] = hb["w_in"]
        emit_p1_setup(K, C, b1, hb)
        emit_p1(K, C, b1, xA, xB)
    with K.scope():
        b2 = alloc_p0(K, "p2_")
        emit_p0_setup(K, C, b2, hb["wT"], hb["bias"], hb["sin"], hb["sw"])
        emit_p0(K, C, b2, (xB, xC), 512)
    with K.scope():
        b3 = alloc_p3(K, "m_")
        emit_p3_setup(K, C, b3, hb)
        emit_p3(K, C, b3, xC, xD, hb, scratch)
    with K.scope():
        b4 = alloc_p4(K)
        emit_p4_setup(K, C, b4, hb)
        emit_p4(K, C, b4, xD, xE)
    with K.scope():
        b5 = alloc_p3(K, "n_")
        emit_p3_setup(K, C, b5, hb)
        emit_p3(K, C, b5, xE, xout, hb, scratch)
    K.finish()
    return K.build()


def kernel(**ins):
    x = np.asarray(ins['L0_attn_x'], np.float32)
    w = {
        "wT": np.ascontiguousarray(np.asarray(ins['L0_ffn_weight_fp8'], np.float32).T),
        "bias": np.asarray(ins['L0_ffn_bias'], np.float32),
        "sin": np.asarray(ins['L0_ffn_input_scale'], np.float32),
        "sw": np.asarray(ins['L0_ffn_weight_scale'], np.float32),
        "w_in": np.asarray(ins['L0_attn_in_proj_w'], np.float32),
        "w_op": np.asarray(ins['L0_attn_out_proj_w'], np.float32),
        "cwt": np.ascontiguousarray(np.asarray(ins['L0_attn_conv_w'], np.float32).reshape(4, 18, 128).transpose(2, 1, 0).reshape(128, 72)),
        "convb": np.ascontiguousarray(np.asarray(ins['L0_attn_conv_b'], np.float32).reshape(18, 128).T),
        "dtb": np.asarray(ins['L0_attn_dt_bias'], np.float32).reshape(32, 1),
        "A_log": np.asarray(ins['L0_attn_A_log'], np.float32).reshape(32, 1),
        "Dcol": np.ascontiguousarray(np.repeat(np.asarray(ins['L0_attn_D'], np.float32), 64).reshape(16, 128).T),
        "nwcol": np.ascontiguousarray(np.asarray(ins['L0_attn_norm_w'], np.float32).reshape(16, 128).T),
        "tril": np.triu(np.ones((64, 64), np.float32)),
        "triu": np.triu(np.ones((64, 64), np.float32)),
        "esel": np.ascontiguousarray(np.stack([np.eye(64, 1, k=-31, dtype=np.float32)[:, 0],
                                               np.eye(64, 1, k=-63, dtype=np.float32)[:, 0]], axis=1)),
        "ident": np.eye(64, dtype=np.float32),
        "ident128": np.eye(128, dtype=np.float32),
        "wc": np.ascontiguousarray(np.asarray(ins['L1_ffn_Wc'], np.float32).reshape(1024, 128)),
        "bc": np.ascontiguousarray(np.asarray(ins['L1_ffn_bc'], np.float32).reshape(128, 1)),
        "tie": np.ascontiguousarray(np.broadcast_to(np.tile(np.linspace(0, 1e-6, 32, dtype=np.float32), 16), (128, 512)).copy()),
        "w1": np.ascontiguousarray(np.asarray(ins['L1_ffn_W1'], np.float32).reshape(128, 8, 128, 32)),
        "w2": np.ascontiguousarray(np.asarray(ins['L1_ffn_W2'], np.float32).reshape(128, 1024, 32).transpose(0, 2, 1)),
        "b1": np.ascontiguousarray(np.asarray(ins['L1_ffn_b1'], np.float32).reshape(128, 32).T),
        "b2": np.ascontiguousarray(np.asarray(ins['L1_ffn_b2'], np.float32).reshape(128, 1024)),
        "wkT": np.ascontiguousarray(np.asarray(ins['L1_attn_W_k'], np.float32).T),
        "wvT": np.ascontiguousarray(np.asarray(ins['L1_attn_W_v'], np.float32).T),
        "wqT": np.ascontiguousarray(np.asarray(ins['L1_attn_W_q'], np.float32).T),
    }
    in_maps = []
    for c in range(8):
        lo = 512 * c - HC
        pad = max(0, -lo)
        sl = x[:, max(lo, 0):512 * (c + 1), :]
        xc = np.ascontiguousarray(np.pad(sl, ((0, 0), (pad, 0), (0, 0))).transpose(0, 2, 1))
        im = dict(w)
        im["xin"] = xc
        im["hflag"] = np.array([[0.0 if c == 0 else 1.0]], np.float32)
        in_maps.append(im)
    nc = build_megakernel()
    res = run_bass_kernel_spmd(nc, in_maps, core_ids=list(range(8)))
    out = np.concatenate([np.asarray(r["xout"]).transpose(0, 2, 1) for r in res.results], axis=1)
    return np.ascontiguousarray(out.astype(np.float32))
